# version 04
# baseline (speedup 1.0000x reference)
import math
import contextlib
import numpy as np
import ml_dtypes

import concourse.bass as bass
import concourse.mybir as mybir
from concourse.bass_utils import run_bass_kernel_spmd

F32 = mybir.dt.float32
F32R = mybir.dt.float32r
BF16 = mybir.dt.bfloat16
AF = mybir.ActivationFunctionType
ALU = mybir.AluOpType
AX = mybir.AxisListType
U16 = mybir.dt.uint16

T_FULL, D, NH, DH, NKV = 8192, 2048, 16, 128, 4
GROUP = NH // NKV
DKV = NKV * DH
F_INNER = 8192
NFC = F_INNER // 128
NDC = D // 128
EPS = 1e-5
NCORES = 8
TOK = T_FULL // NCORES
HALO = 1024
TT = TOK + HALO
NT128 = TT // 128
NOWN = TOK // 128
WINDOWS = [384, 512, 768, 1024]
NKT = TT // 128
NEG = -1.0e9
ISQD = 1.0 / math.sqrt(DH)
FSL = 512
NSL = F_INNER // FSL


def _alibi_slopes(n):
    start = 2 ** (-(2 ** (-(math.log2(n) - 3))))
    return [start * start ** i for i in range(n)]


SLOPE_G = _alibi_slopes(NH)[:GROUP]


def _split_bf16(a, n):
    a = np.asarray(a, dtype=np.float64)
    out, rem = [], a.copy()
    for _ in range(n):
        q = rem.astype(ml_dtypes.bfloat16)
        out.append(q)
        rem = rem - q.astype(np.float64)
    return out


class Sem:
    def __init__(self, h):
        self.h = h
        self.n = 0

    def inc(self, k=1):
        self.n += k
        return self.n


class MemMap:
    BASE = 17408
    LIMIT = 229344

    def __init__(self, nc):
        self.nc = nc
        self.regions = []

    def alloc(self, name, shape, dtype_, off, phases):
        off = off + self.BASE
        size = shape[1] * mybir.dt.size(dtype_)
        end = off + size
        assert end <= self.LIMIT, f"{name} exceeds SBUF: {end}"
        for (n2, o2, e2, ph2) in self.regions:
            if off < e2 and o2 < end and (phases & ph2):
                raise AssertionError(f"overlap {name}/{n2} in phases {phases & ph2}")
        self.regions.append((name, off, end, phases))
        return self.nc.alloc_sbuf_tensor_at(name, shape, dtype_, offset=off).ap()


class P:
    pass


def _ser(v, ve, inst):
    inst.then_inc(ve.h, 1)
    ve.inc(1)
    v.wait_ge(ve.h, ve.n)


PH1, PH2, PH3, PH4A, PH4B, PH4C, PH4D = 1, 2, 4, 8, 16, 32, 64
ALL = 127


def build_program(debug=0):
    nc = bass.Bass("TRN2", target_bir_lowering=False,
                   detect_race_conditions=False)
    p = P()
    p.nc = nc
    p.debug = debug
    dt = nc.dram_tensor

    p.xin = dt("xin", [TT, D], F32, kind="ExternalInput").ap()
    p.Wq = dt("Wq", [D, D], F32, kind="ExternalInput").ap()
    p.Wk = dt("Wk", [D, DKV], F32, kind="ExternalInput").ap()
    p.Wv = dt("Wv", [D, DKV], F32, kind="ExternalInput").ap()
    p.Wo = dt("Wo", [D, D], F32, kind="ExternalInput").ap()
    p.w1 = dt("w1", [F_INNER, D], F32, kind="ExternalInput").ap()
    p.b1 = dt("b1", [F_INNER], F32, kind="ExternalInput").ap()
    p.lng = dt("lng", [F_INNER], F32, kind="ExternalInput").ap()
    p.lnb = dt("lnb", [F_INNER], F32, kind="ExternalInput").ap()
    p.w2 = dt("w2", [D, F_INNER], F32, kind="ExternalInput").ap()
    p.b2 = dt("b2", [D], F32, kind="ExternalInput").ap()
    p.identf_d = dt("identf", [128, 128], F32, kind="ExternalInput").ap()
    p.identb_d = dt("identb", [128, 128], U16, kind="ExternalInput").ap().bitcast(BF16)
    p.attl_d = dt("attl", [8, GROUP * NKT * 128], U16, kind="ExternalInput").ap().bitcast(BF16)
    p.attr_d = dt("attr", [8, GROUP * 2 * 512], U16, kind="ExternalInput").ap().bitcast(BF16)
    p.maskb_d = dt("maskb", [128, 896], U16, kind="ExternalInput").ap().bitcast(BF16)
    p.onesc_d = dt("onesc", [128, 2], F32, kind="ExternalInput").ap()
    p.onesr_d = dt("onesr", [1, 128], F32, kind="ExternalInput").ap()
    p.out = dt("out", [TOK, D], F32, kind="ExternalOutput").ap()
    p.xqsp = dt("xqspill", [NOWN, 128, NDC * 128], BF16).ap()
    p.h2sp = dt("h2spill", [NFC, 128, TOK], F32).ap()
    p.sTsp = dt("sTspill", [NDC, 128, TOK], BF16).ap()

    p.dbg = {}
    if debug:
        for name, shape, dtp in [
            ("d_kh", [128, NKV * TT], U16), ("d_kl", [128, NKV * TT], U16),
            ("d_vtm", [128, NKV * NKT * 128], F32),
            ("d_xq", [NOWN, 128, NDC * 128], U16),
            ("d_st", [128, NT128 * 8], F32),
            ("d_sT", [NDC, 128, TOK], U16),
            ("d_den", [1, 2 * NH * 512], F32),
            ("d_h2", [NFC, 128, TOK], F32),
            ("d_st2", [128, 8 * 8], F32),
            ("d_gT", [128, NDC * TOK], F32),
        ]:
            p.dbg[name] = dt(name, shape, dtp, kind="ExternalOutput").ap()

    es = contextlib.ExitStack()
    mm = MemMap(nc)
    p.mm = mm

    p.kh = mm.alloc("kh", [128, NKV * TT], BF16, 0, PH2 | PH3)
    p.kl = mm.alloc("kl", [128, NKV * TT], BF16, 16384, PH2 | PH3)
    p.vtm = mm.alloc("vtm", [128, NKV * NKT * 128], F32, 32768, PH2 | PH3)
    p.identf = mm.alloc("identf_s", [128, 128], F32, 65536, ALL)
    p.identb = mm.alloc("identb_s", [128, 128], BF16, 66048, ALL)
    p.maskb = mm.alloc("maskb_s", [128, 896], BF16, 66304, ALL)
    p.onesf = mm.alloc("onesf", [128, 2], F32, 68096, ALL)
    p.onesr = mm.alloc("onesr_s", [1, 128], F32, 68128, ALL)
    p.b1s = mm.alloc("b1s", [128, 64], F32, 68640, ALL)
    p.lngs = mm.alloc("lngs", [128, 64], F32, 68896, ALL)
    p.lnbs = mm.alloc("lnbs", [128, 64], F32, 69152, ALL)
    p.b2s = mm.alloc("b2s", [128, 16], F32, 69408, ALL)
    stoff = 69472
    for nm in ["mu", "rstd", "mx", "mn", "sx", "sxx", "c1", "dq1", "t0", "t1"]:
        setattr(p, "st_" + nm, mm.alloc("st_" + nm, [128, NT128], F32, stoff, ALL))
        stoff += 64
    st2off = 70112
    for nm in ["S1", "SG", "SB", "S2", "SG2", "mu2", "rstd2", "r2", "mxh",
               "c2", "dq2", "tmp"]:
        setattr(p, "s2_" + nm, mm.alloc("s2_" + nm, [128, NOWN], F32, st2off, ALL))
        st2off += 32
    p.wstA = mm.alloc("wstA", [128, 192], F32R, 70496, ALL)
    p.wstB = mm.alloc("wstB", [128, 128], F32R, 71264, ALL)
    p.qcols = mm.alloc("qcols", [128, 8], F32, 71776, ALL)
    p.denrow = mm.alloc("denrow", [1, 512], F32, 71808, ALL)
    p.recrow = mm.alloc("recrow", [1, 512], F32, 73856, ALL)
    p.thrc = mm.alloc("thrc", [128, 2], F32, 75904, ALL)
    p.thrn = mm.alloc("thrn", [128, 2], F32, 75936, ALL)
    p.wacc = mm.alloc("wacc", [128, 128], F32, 75968, ALL)
    p.attl = mm.alloc("attl_s", [8, GROUP * NKT * 128], BF16, 76480, PH1 | PH2 | PH3)
    p.attr = mm.alloc("attr_s", [8, GROUP * 2 * 512], BF16, 92864, PH1 | PH2 | PH3)
    PH4ALL = PH4A | PH4B | PH4C | PH4D
    p.rows4 = mm.alloc("rows4", [1, 3968], F32, 76480, PH4ALL)
    p.xnT = mm.alloc("xnT", [128, NDC * 512], F32, 101056, PH2 | PH3)
    p.wbuf = mm.alloc("wbuf", [128, NDC * 512], F32, 133824, PH2)
    p.wq0 = mm.alloc("wq0", [128, D], F32, 133824, PH3)
    p.wq1 = mm.alloc("wq1", [128, D], F32, 142016, PH3)
    p.qhb = mm.alloc("qhb", [128, 512], BF16, 150208, PH3)
    p.qlb = mm.alloc("qlb", [128, 512], BF16, 151232, PH3)
    p.rbc = mm.alloc("rbc", [128, 512], F32, 152256, PH3)
    p.pbuf = mm.alloc("pbuf", [128, 2 * 512], F32, 154304, PH3)
    p.xt0 = mm.alloc("xt0", [128, D], F32, 166592, PH1 | PH2)
    p.xnt = mm.alloc("xnt", [128, D], F32, 174784, PH1 | PH2)
    p.scrb = mm.alloc("scrb", [128, D], BF16, 182976, PH1 | PH2)
    p.scrP = mm.alloc("scrP", [128, 1024], F32, 187072, PH1 | PH2)
    p.puP = mm.alloc("puP", [128, 1024], F32, 191168, PH1 | PH2)
    p.pbP = mm.alloc("pbP", [128, 1024], F32, 195264, PH1 | PH2)
    p.xqst = mm.alloc("xqst", [128, D], BF16, 199360, PH1 | PH2)
    p.ysb = mm.alloc("ysb", [128, NH * 512], F32, 166592, PH3)
    p.xqTr = mm.alloc("xqTr", [128, NDC * TOK], BF16, 32768, PH4A)
    p.wtern = mm.alloc("wtern", [128, 4 * D], BF16, 101056, PH4A)
    p.wqT = mm.alloc("wqT", [128, NDC * FSL], BF16, 117440, PH4A)
    p.wfsl = mm.alloc("wfsl", [128, 4 * D], F32, 133824, PH4A)
    p.hbuf = mm.alloc("hbuf", [128, 8 * 512], F32, 166592, PH4A)
    p.tmpb = mm.alloc("tmpb", [128, D], BF16, 182976, PH4A | PH4C)
    p.bc1 = mm.alloc("bc1", [128, 2 * 512], F32, 191168, PH4A | PH4C)
    p.h2st = mm.alloc("h2st", [128, 1024], F32, 187072, PH4B | PH4C)
    p.pe2u = mm.alloc("pe2u", [128, 512], F32, 195264, PH4B | PH4C)
    p.pe2b = mm.alloc("pe2b", [128, 512], F32, 197312, PH4B | PH4C)
    p.pe2c = mm.alloc("pe2c", [128, 512], F32, 199360,
                      PH4A | PH4B | PH4C)
    p.RAb = mm.alloc("RAb", [128, 512], F32, 201408, PH4B | PH4C)
    p.RCb = mm.alloc("RCb", [128, 512], F32, 203456, PH4B | PH4C)
    p.RBb = mm.alloc("RBb", [128, 512], F32, 205504, PH4B | PH4C)
    p.statu = mm.alloc("statu", [128, 512], F32R, 207552, PH4B)
    p.statq = mm.alloc("statq", [128, 512], F32R, 209600, PH4B)
    p.w2sl = mm.alloc("w2sl", [128, F_INNER], F32, 0, PH4C)
    p.w2tn = mm.alloc("w2tn", [128, F_INNER], BF16, 32768, PH4C)
    p.w2tT = mm.alloc("w2tT", [128, F_INNER], BF16, 49152, PH4C)
    p.xq2T = mm.alloc("xq2T", [128, NFC * 512], BF16, 101056, PH4C)
    p.gkeep = mm.alloc("gkeep", [128, NDC * 512], BF16, 166592, PH4C)
    p.xrel = mm.alloc("xrel", [128, D], F32, 0, PH4D)
    p.ostg = mm.alloc("ostg", [128, D], F32, 8192, PH4D)
    p.sTr = mm.alloc("sTr", [128, 512], BF16, 16384, PH4D)
    p.dq1row = p.rows4[:, 0:1024]
    p.dq2row = p.rows4[:, 1024:2048]
    p.rowa = p.rows4[:, 2048:2560]
    p.rowc = p.rows4[:, 2560:3072]
    p.rowd = p.rows4[:, 3072:3584]
    p.thr1row = p.rows4[:, 3584:3712]
    p.thr2row = p.rows4[:, 3712:3840]

    p.pk = [es.enter_context(nc.psum_tensor(f"pk{i}", [128, 512], F32))
            for i in range(8)]

    p.sem = {}
    for name in ["dx", "dc", "dw", "dq", "dsp", "dst", "te", "ve", "se", "ge",
                 "dh2", "dh3", "dw2"]:
        p.sem[name] = Sem(es.enter_context(nc.semaphore(name)))

    p.block = es.enter_context(nc.Block())
    p.T = p.block.tensor
    p.V = p.block.vector
    p.S = p.block.scalar
    p.G = p.block.gpsimd
    p.SY = p.block.sync

    _phase0(p)
    _phase1(p)
    if debug == 4:
        _debug_dump(p)
        _finish(p)
        es.close()
        nc._p = p
        return nc
    _phase2(p, groups=(0, 1, 2), peel=True)
    if debug == 5:
        _phase3(p, qt=0)
    elif debug == 6:
        _phase3(p, qt=0)
        _phase2(p, groups=(3,), peel=True)
    elif debug == 7:
        _phase3(p, qt=0)
        _phase2(p, groups=(3,), peel=True)
        _phase3(p, qt=1)
    elif debug != 2:
        _phase3(p, qt=0)
        _phase2(p, groups=(3,), peel=True)
        _phase3(p, qt=1)
        _phase4(p)
    else:
        _phase2(p, groups=(3,), peel=True)
    _debug_dump(p)
    _finish(p)
    es.close()
    nc._p = p
    return nc


def _phase0(p):
    dc = p.sem["dc"]

    def loads(sy):
        for dst_, src in [(p.identf, p.identf_d), (p.identb, p.identb_d),
                          (p.attl, p.attl_d), (p.attr, p.attr_d),
                          (p.maskb, p.maskb_d), (p.onesf, p.onesc_d),
                          (p.onesr, p.onesr_d)]:
            sy.dma_start(out=dst_, in_=src).then_inc(dc.h, 16)
            dc.inc(16)
        for dst_, src in [(p.b1s, p.b1), (p.lngs, p.lng), (p.lnbs, p.lnb)]:
            sy.dma_start(out=dst_, in_=src.rearrange("(c p) -> p c", p=128),
                         allow_slow_non_contiguous=True).then_inc(dc.h, 16)
            dc.inc(16)
        sy.dma_start(out=p.b2s, in_=p.b2.rearrange("(c p) -> p c", p=128),
                     allow_slow_non_contiguous=True).then_inc(dc.h, 16)
        dc.inc(16)
    p.SY(loads)


def _phase1(p):
    dx, ve, se = p.sem["dx"], p.sem["ve"], p.sem["se"]
    for i in range(NT128):
        def load(sy, i=i):
            if i > 0:
                sy.wait_ge(se.h, se.n)
                sy.wait_ge(ve.h, ve.n)
            sy.dma_start(out=p.xt0, in_=p.xin[i * 128:(i + 1) * 128, :]
                         ).then_inc(dx.h, 16)
            dx.inc(16)
        p.SY(load)

        def scal(s, i=i):
            s.wait_ge(dx.h, dx.n)
            s.activation(p.scrb, p.xt0, AF.Square, accum_out=p.st_sxx[:, i:i + 1])
            s.activation(p.xqst, p.xt0, AF.Copy,
                         accum_out=p.st_sx[:, i:i + 1]).then_inc(se.h, 1)
            se.inc(1)
        p.S(scal)

        def vec(v, i=i):
            v.wait_ge(dx.h, dx.n)
            v.reduce_max(p.st_mx[:, i:i + 1], p.xt0, axis=AX.X)
            v.tensor_reduce(p.st_mn[:, i:i + 1], p.xt0, axis=AX.X,
                            op=ALU.min).then_inc(ve.h, 1)
            ve.inc(1)
        p.V(vec)

    def vst(v):
        v.wait_ge(se.h, se.n)
        inv_d = 1.0 / D
        _ser(v, ve, v.tensor_scalar(p.st_mu, p.st_sx, inv_d, None, ALU.mult))
        _ser(v, ve, v.tensor_scalar(p.st_t0, p.st_sxx, inv_d, None, ALU.mult))
        _ser(v, ve, v.tensor_tensor(p.st_t1, p.st_mu, p.st_mu, ALU.mult))
        _ser(v, ve, v.tensor_tensor(p.st_t0, p.st_t0, p.st_t1, ALU.subtract))
        v.tensor_scalar(p.st_t1, p.st_t0, EPS, None, ALU.add).then_inc(ve.h, 1)
        ve.inc(1)
    p.V(vst)

    def sst(s):
        s.wait_ge(ve.h, ve.n)
        s.activation(p.st_rstd, p.st_t1, AF.Sqrt).then_inc(se.h, 1)
        se.inc(1)
    p.S(sst)

    def vst2(v):
        v.wait_ge(se.h, se.n)
        _ser(v, ve, v.reciprocal(p.st_rstd, p.st_rstd))
        _ser(v, ve, v.tensor_tensor(p.st_t1, p.st_rstd, p.st_rstd, ALU.mult))
        _ser(v, ve, v.tensor_tensor(p.st_t1, p.st_t1, p.st_t0, ALU.mult))
        v.tensor_scalar(p.st_t1, p.st_t1, EPS, None, ALU.add).then_inc(ve.h, 1)
        ve.inc(1)
    p.V(vst2)

    def sst2(s):
        s.wait_ge(ve.h, ve.n)
        s.activation(p.st_t1, p.st_t1, AF.Sqrt).then_inc(se.h, 1)
        se.inc(1)
    p.S(sst2)

    def vst3(v):
        v.wait_ge(se.h, se.n)
        _ser(v, ve, v.reciprocal(p.st_t1, p.st_t1))
        _ser(v, ve, v.tensor_tensor(p.st_mx, p.st_mx, p.st_mu, ALU.subtract))
        _ser(v, ve, v.tensor_tensor(p.st_mn, p.st_mu, p.st_mn, ALU.subtract))
        _ser(v, ve, v.tensor_tensor(p.st_mx, p.st_mx, p.st_mn, ALU.max))
        _ser(v, ve, v.tensor_tensor(p.st_mx, p.st_mx, p.st_rstd, ALU.mult))
        _ser(v, ve, v.reciprocal(p.st_c1, p.st_mx))
        _ser(v, ve, v.tensor_scalar(p.st_c1, p.st_c1, 127.0, None, ALU.mult))
        _ser(v, ve, v.tensor_tensor(p.st_dq1, p.st_t1, p.st_mx, ALU.mult))
        v.tensor_scalar(p.st_dq1, p.st_dq1, 1.0 / 127.0, None,
                        ALU.mult).then_inc(ve.h, 1)
        ve.inc(1)
    p.V(vst3)


def _phase2(p, groups, peel):
    dx, dw, dsp = p.sem["dx"], p.sem["dw"], p.sem["dsp"]
    te, ve, se, dc = p.sem["te"], p.sem["ve"], p.sem["se"], p.sem["dc"]

    def wc(t):
        t.wait_ge(dc.h, dc.n)
    p.T(wc)

    for g4 in groups:
        own = g4 >= 2
        for j in range(4):
            i = g4 * 4 + j

            def load(sy, i=i):
                sy.wait_ge(ve.h, ve.n)
                sy.dma_start(out=p.xt0, in_=p.xin[i * 128:(i + 1) * 128, :]
                             ).then_inc(dx.h, 16)
                dx.inc(16)
            p.SY(load)

            def vxn(v, i=i):
                v.wait_ge(dx.h, dx.n)
                v.tensor_scalar(p.xnt, p.xt0, p.st_mu[:, i:i + 1],
                                p.st_rstd[:, i:i + 1], ALU.subtract,
                                ALU.mult).then_inc(ve.h, 1)
                ve.inc(1)
            p.V(vxn)

            for cq in range(4):
                def ttr(t, cq=cq):
                    t.wait_ge(ve.h, ve.n)
                    for cc in range(4):
                        c = cq * 4 + cc
                        t.transpose(p.pk[cq % 2][:, cc * 128:(cc + 1) * 128],
                                    p.xnt[:, c * 128:(c + 1) * 128],
                                    p.identf).then_inc(te.h, 1)
                        te.inc(1)
                p.T(ttr)

                def vev(v, cq=cq, j=j):
                    v.wait_ge(te.h, te.n)
                    for cc in range(4):
                        c = cq * 4 + cc
                        v.tensor_copy(
                            p.xnT[:, c * 512 + j * 128: c * 512 + (j + 1) * 128],
                            p.pk[cq % 2][:, cc * 128:(cc + 1) * 128])
                    v.memset(p.st_t0[:, 15:16], 0.0).then_inc(ve.h, 1)
                    ve.inc(1)
                p.V(vev)

            if own and peel:
                _peel1(p, i)

        def ldwk(sy, g4=g4):
            sy.wait_ge(te.h, te.n)
            sy.dma_start(out=p.wbuf.rearrange("p (c n) -> p c n", c=NDC),
                         in_=p.Wk.rearrange("(c p) n -> p c n", p=128)
                         ).then_inc(dw.h, 16)
            dw.inc(16)
        p.SY(ldwk)

        for kv in range(NKV):
            def tk(t, kv=kv):
                t.wait_ge(dw.h, dw.n)
                t.wait_ge(ve.h, ve.n)
                for c in range(NDC):
                    mmi = t.matmul(p.pk[2][:, :],
                                   p.wbuf[:, c * 512 + kv * 128:
                                          c * 512 + (kv + 1) * 128],
                                   p.xnT[:, c * 512:(c + 1) * 512],
                                   start=(c == 0), stop=(c == NDC - 1))
                    if c == NDC - 1:
                        mmi.then_inc(te.h, 1)
                te.inc(1)
            p.T(tk)

            def vk(v, kv=kv, g4=g4):
                v.wait_ge(te.h, te.n)
                dsth = p.kh[:, kv * TT + g4 * 512: kv * TT + (g4 + 1) * 512]
                dstl = p.kl[:, kv * TT + g4 * 512: kv * TT + (g4 + 1) * 512]
                _ser(v, ve, v.tensor_copy(dsth, p.pk[2][:, :]))
                v.tensor_tensor(dstl, p.pk[2][:, :], dsth,
                                ALU.subtract).then_inc(ve.h, 1)
                ve.inc(1)
            p.V(vk)

        def ldwv(sy):
            sy.wait_ge(ve.h, ve.n)
            sy.dma_start(out=p.wbuf.rearrange("p (c n) -> p c n", c=NDC),
                         in_=p.Wv.rearrange("(c p) n -> p c n", p=128)
                         ).then_inc(dw.h, 16)
            dw.inc(16)
        p.SY(ldwv)

        for j in range(4):
            kt = g4 * 4 + j

            def tv(t, j=j):
                t.wait_ge(dw.h, dw.n)
                t.wait_ge(ve.h, ve.n)
                for c in range(NDC):
                    mmi = t.matmul(p.pk[3][:, :],
                                   p.xnT[:, c * 512 + j * 128:
                                         c * 512 + (j + 1) * 128],
                                   p.wbuf[:, c * 512:(c + 1) * 512],
                                   start=(c == 0), stop=(c == NDC - 1))
                    if c == NDC - 1:
                        mmi.then_inc(te.h, 1)
                te.inc(1)
            p.T(tv)

            def vv(v, kt=kt):
                v.wait_ge(te.h, te.n)
                for kvh in range(NKV):
                    v.tensor_copy(
                        p.vtm[:, (kt * NKV + kvh) * 128:(kt * NKV + kvh + 1) * 128],
                        p.pk[3][:, kvh * 128:(kvh + 1) * 128])
                v.memset(p.st_t0[:, 15:16], 0.0).then_inc(ve.h, 1)
                ve.inc(1)
            p.V(vv)


def _peel1(p, i):
    ve, te, dsp = p.sem["ve"], p.sem["te"], p.sem["dsp"]
    ot = i - 8

    def vpeel(v, i=i):
        for hh in range(2):
            sl = slice(hh * 1024, (hh + 1) * 1024)
            v.tensor_scalar(p.puP, p.xnt[:, sl], p.st_c1[:, i:i + 1], 128.5,
                            ALU.mult, ALU.add)
            v.tensor_copy(p.scrP, p.puP)
            for c in [128.0, 64.0, 32.0, 16.0, 8.0, 4.0, 2.0, 1.0]:
                v.tensor_scalar(p.pbP, p.puP, c, None, ALU.is_ge)
                v.scalar_tensor_tensor(p.puP, p.pbP, -c, p.puP, ALU.mult, ALU.add)
            v.scalar_tensor_tensor(p.scrb[:, sl], p.scrP, 128.0, p.puP,
                                   ALU.subtract, ALU.subtract)
        v.memset(p.st_t0[:, 15:16], 0.0).then_inc(ve.h, 1)
        ve.inc(1)
    p.V(vpeel)

    pb16 = p.pk[5][:].bitcast(BF16)
    for cq in range(4):
        def ttr(t, cq=cq):
            t.wait_ge(ve.h, ve.n)
            for cc in range(4):
                c = cq * 4 + cc
                t.transpose(pb16[:, cc * 128:(cc + 1) * 128],
                            p.scrb[:, c * 128:(c + 1) * 128],
                            p.identb).then_inc(te.h, 1)
                te.inc(1)
        p.T(ttr)

        def vev(v, cq=cq):
            v.wait_ge(te.h, te.n)
            if cq == 0:
                v.wait_ge(dsp.h, dsp.n)
            for cc in range(4):
                c = cq * 4 + cc
                v.tensor_copy(p.xqst[:, c * 128:(c + 1) * 128],
                              pb16[:, cc * 128:(cc + 1) * 128])
            v.memset(p.st_t0[:, 15:16], 0.0).then_inc(ve.h, 1)
            ve.inc(1)
        p.V(vev)

    def spill(g, ot=ot):
        g.wait_ge(ve.h, ve.n)
        g.dma_start(out=p.xqsp[ot, :, :], in_=p.xqst).then_inc(dsp.h, 16)
        dsp.inc(16)
    p.G(spill)


def _phase3(p, qt):
    dw, te, ve, se, dsp = (p.sem["dw"], p.sem["te"], p.sem["ve"], p.sem["se"],
                           p.sem["dsp"])
    Qloc = HALO + qt * 512
    qtile = Qloc // 128

    for h in range(NH):
        g = h // GROUP
        kv = h // GROUP
        W = WINDOWS[g]
        kt0 = (Qloc - W) // 128
        kt1 = Qloc // 128 + 4
        kts = [kt for kt in range(kt0, kt1) if kt >= 0]

        wslot = p.wq0 if h % 2 == 0 else p.wq1

        def ldwq(sy, h=h, wslot=wslot):
            sy.wait_ge(te.h, te.n)
            sy.dma_start(out=wslot.rearrange("p (c n) -> p c n", c=NDC),
                         in_=p.Wq[:, h * 128:(h + 1) * 128].rearrange(
                             "(c p) n -> p c n", p=128)).then_inc(dw.h, 16)
            dw.inc(16)
        p.SY(ldwq)

        def tq(t, wslot=wslot):
            t.wait_ge(dw.h, dw.n)
            t.wait_ge(ve.h, ve.n)
            for c in range(NDC):
                mmi = t.matmul(p.pk[4][:, :],
                               wslot[:, c * 128:(c + 1) * 128],
                               p.xnT[:, c * 512:(c + 1) * 512],
                               start=(c == 0), stop=(c == NDC - 1))
                if c == NDC - 1:
                    mmi.then_inc(te.h, 1)
            te.inc(1)
        p.T(tq)

        def vq(v):
            v.wait_ge(te.h, te.n)
            _ser(v, ve, v.tensor_scalar(p.pbuf[:, 0:512], p.pk[4][:, :], ISQD,
                                        None, ALU.mult))
            _ser(v, ve, v.tensor_copy(p.qhb, p.pbuf[:, 0:512]))
            v.tensor_tensor(p.qlb, p.pbuf[:, 0:512], p.qhb,
                            ALU.subtract).then_inc(ve.h, 1)
            ve.inc(1)
        p.V(vq)

        nv = len(kts)
        for vi, kt in enumerate(kts):
            sb_ = p.pk[vi % 2]
            diag = kt * 128 >= Qloc

            def tsc(t, kt=kt, vi=vi, sb_=sb_, diag=diag, g=g, kv=kv, h=h,
                    qt=qt, nv=nv):
                if vi == 0:
                    t.wait_ge(ve.h, ve.n)
                if vi >= 2:
                    t.wait_ge(se.h, se.n - 1)
                ksl = slice(kv * TT + kt * 128, kv * TT + (kt + 1) * 128)
                t.matmul(sb_[:, :], p.kh[:, ksl], p.qhb, start=True, stop=False)
                t.matmul(sb_[:, :], p.kh[:, ksl], p.qlb, start=False, stop=False)
                t.matmul(sb_[:, :], p.kl[:, ksl], p.qhb, start=False, stop=False)
                la = p.attl[:, (g * NKT + kt) * 128:(g * NKT + kt + 1) * 128]
                ra = p.attr[:, (g * 2 + qt) * 512:(g * 2 + qt + 1) * 512]
                if diag:
                    t.matmul(sb_[:, :], la, ra, start=False, stop=False)
                    joff = (kt * 128 - Qloc) // 128
                    msl = p.maskb[:, 384 - 128 * joff: 384 - 128 * joff + 512]
                    t.matmul(sb_[:, :], p.identb, msl, start=False,
                             stop=True).then_inc(te.h, 1)
                else:
                    t.matmul(sb_[:, :], la, ra, start=False,
                             stop=True).then_inc(te.h, 1)
                te.inc(1)
            p.T(tsc)

            def sexp(s, vi=vi, sb_=sb_):
                s.wait_ge(te.h, te.n)
                s.activation(p.pbuf[:, (vi % 2) * 512:(vi % 2 + 1) * 512],
                             sb_[:, :], AF.Exp).then_inc(se.h, 1)
                se.inc(1)
            p.S(sexp)

            def tav(t, vi=vi, kt=kt, kv=kv, nv=nv):
                t.wait_ge(se.h, se.n)
                pb = p.pbuf[:, (vi % 2) * 512:(vi % 2 + 1) * 512]
                vsl = p.vtm[:, (kt * NKV + kv) * 128:(kt * NKV + kv + 1) * 128]
                t.matmul(p.pk[2][:, :], vsl, pb,
                         start=(vi == 0), stop=(vi == nv - 1))
                mmi = t.matmul(p.pk[3][0:1, :], p.onesf[:, 0:1], pb,
                               start=(vi == 0), stop=(vi == nv - 1))
                mmi.then_inc(te.h, 1)
                te.inc(1)
            p.T(tav)

        def vden(v):
            v.wait_ge(te.h, te.n)
            _ser(v, ve, v.tensor_copy(p.denrow, p.pk[3][0:1, :]))
            v.reciprocal(p.recrow, p.denrow).then_inc(ve.h, 1)
            ve.inc(1)
        p.V(vden)

        def trb(t):
            t.wait_ge(ve.h, ve.n)
            t.matmul(p.pk[5][:, :], p.onesr, p.recrow, start=True,
                     stop=True).then_inc(te.h, 1)
            te.inc(1)
        p.T(trb)

        def vyev(v, h=h):
            v.wait_ge(te.h, te.n)
            v.tensor_copy(p.rbc, p.pk[5][:, :])
            v.scalar_tensor_tensor(p.ysb[:, h * 512:(h + 1) * 512],
                                   p.pk[2][:, :], 0.0, p.rbc,
                                   ALU.add, ALU.mult).then_inc(ve.h, 1)
            ve.inc(1)
        p.V(vyev)

        if p.debug and qt == 0:
            def dden(g_, h=h):
                g_.wait_ge(ve.h, ve.n)
                g_.dma_start(out=p.dbg["d_den"][:, h * 512:(h + 1) * 512],
                             in_=p.denrow).then_inc(p.sem["dst"].h, 16)
                p.sem["dst"].inc(16)
            p.G(dden)

    for dc_ in range(NDC):
        def ldwo(sy, dc_=dc_):
            sy.wait_ge(te.h, te.n)
            slot = p.wq0 if dc_ % 2 == 0 else p.wq1
            sy.dma_start(out=slot.rearrange("p (c n) -> p c n", c=NDC),
                         in_=p.Wo[:, dc_ * 128:(dc_ + 1) * 128].rearrange(
                             "(c p) n -> p c n", p=128)).then_inc(dw.h, 16)
            dw.inc(16)
        p.SY(ldwo)

        def two(t, dc_=dc_):
            t.wait_ge(dw.h, dw.n)
            if dc_ == 0:
                t.wait_ge(ve.h, ve.n)
            elif dc_ >= 2:
                t.wait_ge(ve.h, ve.n - 1)
            slot = p.wq0 if dc_ % 2 == 0 else p.wq1
            bank = p.pk[6] if dc_ % 2 == 0 else p.pk[7]
            for c in range(NDC):
                mmi = t.matmul(bank[:, :], slot[:, c * 128:(c + 1) * 128],
                               p.ysb[:, c * 512:(c + 1) * 512],
                               start=(c == 0), stop=(c == NDC - 1))
                if c == NDC - 1:
                    mmi.then_inc(te.h, 1)
            te.inc(1)
        p.T(two)

        def vso(v, dc_=dc_):
            v.wait_ge(te.h, te.n)
            if dc_ >= 2:
                v.wait_ge(dsp.h, dsp.n - 16)
            bank = p.pk[6] if dc_ % 2 == 0 else p.pk[7]
            v.tensor_copy(p.pbuf[:, (dc_ % 2) * 512:(dc_ % 2 + 1) * 512],
                          bank[:, :]).then_inc(ve.h, 1)
            ve.inc(1)
        p.V(vso)

        def sspill(g_, dc_=dc_, qt=qt):
            g_.wait_ge(ve.h, ve.n)
            g_.dma_start(out=p.sTsp[dc_, :, qt * 512:(qt + 1) * 512],
                         in_=p.pbuf[:, (dc_ % 2) * 512:(dc_ % 2 + 1) * 512]
                         ).then_inc(dsp.h, 16)
            dsp.inc(16)
        p.G(sspill)


def _col2row(p, src_ap, ncols, psbank, dst_row):
    te, ve, dsp = p.sem["te"], p.sem["ve"], p.sem["dsp"]

    def tt_(t):
        t.wait_ge(ve.h, ve.n)
        t.transpose(psbank[0:ncols, 0:128], src_ap, p.identf).then_inc(te.h, 1)
        te.inc(1)
    p.T(tt_)

    def vv_(v):
        v.wait_ge(te.h, te.n)
        v.wait_ge(dsp.h, dsp.n)
        v.tensor_copy(p.pe2c[0:ncols, 0:128], psbank[0:ncols, 0:128]
                      ).then_inc(ve.h, 1)
        ve.inc(1)
    p.V(vv_)

    def gg_(g):
        g.wait_ge(ve.h, ve.n)
        g.dma_start(out=dst_row.rearrange("a (c b) -> a c b", b=128),
                    in_=p.pe2c[0:ncols, 0:128],
                    allow_slow_non_contiguous=True).then_inc(dsp.h, 16)
        dsp.inc(16)
    p.G(gg_)


def _ternarize(p, src_f32, dst_bf16, thr_col, width):
    se, ve = p.sem["se"], p.sem["ve"]

    def s_(s):
        s.wait_ge(ve.h, ve.n)
        s.activation(dst_bf16, src_f32, AF.Sign,
                     bias=p.thrn[:, thr_col:thr_col + 1])
        s.activation(p.tmpb[:, 0:width], src_f32, AF.Sign,
                     bias=p.thrc[:, thr_col:thr_col + 1]).then_inc(se.h, 1)
        se.inc(1)
    p.S(s_)

    def v_(v):
        v.wait_ge(se.h, se.n)
        v.tensor_tensor(dst_bf16, dst_bf16, p.tmpb[:, 0:width],
                        ALU.add).then_inc(ve.h, 1)
        ve.inc(1)
    p.V(v_)


def _phase4(p):
    dw, dh2, te, ve, se, dsp = (p.sem["dw"], p.sem["dh2"], p.sem["te"],
                                p.sem["ve"], p.sem["se"], p.sem["dsp"])
    dq, dx, dst = p.sem["dq"], p.sem["dx"], p.sem["dst"]

    for t4 in range(16):
        def ldw(sy, t4=t4):
            sy.wait_ge(se.h, se.n)
            sy.wait_ge(te.h, te.n)
            if t4 == 0:
                sy.wait_ge(dsp.h, dsp.n)
            sy.dma_start(out=p.wfsl.rearrange("p (s d) -> p s d", s=4),
                         in_=p.w1[t4 * 512:(t4 + 1) * 512, :].rearrange(
                             "(s p) d -> p s d", p=128)).then_inc(dw.h, 16)
            dw.inc(16)
        p.SY(ldw)

        def sab(s, t4=t4):
            s.wait_ge(dw.h, dw.n)
            for sub in range(4):
                s.activation(p.tmpb, p.wfsl[:, sub * D:(sub + 1) * D], AF.Abs,
                             accum_out=p.wacc[:, t4 * 4 + sub:t4 * 4 + sub + 1])
            s.memzero(p.st_t1[:, 15:16]).then_inc(se.h, 1)
            se.inc(1)
        p.S(sab)
    for t4 in range(16):
        def ldw2(sy, t4=t4):
            sy.wait_ge(se.h, se.n)
            sy.wait_ge(te.h, te.n)
            sy.dma_start(out=p.w2sl, in_=p.w2[t4 * 128:(t4 + 1) * 128, :]
                         ).then_inc(dw.h, 16)
            dw.inc(16)
        p.SY(ldw2)

        def sab2(s, t4=t4):
            s.wait_ge(dw.h, dw.n)
            for sub in range(4):
                s.activation(p.tmpb, p.w2sl[:, sub * D:(sub + 1) * D], AF.Abs,
                             accum_out=p.wacc[:, 64 + t4 * 4 + sub:
                                              64 + t4 * 4 + sub + 1])
            s.memzero(p.st_t1[:, 15:16]).then_inc(se.h, 1)
            se.inc(1)
        p.S(sab2)

    if p.debug == 11:
        return

    def vtot(v):
        v.wait_ge(se.h, se.n)
        v.reduce_sum(p.st_t0[:, 0:1], p.wacc[:, 0:64], axis=AX.X)
        v.reduce_sum(p.st_t0[:, 1:2], p.wacc[:, 64:128], axis=AX.X
                     ).then_inc(ve.h, 1)
        ve.inc(1)
    p.V(vtot)

    def ttot(t):
        t.wait_ge(ve.h, ve.n)
        t.matmul(p.pk[0][0:1, 0:2], p.onesf[:, 0:1], p.st_t0[:, 0:2],
                 start=True, stop=True).then_inc(te.h, 1)
        te.inc(1)
    p.T(ttot)

    def vthr(v):
        v.wait_ge(te.h, te.n)
        v.tensor_scalar(p.rowd[:, 0:2], p.pk[0][0:1, 0:2],
                        0.5 / (F_INNER * D), None, ALU.mult).then_inc(ve.h, 1)
        ve.inc(1)
    p.V(vthr)

    def tthr(t):
        t.wait_ge(ve.h, ve.n)
        t.matmul(p.pk[1][:, 0:2], p.onesr, p.rowd[:, 0:2],
                 start=True, stop=True)
        t.matmul(p.pk[2][0:1, 0:128], p.rowd[:, 0:1], p.onesr,
                 start=True, stop=True)
        t.matmul(p.pk[3][0:1, 0:128], p.rowd[:, 1:2], p.onesr,
                 start=True, stop=True).then_inc(te.h, 1)
        te.inc(1)
    p.T(tthr)

    def vthr2(v):
        v.wait_ge(te.h, te.n)
        _ser(v, ve, v.tensor_copy(p.thrc[:, 0:2], p.pk[1][:, 0:2]))
        _ser(v, ve, v.tensor_scalar(p.thrn[:, 0:2], p.pk[1][:, 0:2], -1.0,
                                    None, ALU.mult))
        v.tensor_copy(p.thr1row, p.pk[2][0:1, 0:128])
        v.tensor_copy(p.thr2row, p.pk[3][0:1, 0:128]).then_inc(ve.h, 1)
        ve.inc(1)
    p.V(vthr2)

    if p.debug == 12:
        return
    _col2row(p, p.st_dq1[:, 8:16], 8, p.pk[4], p.dq1row)
    if p.debug == 15:
        return
    for tt in range(2):
        def tbc(t, tt=tt):
            t.wait_ge(dsp.h, dsp.n)
            t.wait_ge(ve.h, ve.n)
            t.matmul(p.pk[5][:, :], p.thr1row,
                     p.dq1row[:, tt * 512:(tt + 1) * 512],
                     start=True, stop=True).then_inc(te.h, 1)
            te.inc(1)
        p.T(tbc)

        def vbc(v, tt=tt):
            v.wait_ge(te.h, te.n)
            v.tensor_copy(p.bc1[:, tt * 512:(tt + 1) * 512],
                          p.pk[5][:, :]).then_inc(ve.h, 1)
            ve.inc(1)
        p.V(vbc)

    if p.debug == 13:
        return

    def swst(s):
        s.wait_ge(ve.h, ve.n)
        s.activation(p.wstA.rearrange("p (c k) -> p c k", k=3)[:, :, 1],
                     p.lngs, AF.Square, accum_out=p.st_t1[:, 0:1])
        s.activation(p.wstB.rearrange("p (c k) -> p c k", k=2)[:, :, 1],
                     p.lngs, AF.Square).then_inc(se.h, 1)
        se.inc(1)
    p.S(swst)

    def vwst(v):
        v.wait_ge(se.h, se.n)
        v.memset(p.pe2c[:, 64:128], 1.0)
        v.tensor_copy(p.wstA.rearrange("p (c k) -> p c k", k=3)[:, :, 0],
                      p.pe2c[:, 64:128])
        v.tensor_copy(p.wstB.rearrange("p (c k) -> p c k", k=2)[:, :, 0],
                      p.pe2c[:, 64:128])
        v.tensor_tensor(p.wstA.rearrange("p (c k) -> p c k", k=3)[:, :, 2],
                        p.lngs, p.lnbs, ALU.mult)
        _ser(v, ve, v.tensor_tensor(p.pe2c[:, 0:64], p.lngs, p.lnbs, ALU.mult))
        _ser(v, ve, v.reduce_sum(p.st_t1[:, 1:2], p.pe2c[:, 0:64], axis=AX.X))
        _ser(v, ve, v.tensor_tensor(p.pe2c[:, 0:64], p.lnbs, p.lnbs, ALU.mult))
        v.reduce_sum(p.st_t1[:, 2:3], p.pe2c[:, 0:64], axis=AX.X
                     ).then_inc(ve.h, 1)
        ve.inc(1)
    p.V(vwst)

    def tq3(t):
        t.wait_ge(ve.h, ve.n)
        t.matmul(p.pk[0][0:1, 0:3], p.onesf[:, 0:1], p.st_t1[:, 0:3],
                 start=True, stop=True).then_inc(te.h, 1)
        te.inc(1)
    p.T(tq3)

    def vq4b(v):
        v.wait_ge(te.h, te.n)
        v.tensor_copy(p.rowd[:, 8:11], p.pk[0][0:1, 0:3]).then_inc(ve.h, 1)
        ve.inc(1)
    p.V(vq4b)

    def tq4(t):
        t.wait_ge(ve.h, ve.n)
        t.matmul(p.pk[1][:, 0:3], p.onesr, p.rowd[:, 8:11],
                 start=True, stop=True).then_inc(te.h, 1)
        te.inc(1)
    p.T(tq4)

    def vq5(v):
        v.wait_ge(te.h, te.n)
        v.tensor_copy(p.qcols[:, 0:3], p.pk[1][:, 0:3]).then_inc(ve.h, 1)
        ve.inc(1)
    p.V(vq5)

    if p.debug == 14:
        return
    for ot in range(NOWN):
        def ldxq(g, ot=ot):
            g.wait_ge(te.h, te.n)
            g.dma_start(
                out=p.xqTr.rearrange("p (c n) -> p c n", c=NDC)[
                    :, :, ot * 128:(ot + 1) * 128],
                in_=p.xqsp[ot].rearrange("p (c n) -> p c n", c=NDC)
            ).then_inc(dq.h, 16)
            dq.inc(16)
        p.G(ldxq)

    if p.debug == 8:
        return
    for sl in range(NSL):
        def ldw(sy, sl=sl):
            sy.wait_ge(se.h, se.n)
            sy.dma_start(out=p.wfsl.rearrange("p (s d) -> p s d", s=4),
                         in_=p.w1[sl * FSL:(sl + 1) * FSL, :].rearrange(
                             "(s p) d -> p s d", p=128)).then_inc(dw.h, 16)
            dw.inc(16)
        p.SY(ldw)

        def sww(s):
            s.wait_ge(dw.h, dw.n)
        p.S(sww)
        for sub in range(4):
            _ternarize(p, p.wfsl[:, sub * D:(sub + 1) * D],
                       p.wtern[:, sub * D:(sub + 1) * D], 0, D)
        pb16 = p.pk[6][:].bitcast(BF16)
        for c in range(NDC):
            def ttr(t, c=c):
                t.wait_ge(ve.h, ve.n)
                for sub in range(4):
                    t.transpose(pb16[:, sub * 128:(sub + 1) * 128],
                                p.wtern[:, sub * D + c * 128:
                                        sub * D + (c + 1) * 128],
                                p.identb).then_inc(te.h, 1)
                    te.inc(1)
            p.T(ttr)

            def vev(v, c=c):
                v.wait_ge(te.h, te.n)
                v.tensor_copy(p.wqT[:, c * 512:(c + 1) * 512],
                              pb16[:, 0:512]).then_inc(ve.h, 1)
                ve.inc(1)
            p.V(vev)

        if sl == 0:
            def wxq(t):
                t.wait_ge(dq.h, dq.n)
            p.T(wxq)
        for k in range(8):
            tt, sub = k // 4, k % 4

            def tmm(t, k=k, tt=tt, sub=sub, sl=sl):
                t.wait_ge(ve.h, ve.n)
                if sl > 0:
                    t.wait_ge(se.h, se.n)
                for c in range(NDC):
                    mmi = t.matmul(
                        p.pk[k][:, :],
                        p.wqT[:, c * 512 + sub * 128:c * 512 + (sub + 1) * 128],
                        p.xqTr[:, c * TOK + tt * 512:c * TOK + (tt + 1) * 512],
                        start=(c == 0), stop=(c == NDC - 1))
                    if c == NDC - 1:
                        mmi.then_inc(te.h, 1)
                te.inc(1)
            p.T(tmm)

            def vdq(v, k=k, tt=tt):
                v.wait_ge(te.h, te.n)
                v.scalar_tensor_tensor(
                    p.hbuf[:, k * 512:(k + 1) * 512], p.pk[k][:, :], 0.0,
                    p.bc1[:, tt * 512:(tt + 1) * 512], ALU.add,
                    ALU.mult).then_inc(ve.h, 1)
                ve.inc(1)
            p.V(vdq)

        def ssilu(s, sl=sl):
            s.wait_ge(ve.h, ve.n)
            for k in range(8):
                sub = k % 4
                fcol = sl * 4 + sub
                s.activation(p.pk[k][:, :], p.hbuf[:, k * 512:(k + 1) * 512],
                             AF.Silu, bias=p.b1s[:, fcol:fcol + 1])
            s.memzero(p.st_t1[:, 15:16]).then_inc(se.h, 1)
            se.inc(1)
        p.S(ssilu)

        def sgelu(s, sl=sl):
            for k in range(8):
                s.activation(p.hbuf[:, k * 512:(k + 1) * 512], p.pk[k][:, :],
                             AF.Gelu)
            s.memzero(p.st_t1[:, 15:16]).then_inc(se.h, 1)
            se.inc(1)
        p.S(sgelu)

        for k in range(8):
            tt, sub = k // 4, k % 4

            def gsp(g, k=k, tt=tt, sub=sub, sl=sl):
                g.wait_ge(se.h, se.n)
                fc = sl * 4 + sub
                g.dma_start(out=p.h2sp[fc, :, tt * 512:(tt + 1) * 512],
                            in_=p.hbuf[:, k * 512:(k + 1) * 512]
                            ).then_inc(dh2.h, 16)
                p.sem["dh2"].inc(16)
            p.G(gsp)

    if p.debug == 9:
        return
    dh = p.sem["dh3"]
    dh2s = p.sem["dh2"]
    for fc in range(NFC):
        def ldh(sc, fc=fc):
            if fc == 0:
                sc.wait_ge(dh2s.h, dh2s.n)
            sc.wait_ge(te.h, te.n)
            sc.dma_start(out=p.h2st, in_=p.h2sp[fc]).then_inc(dh.h, 16)
            dh.inc(16)
        p.S(ldh)

        for tt in range(2):
            def vcp(v, tt=tt):
                v.wait_ge(dh.h, dh.n)
                v.wait_ge(te.h, te.n)
                v.tensor_copy(p.statu,
                              p.h2st[:, tt * 512:(tt + 1) * 512]
                              ).then_inc(ve.h, 1)
                ve.inc(1)
            p.V(vcp)

            def ssq(s, tt=tt):
                s.wait_ge(dh.h, dh.n)
                s.wait_ge(te.h, te.n)
                s.activation(p.statq,
                             p.h2st[:, tt * 512:(tt + 1) * 512],
                             AF.Square).then_inc(se.h, 1)
                se.inc(1)
            p.S(ssq)

            def tst(t, fc=fc, tt=tt):
                t.wait_ge(ve.h, ve.n)
                t.wait_ge(se.h, se.n)
                t.matmul(p.pk[0 + tt][0:3, :], p.wstA[:, fc * 3:(fc + 1) * 3],
                         p.statu, start=(fc == 0),
                         stop=(fc == NFC - 1))
                mmi = t.matmul(p.pk[2 + tt][0:2, :],
                               p.wstB[:, fc * 2:(fc + 1) * 2],
                               p.statq, start=(fc == 0),
                               stop=(fc == NFC - 1))
                mmi.then_inc(te.h, 1)
                te.inc(1)
            p.T(tst)

    for tt in range(2):
        def vevs(v, tt=tt):
            v.wait_ge(te.h, te.n)
            v.tensor_copy(p.pe2c[0:3, 0:512], p.pk[0 + tt][0:3, :])
            v.tensor_copy(p.pe2u[0:2, 0:512], p.pk[2 + tt][0:2, :]
                          ).then_inc(ve.h, 1)
            ve.inc(1)
        p.V(vevs)
        for j in range(4):
            def tts(t, tt=tt, j=j):
                t.wait_ge(ve.h, ve.n)
                t.transpose(p.pk[4][:, 0:3], p.pe2c[0:3, j * 128:(j + 1) * 128],
                            p.identf[0:3, 0:3])
                t.transpose(p.pk[5][:, 0:2], p.pe2u[0:2, j * 128:(j + 1) * 128],
                            p.identf[0:2, 0:2]).then_inc(te.h, 1)
                te.inc(1)
            p.T(tts)

            def vcs(v, tt=tt, j=j):
                v.wait_ge(te.h, te.n)
                col = tt * 4 + j
                v.tensor_copy(p.s2_S1[:, col:col + 1], p.pk[4][:, 0:1])
                v.tensor_copy(p.s2_SG[:, col:col + 1], p.pk[4][:, 1:2])
                v.tensor_copy(p.s2_SB[:, col:col + 1], p.pk[4][:, 2:3])
                v.tensor_copy(p.s2_S2[:, col:col + 1], p.pk[5][:, 0:1])
                v.tensor_copy(p.s2_SG2[:, col:col + 1], p.pk[5][:, 1:2]
                              ).then_inc(ve.h, 1)
                ve.inc(1)
            p.V(vcs)

    IFI = 1.0 / F_INNER

    def vf1(v):
        v.wait_ge(ve.h, ve.n)
        _ser(v, ve, v.tensor_scalar(p.s2_mu2, p.s2_S1, IFI, None, ALU.mult))
        _ser(v, ve, v.tensor_scalar(p.s2_tmp, p.s2_S2, IFI, None, ALU.mult))
        _ser(v, ve, v.tensor_tensor(p.s2_rstd2, p.s2_mu2, p.s2_mu2, ALU.mult))
        _ser(v, ve, v.tensor_tensor(p.s2_tmp, p.s2_tmp, p.s2_rstd2, ALU.subtract))
        v.tensor_scalar(p.s2_tmp, p.s2_tmp, EPS, None, ALU.add).then_inc(ve.h, 1)
        ve.inc(1)
    p.V(vf1)

    def sf1(s):
        s.wait_ge(ve.h, ve.n)
        s.activation(p.s2_rstd2, p.s2_tmp, AF.Sqrt).then_inc(se.h, 1)
        se.inc(1)
    p.S(sf1)

    def vf2(v):
        v.wait_ge(se.h, se.n)
        _ser(v, ve, v.reciprocal(p.s2_rstd2, p.s2_rstd2))
        _ser(v, ve, v.scalar_tensor_tensor(p.s2_tmp, p.s2_mu2, -2.0, p.s2_SG,
                                           ALU.mult, ALU.mult))
        _ser(v, ve, v.tensor_tensor(p.s2_tmp, p.s2_tmp, p.s2_SG2, ALU.add))
        _ser(v, ve, v.tensor_tensor(p.s2_r2, p.s2_mu2, p.s2_mu2, ALU.mult))
        _ser(v, ve, v.scalar_tensor_tensor(p.s2_tmp, p.s2_r2, p.qcols[:, 0:1],
                                           p.s2_tmp, ALU.mult, ALU.add))
        _ser(v, ve, v.tensor_tensor(p.s2_r2, p.s2_rstd2, p.s2_rstd2, ALU.mult))
        _ser(v, ve, v.tensor_tensor(p.s2_tmp, p.s2_tmp, p.s2_r2, ALU.mult))
        _ser(v, ve, v.scalar_tensor_tensor(p.s2_r2, p.s2_mu2, p.qcols[:, 1:2],
                                           p.s2_SB, ALU.mult, ALU.subtract))
        _ser(v, ve, v.tensor_tensor(p.s2_r2, p.s2_r2, p.s2_rstd2, ALU.mult))
        _ser(v, ve, v.scalar_tensor_tensor(p.s2_tmp, p.s2_r2, -2.0, p.s2_tmp,
                                           ALU.mult, ALU.add))
        _ser(v, ve, v.tensor_scalar(p.s2_tmp, p.s2_tmp, p.qcols[:, 2:3], None,
                                    ALU.add))
        v.tensor_scalar(p.s2_tmp, p.s2_tmp, 1.0 / F_INNER, EPS,
                        ALU.mult, ALU.add).then_inc(ve.h, 1)
        ve.inc(1)
    p.V(vf2)

    def sf2(s):
        s.wait_ge(ve.h, ve.n)
        s.activation(p.s2_tmp, p.s2_tmp, AF.Sqrt).then_inc(se.h, 1)
        se.inc(1)
    p.S(sf2)

    def vf3(v):
        v.wait_ge(se.h, se.n)
        v.reciprocal(p.s2_r2, p.s2_tmp).then_inc(ve.h, 1)
        ve.inc(1)
    p.V(vf3)

    for tt in range(2):
        _col2row(p, p.s2_rstd2[:, tt * 4:(tt + 1) * 4], 4, p.pk[6],
                 p.rowa)
        _col2row(p, p.s2_mu2[:, tt * 4:(tt + 1) * 4], 4, p.pk[6], p.rowc)

        def trb(t, tt=tt):
            t.wait_ge(dsp.h, dsp.n)
            t.matmul(p.pk[6][:, :], p.onesr, p.rowa, start=True, stop=True
                     ).then_inc(te.h, 1)
            te.inc(1)
        p.T(trb)

        def vrb(v, tt=tt):
            v.wait_ge(te.h, te.n)
            v.tensor_copy(p.RAb, p.pk[6][:, :])
            _ser(v, ve, v.tensor_tensor(p.rowd, p.rowc, p.rowa, ALU.mult))
            v.tensor_scalar(p.rowd, p.rowd, -1.0, None, ALU.mult
                            ).then_inc(ve.h, 1)
            ve.inc(1)
        p.V(vrb)

        def trc(t, tt=tt):
            t.wait_ge(ve.h, ve.n)
            t.matmul(p.pk[6][:, :], p.onesr, p.rowd, start=True, stop=True
                     ).then_inc(te.h, 1)
            te.inc(1)
        p.T(trc)

        def vrc(v, tt=tt):
            v.wait_ge(te.h, te.n)
            v.tensor_copy(p.RCb, p.pk[6][:, :])
            v.memset(p.s2_mxh[:, tt * 4:(tt + 1) * 4], 0.0).then_inc(ve.h, 1)
            ve.inc(1)
        p.V(vrc)

        for fc in range(NFC):
            def ldh(sc, fc=fc, tt=tt):
                sc.wait_ge(ve.h, ve.n)
                sc.dma_start(out=p.h2st[:, 0:512],
                             in_=p.h2sp[fc, :, tt * 512:(tt + 1) * 512]
                             ).then_inc(dh.h, 16)
                dh.inc(16)
            p.S(ldh)

            def vhn(v, fc=fc):
                v.wait_ge(dh.h, dh.n)
                gc = p.lngs[:, fc:fc + 1]
                bc = p.lnbs[:, fc:fc + 1]
                v.scalar_tensor_tensor(p.pe2u, p.h2st[:, 0:512], gc, p.RAb,
                                       ALU.mult, ALU.mult)
                v.scalar_tensor_tensor(p.pe2u, p.RCb, gc, p.pe2u,
                                       ALU.mult, ALU.add)
                v.tensor_scalar(p.pe2u, p.pe2u, bc, None, ALU.add
                                ).then_inc(ve.h, 1)
                ve.inc(1)
            p.V(vhn)

            for j in range(4):
                def tth(t, j=j):
                    if j == 0:
                        t.wait_ge(ve.h, ve.n)
                    t.transpose(p.pk[7][:, j * 128:(j + 1) * 128],
                                p.pe2u[:, j * 128:(j + 1) * 128],
                                p.identf).then_inc(te.h, 1)
                    te.inc(1)
                p.T(tth)

            def vmx(v, tt=tt, fc=fc):
                v.wait_ge(te.h, te.n)
                for j in range(4):
                    col = tt * 4 + j
                    _ser(v, ve, v.reduce_max(
                        p.s2_tmp[:, 0:1],
                        p.pk[7][:, j * 128:(j + 1) * 128], axis=AX.X,
                        apply_absolute_value=True))
                    _ser(v, ve, v.tensor_tensor(p.s2_mxh[:, col:col + 1],
                                                p.s2_mxh[:, col:col + 1],
                                                p.s2_tmp[:, 0:1], ALU.max))
                v.memset(p.st_t0[:, 15:16], 0.0).then_inc(ve.h, 1)
                ve.inc(1)
            p.V(vmx)

    if p.debug == 3:
        return
    def vf4(v):
        v.wait_ge(ve.h, ve.n)
        _ser(v, ve, v.reciprocal(p.s2_c2, p.s2_mxh))
        _ser(v, ve, v.tensor_scalar(p.s2_c2, p.s2_c2, 127.0, None, ALU.mult))
        _ser(v, ve, v.tensor_tensor(p.s2_dq2, p.s2_r2, p.s2_mxh, ALU.mult))
        v.tensor_scalar(p.s2_dq2, p.s2_dq2, 1.0 / 127.0, None, ALU.mult
                        ).then_inc(ve.h, 1)
        ve.inc(1)
    p.V(vf4)
    _col2row(p, p.s2_dq2, 8, p.pk[6], p.dq2row)

    for tt in range(2):
        _phase4c_tt(p, tt)
        if p.debug == 10 and tt == 0:
            return


def _phase4c_tt(p, tt):
    dw, dh, te, ve, se, dsp = (p.sem["dw"], p.sem["dh3"], p.sem["te"],
                               p.sem["ve"], p.sem["se"], p.sem["dsp"])
    dx, dst = p.sem["dx"], p.sem["dst"]

    def vr0(v):
        v.wait_ge(ve.h, ve.n)
        v.tensor_tensor(p.s2_tmp, p.s2_rstd2, p.s2_c2, ALU.mult
                        ).then_inc(ve.h, 1)
        ve.inc(1)
        v.wait_ge(ve.h, ve.n)
    p.V(vr0)
    _col2row(p, p.s2_tmp[:, tt * 4:(tt + 1) * 4], 4, p.pk[6], p.rowa)
    _col2row(p, p.s2_c2[:, tt * 4:(tt + 1) * 4], 4, p.pk[6], p.rowc)

    def vr1(v):
        v.wait_ge(dsp.h, dsp.n)
        _ser(v, ve, v.tensor_tensor(p.s2_tmp, p.s2_tmp, p.s2_mu2, ALU.mult))
        v.tensor_scalar(p.s2_tmp, p.s2_tmp, -1.0, None, ALU.mult
                        ).then_inc(ve.h, 1)
        ve.inc(1)
    p.V(vr1)
    _col2row(p, p.s2_tmp[:, tt * 4:(tt + 1) * 4], 4, p.pk[6], p.rowd)

    for nm, row, dstb in [("ra", "rowa", "RAb"), ("rb", "rowc", "RBb"),
                          ("rc", "rowd", "RCb")]:
        def tbb(t, row=row):
            t.wait_ge(dsp.h, dsp.n)
            t.wait_ge(ve.h, ve.n)
            t.matmul(p.pk[6][:, :], p.onesr, getattr(p, row), start=True,
                     stop=True).then_inc(te.h, 1)
            te.inc(1)
        p.T(tbb)

        def vbb(v, dstb=dstb):
            v.wait_ge(te.h, te.n)
            v.tensor_copy(getattr(p, dstb), p.pk[6][:, :]).then_inc(ve.h, 1)
            ve.inc(1)
        p.V(vbb)

    def tb2(t, tt=tt):
        t.wait_ge(ve.h, ve.n)
        t.matmul(p.pk[6][:, :], p.thr2row, p.dq2row[:, tt * 512:(tt + 1) * 512],
                 start=True, stop=True).then_inc(te.h, 1)
        te.inc(1)
    p.T(tb2)

    def vb2(v):
        v.wait_ge(te.h, te.n)
        v.tensor_copy(p.bc1[:, 0:512], p.pk[6][:, :]).then_inc(ve.h, 1)
        ve.inc(1)
    p.V(vb2)

    for fc in range(NFC):
        def ldh(sc, fc=fc, tt=tt):
            sc.wait_ge(ve.h, ve.n)
            sc.dma_start(out=p.h2st[:, 0:512],
                         in_=p.h2sp[fc, :, tt * 512:(tt + 1) * 512]
                         ).then_inc(dh.h, 16)
            dh.inc(16)
        p.S(ldh)

        def vq2(v, fc=fc):
            v.wait_ge(dh.h, dh.n)
            gc = p.lngs[:, fc:fc + 1]
            bc = p.lnbs[:, fc:fc + 1]
            v.scalar_tensor_tensor(p.pe2u, p.h2st[:, 0:512], gc, p.RAb,
                                   ALU.mult, ALU.mult)
            v.scalar_tensor_tensor(p.pe2u, p.RCb, gc, p.pe2u, ALU.mult, ALU.add)
            v.scalar_tensor_tensor(p.pe2u, p.RBb, bc, p.pe2u, ALU.mult, ALU.add)
            v.tensor_scalar(p.pe2b, p.pe2u, 128.5, None, ALU.add)
            v.tensor_copy(p.pe2u, p.pe2b)
            for c in [128.0, 64.0, 32.0, 16.0, 8.0, 4.0, 2.0, 1.0]:
                v.tensor_scalar(p.pe2c, p.pe2b, c, None, ALU.is_ge)
                v.scalar_tensor_tensor(p.pe2b, p.pe2c, -c, p.pe2b,
                                       ALU.mult, ALU.add)
            v.scalar_tensor_tensor(p.xq2T[:, fc * 512:(fc + 1) * 512],
                                   p.pe2u, 128.0, p.pe2b,
                                   ALU.subtract, ALU.subtract).then_inc(ve.h, 1)
            ve.inc(1)
        p.V(vq2)

    for ds in range(NDC):
        def ldw2(sy, ds=ds):
            sy.wait_ge(se.h, se.n)
            if ds == 0:
                sy.wait_ge(dst.h, dst.n)
            sy.dma_start(out=p.w2sl, in_=p.w2[ds * 128:(ds + 1) * 128, :]
                         ).then_inc(dw.h, 16)
            dw.inc(16)
        p.SY(ldw2)

        def sw2(s):
            s.wait_ge(dw.h, dw.n)
        p.S(sw2)
        for sub in range(4):
            _ternarize(p, p.w2sl[:, sub * D:(sub + 1) * D],
                       p.w2tn[:, sub * D:(sub + 1) * D], 1, D)
        pb16 = p.pk[6][:].bitcast(BF16)
        for fq in range(16):
            def ttr(t, fq=fq):
                t.wait_ge(ve.h, ve.n)
                for ff in range(4):
                    fc = fq * 4 + ff
                    t.transpose(pb16[:, ff * 128:(ff + 1) * 128],
                                p.w2tn[:, fc * 128:(fc + 1) * 128],
                                p.identb).then_inc(te.h, 1)
                    te.inc(1)
            p.T(ttr)

            def vev(v, fq=fq):
                v.wait_ge(te.h, te.n)
                v.tensor_copy(p.w2tT[:, fq * 512:(fq + 1) * 512],
                              pb16[:, 0:512]).then_inc(ve.h, 1)
                ve.inc(1)
            p.V(vev)

        bank = p.pk[7] if ds % 2 == 0 else p.pk[5]

        def tm2(t, ds=ds, bank=bank):
            t.wait_ge(ve.h, ve.n)
            if ds >= 2:
                t.wait_ge(se.h, se.n)
            for fc in range(NFC):
                mmi = t.matmul(bank[:, :],
                               p.w2tT[:, fc * 128:(fc + 1) * 128],
                               p.xq2T[:, fc * 512:(fc + 1) * 512],
                               start=(fc == 0), stop=(fc == NFC - 1))
                if fc == NFC - 1:
                    mmi.then_inc(te.h, 1)
            te.inc(1)
        p.T(tm2)

        def vdq2(v, ds=ds, bank=bank):
            v.wait_ge(te.h, te.n)
            v.wait_ge(se.h, se.n)
            v.scalar_tensor_tensor(p.pe2u, bank[:, :], 0.0, p.bc1[:, 0:512],
                                   ALU.add, ALU.mult).then_inc(ve.h, 1)
            ve.inc(1)
        p.V(vdq2)

        def sg2(s, ds=ds):
            s.wait_ge(ve.h, ve.n)
            s.activation(p.gkeep[:, ds * 512:(ds + 1) * 512], p.pe2u, AF.Silu,
                         bias=p.b2s[:, ds:ds + 1]).then_inc(se.h, 1)
            se.inc(1)
        p.S(sg2)

    for ds in range(NDC):
        def ldsT(sc, ds=ds, tt=tt):
            sc.wait_ge(ve.h, ve.n)
            sc.dma_start(out=p.sTr, in_=p.sTsp[ds, :, tt * 512:(tt + 1) * 512]
                         ).then_inc(p.sem["dw2"].h, 16)
            p.sem["dw2"].inc(16)
        p.S(ldsT)

        def vsg(v, ds=ds):
            v.wait_ge(p.sem["dw2"].h, p.sem["dw2"].n)
            v.wait_ge(se.h, se.n)
            v.tensor_tensor(p.xq2T[:, ds * 512:(ds + 1) * 512], p.sTr,
                            p.gkeep[:, ds * 512:(ds + 1) * 512],
                            ALU.mult).then_inc(ve.h, 1)
            ve.inc(1)
        p.V(vsg)

    for j in range(4):
        ot = tt * 4 + j

        def ldx(sy, ot=ot):
            sy.wait_ge(ve.h, ve.n)
            sy.dma_start(out=p.xrel,
                         in_=p.xin[(8 + ot) * 128:(9 + ot) * 128, :]
                         ).then_inc(dx.h, 16)
            dx.inc(16)
        p.SY(ldx)

        pb16g = p.pk[4][:].bitcast(BF16)
        for dq4 in range(4):
            def ttg(t, j=j, dq4=dq4):
                t.wait_ge(ve.h, ve.n)
                for dd in range(4):
                    ds = dq4 * 4 + dd
                    t.transpose(pb16g[:, dd * 128:(dd + 1) * 128],
                                p.xq2T[:, ds * 512 + j * 128:
                                       ds * 512 + (j + 1) * 128],
                                p.identb).then_inc(te.h, 1)
                    te.inc(1)
            p.T(ttg)

            def vout(v, dq4=dq4):
                v.wait_ge(te.h, te.n)
                v.wait_ge(dx.h, dx.n)
                if dq4 == 0:
                    v.wait_ge(dst.h, dst.n)
                for dd in range(4):
                    ds = dq4 * 4 + dd
                    v.tensor_tensor(p.ostg[:, ds * 128:(ds + 1) * 128],
                                    p.xrel[:, ds * 128:(ds + 1) * 128],
                                    pb16g[:, dd * 128:(dd + 1) * 128], ALU.add)
                v.memset(p.st_t0[:, 15:16], 0.0).then_inc(ve.h, 1)
                ve.inc(1)
            p.V(vout)

        def sto(g, ot=ot):
            g.wait_ge(ve.h, ve.n)
            g.dma_start(out=p.out[ot * 128:(ot + 1) * 128, :],
                        in_=p.ostg).then_inc(dst.h, 16)
            dst.inc(16)
        p.G(sto)



def _debug_dump(p):
    if not p.debug:
        return
    dst, ve, te, dsp = p.sem["dst"], p.sem["ve"], p.sem["te"], p.sem["dsp"]
    d = p.dbg

    def gdump(g):
        g.wait_ge(ve.h, ve.n)
        g.wait_ge(te.h, te.n)
        for s_ in ["dx", "dc", "dw", "dq", "dsp", "se", "ge", "dh2", "dh3",
                   "dw2"]:
            g.wait_ge(p.sem[s_].h, p.sem[s_].n)
        if p.debug == 2:
            for dst_ap, src in [(d["d_kh"], p.kh), (d["d_kl"], p.kl)]:
                g.dma_start(out=dst_ap.bitcast(BF16), in_=src).then_inc(dst.h, 16)
                dst.inc(16)
            g.dma_start(out=d["d_vtm"], in_=p.vtm).then_inc(dst.h, 16)
            dst.inc(16)
            g.dma_start(out=d["d_xq"].bitcast(BF16), in_=p.xqsp).then_inc(dst.h, 16)
            dst.inc(16)
        st_list = [p.st_mu, p.st_rstd, p.st_c1, p.st_dq1, p.st_mx, p.st_t1,
                   p.st_sx, p.st_sxx]
        for k, stt in enumerate(st_list):
            g.dma_start(out=d["d_st"][:, k * NT128:(k + 1) * NT128],
                        in_=stt).then_inc(dst.h, 16)
            dst.inc(16)
    p.G(gdump)


def _finish(p):
    dst = p.sem["dst"]

    def fin(g):
        g.wait_ge(dst.h, dst.n)
    p.G(fin)


def _host_consts(core):
    identf = np.eye(128, dtype=np.float32)
    identb = identf.astype(ml_dtypes.bfloat16)
    start = core * TOK
    attl = np.zeros((8, GROUP * NKT * 128), dtype=np.float64)
    attr = np.zeros((8, GROUP * 2 * 512), dtype=np.float64)
    attl[4:8] = 1.0
    for g in range(GROUP):
        s = SLOPE_G[g]
        for kt in range(NKT):
            kl = np.arange(128)
            kg = start - HALO + kt * 128 + kl
            a = s * (kt * 128 + kl - HALO).astype(np.float64)
            a[kg < 0] = NEG
            sp = _split_bf16(a, 4)
            for r in range(4):
                attl[r, (g * NKT + kt) * 128:(g * NKT + kt + 1) * 128] = \
                    sp[r].astype(np.float64)
        for qt in range(2):
            ql = np.arange(512)
            bv = -s * (qt * 512 + ql).astype(np.float64)
            sp = _split_bf16(bv, 4)
            col = (g * 2 + qt) * 512
            attr[0:4, col:col + 512] = 1.0
            for r in range(4):
                attr[4 + r, col:col + 512] = sp[r].astype(np.float64)
    maskb = np.zeros((128, 896), dtype=np.float64)
    for klp in range(128):
        maskb[klp, : klp + 384] = NEG
    return {
        "identf": identf, "identb": identb.view(np.uint16),
        "attl": attl.astype(ml_dtypes.bfloat16).view(np.uint16),
        "attr": attr.astype(ml_dtypes.bfloat16).view(np.uint16),
        "maskb": maskb.astype(ml_dtypes.bfloat16).view(np.uint16),
        "onesc": np.ones((128, 2), dtype=np.float32),
        "onesr": np.ones((1, 128), dtype=np.float32),
    }


def make_inmaps(inputs):
    x = np.ascontiguousarray(np.asarray(inputs["L0_attn_x"], dtype=np.float32)
                             .reshape(T_FULL, D))
    full = {
        "Wq": np.ascontiguousarray(inputs["L0_attn_Wq"], dtype=np.float32),
        "Wk": np.ascontiguousarray(inputs["L0_attn_Wk"], dtype=np.float32),
        "Wv": np.ascontiguousarray(inputs["L0_attn_Wv"], dtype=np.float32),
        "Wo": np.ascontiguousarray(inputs["L0_attn_Wo"], dtype=np.float32),
        "w1": np.ascontiguousarray(inputs["L0_ffn_w1"], dtype=np.float32),
        "b1": np.ascontiguousarray(inputs["L0_ffn_b1"], dtype=np.float32),
        "lng": np.ascontiguousarray(inputs["L0_ffn_ln_g"], dtype=np.float32),
        "lnb": np.ascontiguousarray(inputs["L0_ffn_ln_b"], dtype=np.float32),
        "w2": np.ascontiguousarray(inputs["L0_ffn_w2"], dtype=np.float32),
        "b2": np.ascontiguousarray(inputs["L0_ffn_b2"], dtype=np.float32),
    }
    maps = []
    for c in range(NCORES):
        lo = c * TOK - HALO
        if lo < 0:
            xs = np.concatenate([np.zeros((-lo, D), np.float32),
                                 x[: (c + 1) * TOK]], axis=0)
        else:
            xs = x[lo:(c + 1) * TOK]
        m = dict(full)
        m["xin"] = np.ascontiguousarray(xs)
        m.update(_host_consts(c))
        maps.append(m)
    return maps


def kernel(**inputs):
    nc = build_program(debug=0)
    maps = make_inmaps(inputs)
    res = run_bass_kernel_spmd(nc, maps, core_ids=list(range(NCORES)))
    out = np.concatenate([np.asarray(r["out"]) for r in res.results], axis=0)
    return out.reshape(1, T_FULL, D)


# version 05
# speedup vs baseline: 1.1829x; 1.1829x over previous
import math
import contextlib
import numpy as np
import ml_dtypes

import concourse.bass as bass
import concourse.mybir as mybir
from concourse.bass_utils import run_bass_kernel_spmd

F32 = mybir.dt.float32
F32R = mybir.dt.float32r
BF16 = mybir.dt.bfloat16
AF = mybir.ActivationFunctionType
ALU = mybir.AluOpType
AX = mybir.AxisListType
U16 = mybir.dt.uint16

T_FULL, D, NH, DH, NKV = 8192, 2048, 16, 128, 4
GROUP = NH // NKV
DKV = NKV * DH
F_INNER = 8192
NFC = F_INNER // 128
NDC = D // 128
EPS = 1e-5
NCORES = 8
TOK = T_FULL // NCORES
HALO = 1024
TT = TOK + HALO
NT128 = TT // 128
NOWN = TOK // 128
WINDOWS = [384, 512, 768, 1024]
NKT = TT // 128
NEG = -1.0e9
ISQD = 1.0 / math.sqrt(DH)
FSL = 512
NSL = F_INNER // FSL


def _alibi_slopes(n):
    start = 2 ** (-(2 ** (-(math.log2(n) - 3))))
    return [start * start ** i for i in range(n)]


SLOPE_G = _alibi_slopes(NH)[:GROUP]


def _split_bf16(a, n):
    a = np.asarray(a, dtype=np.float64)
    out, rem = [], a.copy()
    for _ in range(n):
        q = rem.astype(ml_dtypes.bfloat16)
        out.append(q)
        rem = rem - q.astype(np.float64)
    return out


class Sem:
    def __init__(self, h):
        self.h = h
        self.n = 0

    def inc(self, k=1):
        self.n += k
        return self.n


class MemMap:
    BASE = 17408
    LIMIT = 229344

    def __init__(self, nc):
        self.nc = nc
        self.regions = []

    def alloc(self, name, shape, dtype_, off, phases):
        off = off + self.BASE
        size = shape[1] * mybir.dt.size(dtype_)
        end = off + size
        assert end <= self.LIMIT, f"{name} exceeds SBUF: {end}"
        for (n2, o2, e2, ph2) in self.regions:
            if off < e2 and o2 < end and (phases & ph2):
                raise AssertionError(f"overlap {name}/{n2} in phases {phases & ph2}")
        self.regions.append((name, off, end, phases))
        return self.nc.alloc_sbuf_tensor_at(name, shape, dtype_, offset=off).ap()


class P:
    pass


def _ser(v, ve, inst):
    inst.then_inc(ve.h, 1)
    ve.inc(1)
    v.wait_ge(ve.h, ve.n)


PH1, PH2, PH3, PH4A, PH4B, PH4C, PH4D = 1, 2, 4, 8, 16, 32, 64
ALL = 127


def build_program(debug=0):
    nc = bass.Bass("TRN2", target_bir_lowering=False,
                   detect_race_conditions=False)
    p = P()
    p.nc = nc
    p.debug = debug
    dt = nc.dram_tensor

    p.xin = dt("xin", [TT, D], F32, kind="ExternalInput").ap()
    p.Wq = dt("Wq", [D, D], F32, kind="ExternalInput").ap()
    p.Wk = dt("Wk", [D, DKV], F32, kind="ExternalInput").ap()
    p.Wv = dt("Wv", [D, DKV], F32, kind="ExternalInput").ap()
    p.Wo = dt("Wo", [D, D], F32, kind="ExternalInput").ap()
    p.w1 = dt("w1", [F_INNER, D], F32, kind="ExternalInput").ap()
    p.b1 = dt("b1", [F_INNER], F32, kind="ExternalInput").ap()
    p.lng = dt("lng", [F_INNER], F32, kind="ExternalInput").ap()
    p.lnb = dt("lnb", [F_INNER], F32, kind="ExternalInput").ap()
    p.w2 = dt("w2", [D, F_INNER], F32, kind="ExternalInput").ap()
    p.b2 = dt("b2", [D], F32, kind="ExternalInput").ap()
    p.identf_d = dt("identf", [128, 128], F32, kind="ExternalInput").ap()
    p.identb_d = dt("identb", [128, 128], U16, kind="ExternalInput").ap().bitcast(BF16)
    p.attl_d = dt("attl", [8, GROUP * NKT * 128], U16, kind="ExternalInput").ap().bitcast(BF16)
    p.attr_d = dt("attr", [8, GROUP * 2 * 512], U16, kind="ExternalInput").ap().bitcast(BF16)
    p.maskb_d = dt("maskb", [128, 896], U16, kind="ExternalInput").ap().bitcast(BF16)
    p.onesc_d = dt("onesc", [128, 2], F32, kind="ExternalInput").ap()
    p.onesr_d = dt("onesr", [1, 128], F32, kind="ExternalInput").ap()
    p.out = dt("out", [TOK, D], F32, kind="ExternalOutput").ap()
    p.xqsp = dt("xqspill", [NOWN, 128, NDC * 128], BF16).ap()
    p.h2sp = dt("h2spill", [NFC, 128, TOK], F32).ap()
    p.sTsp = dt("sTspill", [NDC, 128, TOK], BF16).ap()

    p.dbg = {}
    if debug:
        for name, shape, dtp in [
            ("d_kh", [128, NKV * TT], U16), ("d_kl", [128, NKV * TT], U16),
            ("d_vtm", [128, NKV * NKT * 128], F32),
            ("d_xq", [NOWN, 128, NDC * 128], U16),
            ("d_st", [128, NT128 * 8], F32),
            ("d_sT", [NDC, 128, TOK], U16),
            ("d_den", [1, 2 * NH * 512], F32),
            ("d_h2", [NFC, 128, TOK], F32),
            ("d_st2", [128, 8 * 8], F32),
            ("d_gT", [128, NDC * TOK], F32),
        ]:
            p.dbg[name] = dt(name, shape, dtp, kind="ExternalOutput").ap()

    es = contextlib.ExitStack()
    mm = MemMap(nc)
    p.mm = mm

    p.kh = mm.alloc("kh", [128, NKV * TT], BF16, 0, PH2 | PH3)
    p.kl = mm.alloc("kl", [128, NKV * TT], BF16, 16384, PH2 | PH3)
    p.vtm = mm.alloc("vtm", [128, NKV * NKT * 128], F32, 32768, PH2 | PH3)
    p.identf = mm.alloc("identf_s", [128, 128], F32, 65536, ALL)
    p.identb = mm.alloc("identb_s", [128, 128], BF16, 66048, ALL)
    p.maskb = mm.alloc("maskb_s", [128, 896], BF16, 66304, ALL)
    p.onesf = mm.alloc("onesf", [128, 2], F32, 68096, ALL)
    p.onesr = mm.alloc("onesr_s", [1, 128], F32, 68128, ALL)
    p.b1s = mm.alloc("b1s", [128, 64], F32, 68640, ALL)
    p.lngs = mm.alloc("lngs", [128, 64], F32, 68896, ALL)
    p.lnbs = mm.alloc("lnbs", [128, 64], F32, 69152, ALL)
    p.b2s = mm.alloc("b2s", [128, 16], F32, 69408, ALL)
    stoff = 69472
    for nm in ["mu", "rstd", "mx", "mn", "sx", "sxx", "c1", "dq1", "t0", "t1"]:
        setattr(p, "st_" + nm, mm.alloc("st_" + nm, [128, NT128], F32, stoff, ALL))
        stoff += 64
    st2off = 70112
    for nm in ["S1", "SG", "SB", "S2", "SG2", "mu2", "rstd2", "r2", "mxh",
               "c2", "dq2", "tmp"]:
        setattr(p, "s2_" + nm, mm.alloc("s2_" + nm, [128, NOWN], F32, st2off, ALL))
        st2off += 32
    p.wstA = mm.alloc("wstA", [128, 192], F32R, 70496, ALL)
    p.wstB = mm.alloc("wstB", [128, 128], F32R, 71264, ALL)
    p.qcols = mm.alloc("qcols", [128, 8], F32, 71776, ALL)
    p.denrow = mm.alloc("denrow", [1, 512], F32, 71808, ALL)
    p.recrow = mm.alloc("recrow", [1, 512], F32, 73856, ALL)
    p.thrc = mm.alloc("thrc", [128, 2], F32, 75904, ALL)
    p.thrn = mm.alloc("thrn", [128, 2], F32, 75936, ALL)
    p.wacc = mm.alloc("wacc", [128, 128], F32, 75968, ALL)
    p.attl = mm.alloc("attl_s", [8, GROUP * NKT * 128], BF16, 76480, PH1 | PH2 | PH3)
    p.attr = mm.alloc("attr_s", [8, GROUP * 2 * 512], BF16, 92864, PH1 | PH2 | PH3)
    PH4ALL = PH4A | PH4B | PH4C | PH4D
    p.rows4 = mm.alloc("rows4", [1, 3968], F32, 76480, PH4ALL)
    p.xnT = mm.alloc("xnT", [128, NDC * 512], F32, 101056, PH2 | PH3)
    p.wbuf = mm.alloc("wbuf", [128, NDC * 512], F32, 133824, PH2)
    p.wq0 = mm.alloc("wq0", [128, D], F32, 133824, PH3)
    p.wq1 = mm.alloc("wq1", [128, D], F32, 142016, PH3)
    p.qhb = mm.alloc("qhb", [128, 512], BF16, 150208, PH3)
    p.qlb = mm.alloc("qlb", [128, 512], BF16, 151232, PH3)
    p.rbc = mm.alloc("rbc", [128, 512], F32, 152256, PH3)
    p.pbuf = mm.alloc("pbuf", [128, 2 * 512], F32, 154304, PH3)
    p.sTst = mm.alloc("sTst", [128, 2 * 512], BF16, 158400, PH3)
    p.xt0 = mm.alloc("xt0", [128, D], F32, 166592, PH1 | PH2)
    p.xnt = mm.alloc("xnt", [128, D], F32, 174784, PH1 | PH2)
    p.scrb = mm.alloc("scrb", [128, D], BF16, 182976, PH1 | PH2)
    p.scrP = mm.alloc("scrP", [128, 1024], F32, 187072, PH1 | PH2)
    p.puP = mm.alloc("puP", [128, 1024], F32, 191168, PH1 | PH2)
    p.pbP = mm.alloc("pbP", [128, 1024], F32, 195264, PH1 | PH2)
    p.xqst = mm.alloc("xqst", [128, D], BF16, 199360, PH1 | PH2)
    p.ysb = mm.alloc("ysb", [128, NH * 512], F32, 166592, PH3)
    p.xqTr = mm.alloc("xqTr", [128, NDC * TOK], BF16, 32768, PH4A)
    p.wtern = mm.alloc("wtern", [128, 4 * D], BF16, 101056, PH4A)
    p.wqT = mm.alloc("wqT", [128, NDC * FSL], BF16, 117440, PH4A)
    p.wfsl = mm.alloc("wfsl", [128, 4 * D], F32, 133824, PH4A)
    p.hbuf = mm.alloc("hbuf", [128, 8 * 512], F32, 166592, PH4A)
    p.tmpb = mm.alloc("tmpb", [128, D], BF16, 182976, PH4A | PH4C)
    p.bc1 = mm.alloc("bc1", [128, 2 * 512], F32, 191168, PH4A | PH4C)
    p.h2st = mm.alloc("h2st", [128, 1024], F32, 187072, PH4B | PH4C)
    p.pe2u = mm.alloc("pe2u", [128, 512], F32, 195264, PH4B | PH4C)
    p.pe2b = mm.alloc("pe2b", [128, 512], F32, 197312, PH4B | PH4C)
    p.pe2c = mm.alloc("pe2c", [128, 512], F32, 199360,
                      PH4A | PH4B | PH4C)
    p.RAb = mm.alloc("RAb", [128, 512], F32, 201408, PH4B | PH4C)
    p.RCb = mm.alloc("RCb", [128, 512], F32, 203456, PH4B | PH4C)
    p.RBb = mm.alloc("RBb", [128, 512], F32, 205504, PH4B | PH4C)
    p.statu = mm.alloc("statu", [128, 512], F32R, 207552, PH4B)
    p.statq = mm.alloc("statq", [128, 512], F32R, 209600, PH4B)
    p.w2sl = mm.alloc("w2sl", [128, F_INNER], F32, 0, PH4C)
    p.w2tn = mm.alloc("w2tn", [128, F_INNER], BF16, 32768, PH4C)
    p.w2tT = mm.alloc("w2tT", [128, F_INNER], BF16, 49152, PH4C)
    p.xq2T = mm.alloc("xq2T", [128, NFC * 512], BF16, 101056, PH4C)
    p.gkeep = mm.alloc("gkeep", [128, NDC * 512], BF16, 166592, PH4C)
    p.xrel = mm.alloc("xrel", [128, D], F32, 0, PH4D)
    p.ostg = mm.alloc("ostg", [128, D], F32, 8192, PH4D)
    p.sTr = mm.alloc("sTr", [128, 512], BF16, 16384, PH4D)
    p.dq1row = p.rows4[:, 0:1024]
    p.dq2row = p.rows4[:, 1024:2048]
    p.rowa = p.rows4[:, 2048:2560]
    p.rowc = p.rows4[:, 2560:3072]
    p.rowd = p.rows4[:, 3072:3584]
    p.thr1row = p.rows4[:, 3584:3712]
    p.thr2row = p.rows4[:, 3712:3840]

    p.pk = [es.enter_context(nc.psum_tensor(f"pk{i}", [128, 512], F32))
            for i in range(8)]

    p.sem = {}
    for name in ["dx", "dc", "dw", "dq", "dsp", "dst", "te", "ve", "se", "ge",
                 "dh2", "dh3", "dw2"]:
        p.sem[name] = Sem(es.enter_context(nc.semaphore(name)))

    p.block = es.enter_context(nc.Block())
    p.T = p.block.tensor
    p.V = p.block.vector
    p.S = p.block.scalar
    p.G = p.block.gpsimd
    p.SY = p.block.sync
    p.wqslot_snap = [None, None]
    p.sspill_snap = {}

    _phase0(p)
    _phase1(p)
    if debug == 4:
        _debug_dump(p)
        _finish(p)
        es.close()
        nc._p = p
        return nc
    _phase2(p, groups=(0, 1, 2), peel=True)
    if debug == 5:
        _phase3(p, qt=0)
    elif debug == 6:
        _phase3(p, qt=0)
        _phase2(p, groups=(3,), peel=True)
    elif debug == 7:
        _phase3(p, qt=0)
        _phase2(p, groups=(3,), peel=True)
        _phase3(p, qt=1)
    elif debug != 2:
        _phase3(p, qt=0)
        _phase2(p, groups=(3,), peel=True)
        _phase3(p, qt=1)
        _phase4(p)
    else:
        _phase2(p, groups=(3,), peel=True)
    _debug_dump(p)
    _finish(p)
    es.close()
    nc._p = p
    return nc


def _phase0(p):
    dc = p.sem["dc"]

    def loads(sy):
        for dst_, src in [(p.identf, p.identf_d), (p.identb, p.identb_d),
                          (p.attl, p.attl_d), (p.attr, p.attr_d),
                          (p.maskb, p.maskb_d), (p.onesf, p.onesc_d),
                          (p.onesr, p.onesr_d)]:
            sy.dma_start(out=dst_, in_=src).then_inc(dc.h, 16)
            dc.inc(16)
        for dst_, src in [(p.b1s, p.b1), (p.lngs, p.lng), (p.lnbs, p.lnb)]:
            sy.dma_start(out=dst_, in_=src.rearrange("(c p) -> p c", p=128),
                         allow_slow_non_contiguous=True).then_inc(dc.h, 16)
            dc.inc(16)
        sy.dma_start(out=p.b2s, in_=p.b2.rearrange("(c p) -> p c", p=128),
                     allow_slow_non_contiguous=True).then_inc(dc.h, 16)
        dc.inc(16)
    p.SY(loads)


def _phase1(p):
    dx, ve, se = p.sem["dx"], p.sem["ve"], p.sem["se"]
    for i in range(NT128):
        def load(sy, i=i):
            if i > 0:
                sy.wait_ge(se.h, se.n)
                sy.wait_ge(ve.h, ve.n)
            sy.dma_start(out=p.xt0, in_=p.xin[i * 128:(i + 1) * 128, :]
                         ).then_inc(dx.h, 16)
            dx.inc(16)
        p.SY(load)

        def scal(s, i=i):
            s.wait_ge(dx.h, dx.n)
            s.activation(p.scrb, p.xt0, AF.Square, accum_out=p.st_sxx[:, i:i + 1])
            s.activation(p.xqst, p.xt0, AF.Copy,
                         accum_out=p.st_sx[:, i:i + 1]).then_inc(se.h, 1)
            se.inc(1)
        p.S(scal)

        def vec(v, i=i):
            v.wait_ge(dx.h, dx.n)
            v.reduce_max(p.st_mx[:, i:i + 1], p.xt0, axis=AX.X)
            v.tensor_reduce(p.st_mn[:, i:i + 1], p.xt0, axis=AX.X,
                            op=ALU.min).then_inc(ve.h, 1)
            ve.inc(1)
        p.V(vec)

    def vst(v):
        v.wait_ge(se.h, se.n)
        inv_d = 1.0 / D
        _ser(v, ve, v.tensor_scalar(p.st_mu, p.st_sx, inv_d, None, ALU.mult))
        _ser(v, ve, v.tensor_scalar(p.st_t0, p.st_sxx, inv_d, None, ALU.mult))
        _ser(v, ve, v.tensor_tensor(p.st_t1, p.st_mu, p.st_mu, ALU.mult))
        _ser(v, ve, v.tensor_tensor(p.st_t0, p.st_t0, p.st_t1, ALU.subtract))
        v.tensor_scalar(p.st_t1, p.st_t0, EPS, None, ALU.add).then_inc(ve.h, 1)
        ve.inc(1)
    p.V(vst)

    def sst(s):
        s.wait_ge(ve.h, ve.n)
        s.activation(p.st_rstd, p.st_t1, AF.Sqrt).then_inc(se.h, 1)
        se.inc(1)
    p.S(sst)

    def vst2(v):
        v.wait_ge(se.h, se.n)
        _ser(v, ve, v.reciprocal(p.st_rstd, p.st_rstd))
        _ser(v, ve, v.tensor_tensor(p.st_t1, p.st_rstd, p.st_rstd, ALU.mult))
        _ser(v, ve, v.tensor_tensor(p.st_t1, p.st_t1, p.st_t0, ALU.mult))
        v.tensor_scalar(p.st_t1, p.st_t1, EPS, None, ALU.add).then_inc(ve.h, 1)
        ve.inc(1)
    p.V(vst2)

    def sst2(s):
        s.wait_ge(ve.h, ve.n)
        s.activation(p.st_t1, p.st_t1, AF.Sqrt).then_inc(se.h, 1)
        se.inc(1)
    p.S(sst2)

    def vst3(v):
        v.wait_ge(se.h, se.n)
        _ser(v, ve, v.reciprocal(p.st_t1, p.st_t1))
        _ser(v, ve, v.tensor_tensor(p.st_mx, p.st_mx, p.st_mu, ALU.subtract))
        _ser(v, ve, v.tensor_tensor(p.st_mn, p.st_mu, p.st_mn, ALU.subtract))
        _ser(v, ve, v.tensor_tensor(p.st_mx, p.st_mx, p.st_mn, ALU.max))
        _ser(v, ve, v.tensor_tensor(p.st_mx, p.st_mx, p.st_rstd, ALU.mult))
        _ser(v, ve, v.reciprocal(p.st_c1, p.st_mx))
        _ser(v, ve, v.tensor_scalar(p.st_c1, p.st_c1, 127.0, None, ALU.mult))
        _ser(v, ve, v.tensor_tensor(p.st_dq1, p.st_t1, p.st_mx, ALU.mult))
        v.tensor_scalar(p.st_dq1, p.st_dq1, 1.0 / 127.0, None,
                        ALU.mult).then_inc(ve.h, 1)
        ve.inc(1)
    p.V(vst3)


def _phase2(p, groups, peel):
    dx, dw, dsp = p.sem["dx"], p.sem["dw"], p.sem["dsp"]
    te, ve, se, dc = p.sem["te"], p.sem["ve"], p.sem["se"], p.sem["dc"]

    def wc(t):
        t.wait_ge(dc.h, dc.n)
    p.T(wc)

    for g4 in groups:
        own = g4 >= 2
        for j in range(4):
            i = g4 * 4 + j

            def load(sy, i=i):
                sy.wait_ge(ve.h, ve.n)
                sy.dma_start(out=p.xt0, in_=p.xin[i * 128:(i + 1) * 128, :]
                             ).then_inc(dx.h, 16)
                dx.inc(16)
            p.SY(load)

            def vxn(v, i=i):
                v.wait_ge(dx.h, dx.n)
                v.tensor_scalar(p.xnt, p.xt0, p.st_mu[:, i:i + 1],
                                p.st_rstd[:, i:i + 1], ALU.subtract,
                                ALU.mult).then_inc(ve.h, 1)
                ve.inc(1)
            p.V(vxn)

            for cq in range(4):
                def ttr(t, cq=cq):
                    t.wait_ge(ve.h, ve.n)
                    for cc in range(4):
                        c = cq * 4 + cc
                        t.transpose(p.pk[cq % 2][:, cc * 128:(cc + 1) * 128],
                                    p.xnt[:, c * 128:(c + 1) * 128],
                                    p.identf).then_inc(te.h, 1)
                        te.inc(1)
                p.T(ttr)

                def vev(v, cq=cq, j=j):
                    v.wait_ge(te.h, te.n)
                    for cc in range(4):
                        c = cq * 4 + cc
                        v.tensor_copy(
                            p.xnT[:, c * 512 + j * 128: c * 512 + (j + 1) * 128],
                            p.pk[cq % 2][:, cc * 128:(cc + 1) * 128])
                    v.memset(p.st_t0[:, 15:16], 0.0).then_inc(ve.h, 1)
                    ve.inc(1)
                p.V(vev)

            if own and peel:
                _peel1(p, i)

        def ldwk(sy, g4=g4):
            sy.wait_ge(te.h, te.n)
            sy.wait_ge(p.sem["dh3"].h, p.sem["dh3"].n)
            sy.dma_start(out=p.wbuf.rearrange("p (c n) -> p c n", c=NDC),
                         in_=p.Wk.rearrange("(c p) n -> p c n", p=128)
                         ).then_inc(dw.h, 16)
            dw.inc(16)
        p.SY(ldwk)

        for kv in range(NKV):
            def tk(t, kv=kv):
                t.wait_ge(dw.h, dw.n)
                t.wait_ge(ve.h, ve.n)
                for c in range(NDC):
                    mmi = t.matmul(p.pk[2][:, :],
                                   p.wbuf[:, c * 512 + kv * 128:
                                          c * 512 + (kv + 1) * 128],
                                   p.xnT[:, c * 512:(c + 1) * 512],
                                   start=(c == 0), stop=(c == NDC - 1))
                    if c == NDC - 1:
                        mmi.then_inc(te.h, 1)
                te.inc(1)
            p.T(tk)

            def vk(v, kv=kv, g4=g4):
                v.wait_ge(te.h, te.n)
                dsth = p.kh[:, kv * TT + g4 * 512: kv * TT + (g4 + 1) * 512]
                dstl = p.kl[:, kv * TT + g4 * 512: kv * TT + (g4 + 1) * 512]
                _ser(v, ve, v.tensor_copy(dsth, p.pk[2][:, :]))
                v.tensor_tensor(dstl, p.pk[2][:, :], dsth,
                                ALU.subtract).then_inc(ve.h, 1)
                ve.inc(1)
            p.V(vk)

        def ldwv(sy):
            sy.wait_ge(ve.h, ve.n)
            sy.dma_start(out=p.wbuf.rearrange("p (c n) -> p c n", c=NDC),
                         in_=p.Wv.rearrange("(c p) n -> p c n", p=128)
                         ).then_inc(dw.h, 16)
            dw.inc(16)
        p.SY(ldwv)

        for j in range(4):
            kt = g4 * 4 + j

            def tv(t, j=j):
                t.wait_ge(dw.h, dw.n)
                t.wait_ge(ve.h, ve.n)
                for c in range(NDC):
                    mmi = t.matmul(p.pk[3][:, :],
                                   p.xnT[:, c * 512 + j * 128:
                                         c * 512 + (j + 1) * 128],
                                   p.wbuf[:, c * 512:(c + 1) * 512],
                                   start=(c == 0), stop=(c == NDC - 1))
                    if c == NDC - 1:
                        mmi.then_inc(te.h, 1)
                te.inc(1)
            p.T(tv)

            def vv(v, kt=kt):
                v.wait_ge(te.h, te.n)
                for kvh in range(NKV):
                    v.tensor_copy(
                        p.vtm[:, (kt * NKV + kvh) * 128:(kt * NKV + kvh + 1) * 128],
                        p.pk[3][:, kvh * 128:(kvh + 1) * 128])
                v.memset(p.st_t0[:, 15:16], 0.0).then_inc(ve.h, 1)
                ve.inc(1)
            p.V(vv)
    p.wqslot_snap = [None, None]


def _peel1(p, i):
    ve, te, dsp = p.sem["ve"], p.sem["te"], p.sem["dsp"]
    ot = i - 8

    def vpeel(v, i=i):
        for hh in range(2):
            sl = slice(hh * 1024, (hh + 1) * 1024)
            v.tensor_scalar(p.puP, p.xnt[:, sl], p.st_c1[:, i:i + 1], 128.5,
                            ALU.mult, ALU.add)
            v.tensor_copy(p.scrP, p.puP)
            for c in [128.0, 64.0, 32.0, 16.0, 8.0, 4.0, 2.0, 1.0]:
                v.tensor_scalar(p.pbP, p.puP, c, None, ALU.is_ge)
                v.scalar_tensor_tensor(p.puP, p.pbP, -c, p.puP, ALU.mult, ALU.add)
            v.scalar_tensor_tensor(p.scrb[:, sl], p.scrP, 128.0, p.puP,
                                   ALU.subtract, ALU.subtract)
        v.memset(p.st_t0[:, 15:16], 0.0).then_inc(ve.h, 1)
        ve.inc(1)
    p.V(vpeel)

    pb16 = p.pk[5][:].bitcast(BF16)
    for cq in range(4):
        def ttr(t, cq=cq):
            t.wait_ge(ve.h, ve.n)
            for cc in range(4):
                c = cq * 4 + cc
                t.transpose(pb16[:, cc * 128:(cc + 1) * 128],
                            p.scrb[:, c * 128:(c + 1) * 128],
                            p.identb).then_inc(te.h, 1)
                te.inc(1)
        p.T(ttr)

        def vev(v, cq=cq):
            v.wait_ge(te.h, te.n)
            if cq == 0:
                v.wait_ge(p.sem["dh3"].h, p.sem["dh3"].n)
            for cc in range(4):
                c = cq * 4 + cc
                v.tensor_copy(p.xqst[:, c * 128:(c + 1) * 128],
                              pb16[:, cc * 128:(cc + 1) * 128])
            v.memset(p.st_t0[:, 15:16], 0.0).then_inc(ve.h, 1)
            ve.inc(1)
        p.V(vev)

    def spill(s, ot=ot):
        s.wait_ge(ve.h, ve.n)
        s.dma_start(out=p.xqsp[ot, :, :], in_=p.xqst
                    ).then_inc(p.sem["dh3"].h, 16)
        p.sem["dh3"].inc(16)
    p.S(spill)


def _phase3(p, qt):
    dw, te, ve, se, dsp = (p.sem["dw"], p.sem["te"], p.sem["ve"], p.sem["se"],
                           p.sem["dsp"])
    dh3 = p.sem["dh3"]
    p.sspill_snap = {}
    Qloc = HALO + qt * 512
    qtile = Qloc // 128

    for h in range(NH):
        g = h // GROUP
        kv = h // GROUP
        W = WINDOWS[g]
        kt0 = (Qloc - W) // 128
        kt1 = Qloc // 128 + 4
        kts = [kt for kt in range(kt0, kt1) if kt >= 0]

        wslot = p.wq0 if h % 2 == 0 else p.wq1

        def ldwq(sy, h=h, wslot=wslot, snap=p.wqslot_snap[h % 2]):
            if snap is None:
                sy.wait_ge(te.h, te.n)
            else:
                sy.wait_ge(te.h, snap)
            sy.dma_start(out=wslot.rearrange("p (c n) -> p c n", c=NDC),
                         in_=p.Wq[:, h * 128:(h + 1) * 128].rearrange(
                             "(c p) n -> p c n", p=128)).then_inc(dw.h, 16)
            dw.inc(16)
        p.SY(ldwq)

        def tq(t, wslot=wslot):
            t.wait_ge(dw.h, dw.n)
            t.wait_ge(ve.h, ve.n)
            for c in range(NDC):
                mmi = t.matmul(p.pk[4][:, :],
                               wslot[:, c * 128:(c + 1) * 128],
                               p.xnT[:, c * 512:(c + 1) * 512],
                               start=(c == 0), stop=(c == NDC - 1))
                if c == NDC - 1:
                    mmi.then_inc(te.h, 1)
            te.inc(1)
        p.T(tq)
        p.wqslot_snap[h % 2] = te.n

        def vq(v):
            v.wait_ge(te.h, te.n)
            _ser(v, ve, v.tensor_scalar(p.pbuf[:, 0:512], p.pk[4][:, :], ISQD,
                                        None, ALU.mult))
            _ser(v, ve, v.tensor_copy(p.qhb, p.pbuf[:, 0:512]))
            v.tensor_tensor(p.qlb, p.pbuf[:, 0:512], p.qhb,
                            ALU.subtract).then_inc(ve.h, 1)
            ve.inc(1)
        p.V(vq)

        nv = len(kts)
        for vi, kt in enumerate(kts):
            sb_ = p.pk[vi % 2]
            diag = kt * 128 >= Qloc

            def tsc(t, kt=kt, vi=vi, sb_=sb_, diag=diag, g=g, kv=kv, h=h,
                    qt=qt, nv=nv):
                if vi == 0:
                    t.wait_ge(ve.h, ve.n)
                if vi >= 2:
                    t.wait_ge(se.h, se.n - 1)
                ksl = slice(kv * TT + kt * 128, kv * TT + (kt + 1) * 128)
                t.matmul(sb_[:, :], p.kh[:, ksl], p.qhb, start=True, stop=False)
                t.matmul(sb_[:, :], p.kh[:, ksl], p.qlb, start=False, stop=False)
                t.matmul(sb_[:, :], p.kl[:, ksl], p.qhb, start=False, stop=False)
                la = p.attl[:, (g * NKT + kt) * 128:(g * NKT + kt + 1) * 128]
                ra = p.attr[:, (g * 2 + qt) * 512:(g * 2 + qt + 1) * 512]
                if diag:
                    t.matmul(sb_[:, :], la, ra, start=False, stop=False)
                    joff = (kt * 128 - Qloc) // 128
                    msl = p.maskb[:, 384 - 128 * joff: 384 - 128 * joff + 512]
                    t.matmul(sb_[:, :], p.identb, msl, start=False,
                             stop=True).then_inc(te.h, 1)
                else:
                    t.matmul(sb_[:, :], la, ra, start=False,
                             stop=True).then_inc(te.h, 1)
                te.inc(1)
            p.T(tsc)

            def sexp(s, vi=vi, sb_=sb_):
                s.wait_ge(te.h, te.n)
                s.activation(p.pbuf[:, (vi % 2) * 512:(vi % 2 + 1) * 512],
                             sb_[:, :], AF.Exp).then_inc(se.h, 1)
                se.inc(1)
            p.S(sexp)

            def tav(t, vi=vi, kt=kt, kv=kv, nv=nv):
                t.wait_ge(se.h, se.n)
                pb = p.pbuf[:, (vi % 2) * 512:(vi % 2 + 1) * 512]
                vsl = p.vtm[:, (kt * NKV + kv) * 128:(kt * NKV + kv + 1) * 128]
                t.matmul(p.pk[2][:, :], vsl, pb,
                         start=(vi == 0), stop=(vi == nv - 1))
                mmi = t.matmul(p.pk[3][0:1, :], p.onesf[:, 0:1], pb,
                               start=(vi == 0), stop=(vi == nv - 1))
                mmi.then_inc(te.h, 1)
                te.inc(1)
            p.T(tav)

        def vden(v):
            v.wait_ge(te.h, te.n)
            _ser(v, ve, v.tensor_copy(p.denrow, p.pk[3][0:1, :]))
            v.reciprocal(p.recrow, p.denrow).then_inc(ve.h, 1)
            ve.inc(1)
        p.V(vden)

        def trb(t):
            t.wait_ge(ve.h, ve.n)
            t.matmul(p.pk[5][:, :], p.onesr, p.recrow, start=True,
                     stop=True).then_inc(te.h, 1)
            te.inc(1)
        p.T(trb)

        def vyev(v, h=h):
            v.wait_ge(te.h, te.n)
            v.tensor_copy(p.rbc, p.pk[5][:, :])
            v.scalar_tensor_tensor(p.ysb[:, h * 512:(h + 1) * 512],
                                   p.pk[2][:, :], 0.0, p.rbc,
                                   ALU.add, ALU.mult).then_inc(ve.h, 1)
            ve.inc(1)
        p.V(vyev)

        if p.debug and qt == 0:
            def dden(g_, h=h):
                g_.wait_ge(ve.h, ve.n)
                g_.dma_start(out=p.dbg["d_den"][:, h * 512:(h + 1) * 512],
                             in_=p.denrow).then_inc(p.sem["ge"].h, 16)
                p.sem["ge"].inc(16)
            p.G(dden)

    for dc_ in range(NDC):
        def ldwo(sy, dc_=dc_, snap=p.wqslot_snap[dc_ % 2]):
            if snap is None:
                sy.wait_ge(te.h, te.n)
            else:
                sy.wait_ge(te.h, snap)
            slot = p.wq0 if dc_ % 2 == 0 else p.wq1
            sy.dma_start(out=slot.rearrange("p (c n) -> p c n", c=NDC),
                         in_=p.Wo[:, dc_ * 128:(dc_ + 1) * 128].rearrange(
                             "(c p) n -> p c n", p=128)).then_inc(dw.h, 16)
            dw.inc(16)
        p.SY(ldwo)

        def two(t, dc_=dc_):
            t.wait_ge(dw.h, dw.n)
            if dc_ == 0:
                t.wait_ge(ve.h, ve.n)
            elif dc_ >= 2:
                t.wait_ge(ve.h, ve.n - 1)
            slot = p.wq0 if dc_ % 2 == 0 else p.wq1
            bank = p.pk[6] if dc_ % 2 == 0 else p.pk[7]
            for c in range(NDC):
                mmi = t.matmul(bank[:, :], slot[:, c * 128:(c + 1) * 128],
                               p.ysb[:, c * 512:(c + 1) * 512],
                               start=(c == 0), stop=(c == NDC - 1))
                if c == NDC - 1:
                    mmi.then_inc(te.h, 1)
            te.inc(1)
        p.T(two)
        p.wqslot_snap[dc_ % 2] = te.n

        def vso(v, dc_=dc_, sv=p.sspill_snap.get(dc_ - 2)):
            v.wait_ge(te.h, te.n)
            if sv is not None:
                v.wait_ge(dh3.h, sv)
            bank = p.pk[6] if dc_ % 2 == 0 else p.pk[7]
            v.tensor_copy(p.sTst[:, (dc_ % 2) * 512:(dc_ % 2 + 1) * 512],
                          bank[:, :]).then_inc(ve.h, 1)
            ve.inc(1)
        p.V(vso)

        def sspill(s_, dc_=dc_, qt=qt):
            s_.wait_ge(ve.h, ve.n)
            s_.dma_start(out=p.sTsp[dc_, :, qt * 512:(qt + 1) * 512],
                         in_=p.sTst[:, (dc_ % 2) * 512:(dc_ % 2 + 1) * 512]
                         ).then_inc(dh3.h, 16)
            dh3.inc(16)
        p.S(sspill)
        p.sspill_snap[dc_] = dh3.n


def _col2row(p, src_ap, ncols, psbank, dst_row):
    te, ve, dsp = p.sem["te"], p.sem["ve"], p.sem["dsp"]

    def tt_(t):
        t.wait_ge(ve.h, ve.n)
        t.transpose(psbank[0:ncols, 0:128], src_ap, p.identf).then_inc(te.h, 1)
        te.inc(1)
    p.T(tt_)

    def vv_(v):
        v.wait_ge(te.h, te.n)
        v.wait_ge(dsp.h, dsp.n)
        v.tensor_copy(p.pe2c[0:ncols, 0:128], psbank[0:ncols, 0:128]
                      ).then_inc(ve.h, 1)
        ve.inc(1)
    p.V(vv_)

    def gg_(g):
        g.wait_ge(ve.h, ve.n)
        g.dma_start(out=dst_row.rearrange("a (c b) -> a c b", b=128),
                    in_=p.pe2c[0:ncols, 0:128],
                    allow_slow_non_contiguous=True).then_inc(dsp.h, 16)
        dsp.inc(16)
    p.G(gg_)


def _ternarize(p, src_f32, dst_bf16, thr_col, width):
    se, ve = p.sem["se"], p.sem["ve"]

    def s_(s):
        s.wait_ge(ve.h, ve.n)
        s.activation(dst_bf16, src_f32, AF.Sign,
                     bias=p.thrn[:, thr_col:thr_col + 1])
        s.activation(p.tmpb[:, 0:width], src_f32, AF.Sign,
                     bias=p.thrc[:, thr_col:thr_col + 1]).then_inc(se.h, 1)
        se.inc(1)
    p.S(s_)

    def v_(v):
        v.wait_ge(se.h, se.n)
        v.tensor_tensor(dst_bf16, dst_bf16, p.tmpb[:, 0:width],
                        ALU.add).then_inc(ve.h, 1)
        ve.inc(1)
    p.V(v_)


def _phase4(p):
    dw, dh2, te, ve, se, dsp = (p.sem["dw"], p.sem["dh2"], p.sem["te"],
                                p.sem["ve"], p.sem["se"], p.sem["dsp"])
    dq, dx, dst = p.sem["dq"], p.sem["dx"], p.sem["dst"]

    for t4 in range(16):
        def ldw(sy, t4=t4):
            sy.wait_ge(se.h, se.n)
            sy.wait_ge(te.h, te.n)
            if t4 == 0:
                sy.wait_ge(p.sem["dh3"].h, p.sem["dh3"].n)
            sy.dma_start(out=p.wfsl.rearrange("p (s d) -> p s d", s=4),
                         in_=p.w1[t4 * 512:(t4 + 1) * 512, :].rearrange(
                             "(s p) d -> p s d", p=128)).then_inc(dw.h, 16)
            dw.inc(16)
        p.SY(ldw)

        def sab(s, t4=t4):
            s.wait_ge(dw.h, dw.n)
            for sub in range(4):
                s.activation(p.tmpb, p.wfsl[:, sub * D:(sub + 1) * D], AF.Abs,
                             accum_out=p.wacc[:, t4 * 4 + sub:t4 * 4 + sub + 1])
            s.memzero(p.st_t1[:, 15:16]).then_inc(se.h, 1)
            se.inc(1)
        p.S(sab)
    for t4 in range(16):
        def ldw2(sy, t4=t4):
            sy.wait_ge(se.h, se.n)
            sy.wait_ge(te.h, te.n)
            sy.dma_start(out=p.w2sl, in_=p.w2[t4 * 128:(t4 + 1) * 128, :]
                         ).then_inc(dw.h, 16)
            dw.inc(16)
        p.SY(ldw2)

        def sab2(s, t4=t4):
            s.wait_ge(dw.h, dw.n)
            for sub in range(4):
                s.activation(p.tmpb, p.w2sl[:, sub * D:(sub + 1) * D], AF.Abs,
                             accum_out=p.wacc[:, 64 + t4 * 4 + sub:
                                              64 + t4 * 4 + sub + 1])
            s.memzero(p.st_t1[:, 15:16]).then_inc(se.h, 1)
            se.inc(1)
        p.S(sab2)

    if p.debug == 11:
        return

    def vtot(v):
        v.wait_ge(se.h, se.n)
        v.reduce_sum(p.st_t0[:, 0:1], p.wacc[:, 0:64], axis=AX.X)
        v.reduce_sum(p.st_t0[:, 1:2], p.wacc[:, 64:128], axis=AX.X
                     ).then_inc(ve.h, 1)
        ve.inc(1)
    p.V(vtot)

    def ttot(t):
        t.wait_ge(ve.h, ve.n)
        t.matmul(p.pk[0][0:1, 0:2], p.onesf[:, 0:1], p.st_t0[:, 0:2],
                 start=True, stop=True).then_inc(te.h, 1)
        te.inc(1)
    p.T(ttot)

    def vthr(v):
        v.wait_ge(te.h, te.n)
        v.tensor_scalar(p.rowd[:, 0:2], p.pk[0][0:1, 0:2],
                        0.5 / (F_INNER * D), None, ALU.mult).then_inc(ve.h, 1)
        ve.inc(1)
    p.V(vthr)

    def tthr(t):
        t.wait_ge(ve.h, ve.n)
        t.matmul(p.pk[1][:, 0:2], p.onesr, p.rowd[:, 0:2],
                 start=True, stop=True)
        t.matmul(p.pk[2][0:1, 0:128], p.rowd[:, 0:1], p.onesr,
                 start=True, stop=True)
        t.matmul(p.pk[3][0:1, 0:128], p.rowd[:, 1:2], p.onesr,
                 start=True, stop=True).then_inc(te.h, 1)
        te.inc(1)
    p.T(tthr)

    def vthr2(v):
        v.wait_ge(te.h, te.n)
        _ser(v, ve, v.tensor_copy(p.thrc[:, 0:2], p.pk[1][:, 0:2]))
        _ser(v, ve, v.tensor_scalar(p.thrn[:, 0:2], p.pk[1][:, 0:2], -1.0,
                                    None, ALU.mult))
        v.tensor_copy(p.thr1row, p.pk[2][0:1, 0:128])
        v.tensor_copy(p.thr2row, p.pk[3][0:1, 0:128]).then_inc(ve.h, 1)
        ve.inc(1)
    p.V(vthr2)

    if p.debug == 12:
        return
    _col2row(p, p.st_dq1[:, 8:16], 8, p.pk[4], p.dq1row)
    if p.debug == 15:
        return
    for tt in range(2):
        def tbc(t, tt=tt):
            t.wait_ge(dsp.h, dsp.n)
            t.wait_ge(ve.h, ve.n)
            t.matmul(p.pk[5][:, :], p.thr1row,
                     p.dq1row[:, tt * 512:(tt + 1) * 512],
                     start=True, stop=True).then_inc(te.h, 1)
            te.inc(1)
        p.T(tbc)

        def vbc(v, tt=tt):
            v.wait_ge(te.h, te.n)
            v.tensor_copy(p.bc1[:, tt * 512:(tt + 1) * 512],
                          p.pk[5][:, :]).then_inc(ve.h, 1)
            ve.inc(1)
        p.V(vbc)

    if p.debug == 13:
        return

    def swst(s):
        s.wait_ge(ve.h, ve.n)
        s.activation(p.wstA.rearrange("p (c k) -> p c k", k=3)[:, :, 1],
                     p.lngs, AF.Square, accum_out=p.st_t1[:, 0:1])
        s.activation(p.wstB.rearrange("p (c k) -> p c k", k=2)[:, :, 1],
                     p.lngs, AF.Square).then_inc(se.h, 1)
        se.inc(1)
    p.S(swst)

    def vwst(v):
        v.wait_ge(se.h, se.n)
        v.memset(p.pe2c[:, 64:128], 1.0)
        v.tensor_copy(p.wstA.rearrange("p (c k) -> p c k", k=3)[:, :, 0],
                      p.pe2c[:, 64:128])
        v.tensor_copy(p.wstB.rearrange("p (c k) -> p c k", k=2)[:, :, 0],
                      p.pe2c[:, 64:128])
        v.tensor_tensor(p.wstA.rearrange("p (c k) -> p c k", k=3)[:, :, 2],
                        p.lngs, p.lnbs, ALU.mult)
        _ser(v, ve, v.tensor_tensor(p.pe2c[:, 0:64], p.lngs, p.lnbs, ALU.mult))
        _ser(v, ve, v.reduce_sum(p.st_t1[:, 1:2], p.pe2c[:, 0:64], axis=AX.X))
        _ser(v, ve, v.tensor_tensor(p.pe2c[:, 0:64], p.lnbs, p.lnbs, ALU.mult))
        v.reduce_sum(p.st_t1[:, 2:3], p.pe2c[:, 0:64], axis=AX.X
                     ).then_inc(ve.h, 1)
        ve.inc(1)
    p.V(vwst)

    def tq3(t):
        t.wait_ge(ve.h, ve.n)
        t.matmul(p.pk[0][0:1, 0:3], p.onesf[:, 0:1], p.st_t1[:, 0:3],
                 start=True, stop=True).then_inc(te.h, 1)
        te.inc(1)
    p.T(tq3)

    def vq4b(v):
        v.wait_ge(te.h, te.n)
        v.tensor_copy(p.rowd[:, 8:11], p.pk[0][0:1, 0:3]).then_inc(ve.h, 1)
        ve.inc(1)
    p.V(vq4b)

    def tq4(t):
        t.wait_ge(ve.h, ve.n)
        t.matmul(p.pk[1][:, 0:3], p.onesr, p.rowd[:, 8:11],
                 start=True, stop=True).then_inc(te.h, 1)
        te.inc(1)
    p.T(tq4)

    def vq5(v):
        v.wait_ge(te.h, te.n)
        v.tensor_copy(p.qcols[:, 0:3], p.pk[1][:, 0:3]).then_inc(ve.h, 1)
        ve.inc(1)
    p.V(vq5)

    if p.debug == 14:
        return
    for ot in range(NOWN):
        def ldxq(g, ot=ot):
            g.wait_ge(te.h, te.n)
            g.dma_start(
                out=p.xqTr.rearrange("p (c n) -> p c n", c=NDC)[
                    :, :, ot * 128:(ot + 1) * 128],
                in_=p.xqsp[ot].rearrange("p (c n) -> p c n", c=NDC)
            ).then_inc(dq.h, 16)
            dq.inc(16)
        p.G(ldxq)

    if p.debug == 8:
        return
    tern1_snap = None
    gsp_snap = None
    for sl in range(NSL):
        def ldw(sy, sl=sl, snap=tern1_snap):
            if snap is None:
                sy.wait_ge(se.h, se.n)
                sy.wait_ge(te.h, te.n)
            else:
                sy.wait_ge(se.h, snap)
            sy.dma_start(out=p.wfsl.rearrange("p (s d) -> p s d", s=4),
                         in_=p.w1[sl * FSL:(sl + 1) * FSL, :].rearrange(
                             "(s p) d -> p s d", p=128)).then_inc(dw.h, 16)
            dw.inc(16)
        p.SY(ldw)

        def sww(s):
            s.wait_ge(dw.h, dw.n)
        p.S(sww)
        for sub in range(4):
            _ternarize(p, p.wfsl[:, sub * D:(sub + 1) * D],
                       p.wtern[:, sub * D:(sub + 1) * D], 0, D)
        tern1_snap = se.n
        pb16 = p.pk[6][:].bitcast(BF16)
        for c in range(NDC):
            def ttr(t, c=c):
                t.wait_ge(ve.h, ve.n)
                for sub in range(4):
                    t.transpose(pb16[:, sub * 128:(sub + 1) * 128],
                                p.wtern[:, sub * D + c * 128:
                                        sub * D + (c + 1) * 128],
                                p.identb).then_inc(te.h, 1)
                    te.inc(1)
            p.T(ttr)

            def vev(v, c=c):
                v.wait_ge(te.h, te.n)
                v.tensor_copy(p.wqT[:, c * 512:(c + 1) * 512],
                              pb16[:, 0:512]).then_inc(ve.h, 1)
                ve.inc(1)
            p.V(vev)

        if sl == 0:
            def wxq(t):
                t.wait_ge(dq.h, dq.n)
            p.T(wxq)
        for k in range(8):
            tt, sub = k // 4, k % 4

            def tmm(t, k=k, tt=tt, sub=sub, sl=sl):
                t.wait_ge(ve.h, ve.n)
                if sl > 0:
                    t.wait_ge(se.h, se.n)
                for c in range(NDC):
                    mmi = t.matmul(
                        p.pk[k][:, :],
                        p.wqT[:, c * 512 + sub * 128:c * 512 + (sub + 1) * 128],
                        p.xqTr[:, c * TOK + tt * 512:c * TOK + (tt + 1) * 512],
                        start=(c == 0), stop=(c == NDC - 1))
                    if c == NDC - 1:
                        mmi.then_inc(te.h, 1)
                te.inc(1)
            p.T(tmm)

            def vdq(v, k=k, tt=tt, snap=gsp_snap):
                v.wait_ge(te.h, te.n)
                if k == 0 and snap is not None:
                    v.wait_ge(dh2.h, snap)
                v.scalar_tensor_tensor(
                    p.hbuf[:, k * 512:(k + 1) * 512], p.pk[k][:, :], 0.0,
                    p.bc1[:, tt * 512:(tt + 1) * 512], ALU.add,
                    ALU.mult).then_inc(ve.h, 1)
                ve.inc(1)
            p.V(vdq)

        def ssilu(s, sl=sl):
            s.wait_ge(ve.h, ve.n)
            for k in range(8):
                sub = k % 4
                fcol = sl * 4 + sub
                s.activation(p.pk[k][:, :], p.hbuf[:, k * 512:(k + 1) * 512],
                             AF.Silu, bias=p.b1s[:, fcol:fcol + 1])
            s.memzero(p.st_t1[:, 15:16]).then_inc(se.h, 1)
            se.inc(1)
        p.S(ssilu)

        def sgelu(s, sl=sl):
            for k in range(8):
                s.activation(p.hbuf[:, k * 512:(k + 1) * 512], p.pk[k][:, :],
                             AF.Gelu)
            s.memzero(p.st_t1[:, 15:16]).then_inc(se.h, 1)
            se.inc(1)
        p.S(sgelu)

        for k in range(8):
            tt, sub = k // 4, k % 4

            def gsp(s, k=k, tt=tt, sub=sub, sl=sl):
                fc = sl * 4 + sub
                s.dma_start(out=p.h2sp[fc, :, tt * 512:(tt + 1) * 512],
                            in_=p.hbuf[:, k * 512:(k + 1) * 512]
                            ).then_inc(dh2.h, 16)
                p.sem["dh2"].inc(16)
            p.S(gsp)
        gsp_snap = p.sem["dh2"].n

    if p.debug == 9:
        return
    dh = p.sem["dh3"]
    dh2s = p.sem["dh2"]
    pair_sv = {}
    for fc in range(NFC):
        for tt in range(2):
            k = fc * 2 + tt
            b = k % 2
            sv = pair_sv.get(k - 2)

            def ldh(sc, fc=fc, tt=tt, b=b, sv=sv):
                if sv is None:
                    sc.wait_ge(dh2s.h, dh2s.n)
                    sc.wait_ge(te.h, te.n)
                else:
                    sc.wait_ge(ve.h, sv[0])
                    sc.wait_ge(se.h, sv[1])
                sc.dma_start(out=p.h2st[:, b * 512:(b + 1) * 512],
                             in_=p.h2sp[fc, :, tt * 512:(tt + 1) * 512]
                             ).then_inc(dh.h, 16)
                dh.inc(16)
            p.S(ldh)

            def vcp(v, b=b):
                v.wait_ge(dh.h, dh.n)
                v.wait_ge(te.h, te.n)
                v.tensor_copy(p.statu,
                              p.h2st[:, b * 512:(b + 1) * 512]
                              ).then_inc(ve.h, 1)
                ve.inc(1)
            p.V(vcp)
            ve_snap = ve.n

            def ssq(s, b=b):
                s.wait_ge(dh.h, dh.n)
                s.wait_ge(te.h, te.n)
                s.activation(p.statq,
                             p.h2st[:, b * 512:(b + 1) * 512],
                             AF.Square).then_inc(se.h, 1)
                se.inc(1)
            p.S(ssq)
            pair_sv[k] = (ve_snap, se.n)

            def tst(t, fc=fc, tt=tt):
                t.wait_ge(ve.h, ve.n)
                t.wait_ge(se.h, se.n)
                t.matmul(p.pk[0 + tt][0:3, :], p.wstA[:, fc * 3:(fc + 1) * 3],
                         p.statu, start=(fc == 0),
                         stop=(fc == NFC - 1))
                mmi = t.matmul(p.pk[2 + tt][0:2, :],
                               p.wstB[:, fc * 2:(fc + 1) * 2],
                               p.statq, start=(fc == 0),
                               stop=(fc == NFC - 1))
                mmi.then_inc(te.h, 1)
                te.inc(1)
            p.T(tst)

    for tt in range(2):
        def vevs(v, tt=tt):
            v.wait_ge(te.h, te.n)
            v.tensor_copy(p.pe2c[0:3, 0:512], p.pk[0 + tt][0:3, :])
            v.tensor_copy(p.pe2u[0:2, 0:512], p.pk[2 + tt][0:2, :]
                          ).then_inc(ve.h, 1)
            ve.inc(1)
        p.V(vevs)
        for j in range(4):
            def tts(t, tt=tt, j=j):
                t.wait_ge(ve.h, ve.n)
                t.transpose(p.pk[4][:, 0:3], p.pe2c[0:3, j * 128:(j + 1) * 128],
                            p.identf[0:3, 0:3])
                t.transpose(p.pk[5][:, 0:2], p.pe2u[0:2, j * 128:(j + 1) * 128],
                            p.identf[0:2, 0:2]).then_inc(te.h, 1)
                te.inc(1)
            p.T(tts)

            def vcs(v, tt=tt, j=j):
                v.wait_ge(te.h, te.n)
                col = tt * 4 + j
                v.tensor_copy(p.s2_S1[:, col:col + 1], p.pk[4][:, 0:1])
                v.tensor_copy(p.s2_SG[:, col:col + 1], p.pk[4][:, 1:2])
                v.tensor_copy(p.s2_SB[:, col:col + 1], p.pk[4][:, 2:3])
                v.tensor_copy(p.s2_S2[:, col:col + 1], p.pk[5][:, 0:1])
                v.tensor_copy(p.s2_SG2[:, col:col + 1], p.pk[5][:, 1:2]
                              ).then_inc(ve.h, 1)
                ve.inc(1)
            p.V(vcs)

    IFI = 1.0 / F_INNER

    def vf1(v):
        v.wait_ge(ve.h, ve.n)
        _ser(v, ve, v.tensor_scalar(p.s2_mu2, p.s2_S1, IFI, None, ALU.mult))
        _ser(v, ve, v.tensor_scalar(p.s2_tmp, p.s2_S2, IFI, None, ALU.mult))
        _ser(v, ve, v.tensor_tensor(p.s2_rstd2, p.s2_mu2, p.s2_mu2, ALU.mult))
        _ser(v, ve, v.tensor_tensor(p.s2_tmp, p.s2_tmp, p.s2_rstd2, ALU.subtract))
        v.tensor_scalar(p.s2_tmp, p.s2_tmp, EPS, None, ALU.add).then_inc(ve.h, 1)
        ve.inc(1)
    p.V(vf1)

    def sf1(s):
        s.wait_ge(ve.h, ve.n)
        s.activation(p.s2_rstd2, p.s2_tmp, AF.Sqrt).then_inc(se.h, 1)
        se.inc(1)
    p.S(sf1)

    def vf2(v):
        v.wait_ge(se.h, se.n)
        _ser(v, ve, v.reciprocal(p.s2_rstd2, p.s2_rstd2))
        _ser(v, ve, v.scalar_tensor_tensor(p.s2_tmp, p.s2_mu2, -2.0, p.s2_SG,
                                           ALU.mult, ALU.mult))
        _ser(v, ve, v.tensor_tensor(p.s2_tmp, p.s2_tmp, p.s2_SG2, ALU.add))
        _ser(v, ve, v.tensor_tensor(p.s2_r2, p.s2_mu2, p.s2_mu2, ALU.mult))
        _ser(v, ve, v.scalar_tensor_tensor(p.s2_tmp, p.s2_r2, p.qcols[:, 0:1],
                                           p.s2_tmp, ALU.mult, ALU.add))
        _ser(v, ve, v.tensor_tensor(p.s2_r2, p.s2_rstd2, p.s2_rstd2, ALU.mult))
        _ser(v, ve, v.tensor_tensor(p.s2_tmp, p.s2_tmp, p.s2_r2, ALU.mult))
        _ser(v, ve, v.scalar_tensor_tensor(p.s2_r2, p.s2_mu2, p.qcols[:, 1:2],
                                           p.s2_SB, ALU.mult, ALU.subtract))
        _ser(v, ve, v.tensor_tensor(p.s2_r2, p.s2_r2, p.s2_rstd2, ALU.mult))
        _ser(v, ve, v.scalar_tensor_tensor(p.s2_tmp, p.s2_r2, -2.0, p.s2_tmp,
                                           ALU.mult, ALU.add))
        _ser(v, ve, v.tensor_scalar(p.s2_tmp, p.s2_tmp, p.qcols[:, 2:3], None,
                                    ALU.add))
        v.tensor_scalar(p.s2_tmp, p.s2_tmp, 1.0 / F_INNER, EPS,
                        ALU.mult, ALU.add).then_inc(ve.h, 1)
        ve.inc(1)
    p.V(vf2)

    def sf2(s):
        s.wait_ge(ve.h, ve.n)
        s.activation(p.s2_tmp, p.s2_tmp, AF.Sqrt).then_inc(se.h, 1)
        se.inc(1)
    p.S(sf2)

    def vf3(v):
        v.wait_ge(se.h, se.n)
        v.reciprocal(p.s2_r2, p.s2_tmp).then_inc(ve.h, 1)
        ve.inc(1)
    p.V(vf3)

    for tt in range(2):
        _col2row(p, p.s2_rstd2[:, tt * 4:(tt + 1) * 4], 4, p.pk[6],
                 p.rowa)
        _col2row(p, p.s2_mu2[:, tt * 4:(tt + 1) * 4], 4, p.pk[6], p.rowc)

        def trb(t, tt=tt):
            t.wait_ge(dsp.h, dsp.n)
            t.matmul(p.pk[6][:, :], p.onesr, p.rowa, start=True, stop=True
                     ).then_inc(te.h, 1)
            te.inc(1)
        p.T(trb)

        def vrb(v, tt=tt):
            v.wait_ge(te.h, te.n)
            v.tensor_copy(p.RAb, p.pk[6][:, :])
            _ser(v, ve, v.tensor_tensor(p.rowd, p.rowc, p.rowa, ALU.mult))
            v.tensor_scalar(p.rowd, p.rowd, -1.0, None, ALU.mult
                            ).then_inc(ve.h, 1)
            ve.inc(1)
        p.V(vrb)

        def trc(t, tt=tt):
            t.wait_ge(ve.h, ve.n)
            t.matmul(p.pk[6][:, :], p.onesr, p.rowd, start=True, stop=True
                     ).then_inc(te.h, 1)
            te.inc(1)
        p.T(trc)

        def vrc(v, tt=tt):
            v.wait_ge(te.h, te.n)
            v.tensor_copy(p.RCb, p.pk[6][:, :])
            v.memset(p.s2_mxh[:, tt * 4:(tt + 1) * 4], 0.0).then_inc(ve.h, 1)
            ve.inc(1)
        p.V(vrc)

        vhn_sv = {}
        for fc in range(NFC):
            b = fc % 2
            sv = vhn_sv.get(fc - 2)

            def ldh(sc, fc=fc, tt=tt, b=b, sv=sv):
                if sv is None:
                    sc.wait_ge(ve.h, ve.n)
                else:
                    sc.wait_ge(ve.h, sv)
                sc.dma_start(out=p.h2st[:, b * 512:(b + 1) * 512],
                             in_=p.h2sp[fc, :, tt * 512:(tt + 1) * 512]
                             ).then_inc(dh.h, 16)
                dh.inc(16)
            p.S(ldh)

            def vhn(v, fc=fc, b=b):
                v.wait_ge(dh.h, dh.n)
                gc = p.lngs[:, fc:fc + 1]
                bc = p.lnbs[:, fc:fc + 1]
                v.scalar_tensor_tensor(p.pe2u, p.h2st[:, b * 512:(b + 1) * 512],
                                       gc, p.RAb, ALU.mult, ALU.mult)
                v.scalar_tensor_tensor(p.pe2u, p.RCb, gc, p.pe2u,
                                       ALU.mult, ALU.add)
                v.tensor_scalar(p.pe2u, p.pe2u, bc, None, ALU.add
                                ).then_inc(ve.h, 1)
                ve.inc(1)
            p.V(vhn)
            vhn_sv[fc] = ve.n

            for j in range(4):
                def tth(t, j=j):
                    if j == 0:
                        t.wait_ge(ve.h, ve.n)
                    t.transpose(p.pk[7][:, j * 128:(j + 1) * 128],
                                p.pe2u[:, j * 128:(j + 1) * 128],
                                p.identf).then_inc(te.h, 1)
                    te.inc(1)
                p.T(tth)

            def vmx(v, tt=tt, fc=fc):
                v.wait_ge(te.h, te.n)
                for j in range(3):
                    v.reduce_max(p.s2_tmp[:, j:j + 1],
                                 p.pk[7][:, j * 128:(j + 1) * 128], axis=AX.X,
                                 apply_absolute_value=True)
                _ser(v, ve, v.reduce_max(p.s2_tmp[:, 3:4],
                                         p.pk[7][:, 384:512], axis=AX.X,
                                         apply_absolute_value=True))
                for j in range(4):
                    col = tt * 4 + j
                    inst = v.tensor_tensor(p.s2_mxh[:, col:col + 1],
                                           p.s2_mxh[:, col:col + 1],
                                           p.s2_tmp[:, j:j + 1], ALU.max)
                inst.then_inc(ve.h, 1)
                ve.inc(1)
            p.V(vmx)

    if p.debug == 3:
        return
    def vf4(v):
        v.wait_ge(ve.h, ve.n)
        _ser(v, ve, v.reciprocal(p.s2_c2, p.s2_mxh))
        _ser(v, ve, v.tensor_scalar(p.s2_c2, p.s2_c2, 127.0, None, ALU.mult))
        _ser(v, ve, v.tensor_tensor(p.s2_dq2, p.s2_r2, p.s2_mxh, ALU.mult))
        v.tensor_scalar(p.s2_dq2, p.s2_dq2, 1.0 / 127.0, None, ALU.mult
                        ).then_inc(ve.h, 1)
        ve.inc(1)
    p.V(vf4)
    _col2row(p, p.s2_dq2, 8, p.pk[6], p.dq2row)

    for tt in range(2):
        _phase4c_tt(p, tt)
        if p.debug == 10 and tt == 0:
            return


def _phase4c_tt(p, tt):
    dw, dh, te, ve, se, dsp = (p.sem["dw"], p.sem["dh3"], p.sem["te"],
                               p.sem["ve"], p.sem["se"], p.sem["dsp"])
    dx, dst = p.sem["dx"], p.sem["dst"]

    def vr0(v):
        v.wait_ge(ve.h, ve.n)
        v.tensor_tensor(p.s2_tmp, p.s2_rstd2, p.s2_c2, ALU.mult
                        ).then_inc(ve.h, 1)
        ve.inc(1)
        v.wait_ge(ve.h, ve.n)
    p.V(vr0)
    _col2row(p, p.s2_tmp[:, tt * 4:(tt + 1) * 4], 4, p.pk[6], p.rowa)
    _col2row(p, p.s2_c2[:, tt * 4:(tt + 1) * 4], 4, p.pk[6], p.rowc)

    def vr1(v):
        v.wait_ge(dsp.h, dsp.n)
        _ser(v, ve, v.tensor_tensor(p.s2_tmp, p.s2_tmp, p.s2_mu2, ALU.mult))
        v.tensor_scalar(p.s2_tmp, p.s2_tmp, -1.0, None, ALU.mult
                        ).then_inc(ve.h, 1)
        ve.inc(1)
    p.V(vr1)
    _col2row(p, p.s2_tmp[:, tt * 4:(tt + 1) * 4], 4, p.pk[6], p.rowd)

    for nm, row, dstb in [("ra", "rowa", "RAb"), ("rb", "rowc", "RBb"),
                          ("rc", "rowd", "RCb")]:
        def tbb(t, row=row):
            t.wait_ge(dsp.h, dsp.n)
            t.wait_ge(ve.h, ve.n)
            t.matmul(p.pk[6][:, :], p.onesr, getattr(p, row), start=True,
                     stop=True).then_inc(te.h, 1)
            te.inc(1)
        p.T(tbb)

        def vbb(v, dstb=dstb):
            v.wait_ge(te.h, te.n)
            v.tensor_copy(getattr(p, dstb), p.pk[6][:, :]).then_inc(ve.h, 1)
            ve.inc(1)
        p.V(vbb)

    def tb2(t, tt=tt):
        t.wait_ge(ve.h, ve.n)
        t.matmul(p.pk[6][:, :], p.thr2row, p.dq2row[:, tt * 512:(tt + 1) * 512],
                 start=True, stop=True).then_inc(te.h, 1)
        te.inc(1)
    p.T(tb2)

    def vb2(v):
        v.wait_ge(te.h, te.n)
        v.tensor_copy(p.bc1[:, 0:512], p.pk[6][:, :]).then_inc(ve.h, 1)
        ve.inc(1)
    p.V(vb2)

    vq2_sv = {}
    for fc in range(NFC):
        b = fc % 2
        sv = vq2_sv.get(fc - 2)

        def ldh(sc, fc=fc, tt=tt, b=b, sv=sv):
            if sv is None:
                sc.wait_ge(ve.h, ve.n)
            else:
                sc.wait_ge(ve.h, sv)
            sc.dma_start(out=p.h2st[:, b * 512:(b + 1) * 512],
                         in_=p.h2sp[fc, :, tt * 512:(tt + 1) * 512]
                         ).then_inc(dh.h, 16)
            dh.inc(16)
        p.S(ldh)

        def vq2(v, fc=fc, b=b):
            v.wait_ge(dh.h, dh.n)
            gc = p.lngs[:, fc:fc + 1]
            bc = p.lnbs[:, fc:fc + 1]
            v.scalar_tensor_tensor(p.pe2u, p.h2st[:, b * 512:(b + 1) * 512],
                                   gc, p.RAb, ALU.mult, ALU.mult)
            v.scalar_tensor_tensor(p.pe2u, p.RCb, gc, p.pe2u, ALU.mult, ALU.add)
            v.scalar_tensor_tensor(p.pe2u, p.RBb, bc, p.pe2u, ALU.mult, ALU.add)
            v.tensor_scalar(p.pe2b, p.pe2u, 128.5, None, ALU.add)
            v.tensor_copy(p.pe2u, p.pe2b)
            for c in [128.0, 64.0, 32.0, 16.0, 8.0, 4.0, 2.0, 1.0]:
                v.tensor_scalar(p.pe2c, p.pe2b, c, None, ALU.is_ge)
                v.scalar_tensor_tensor(p.pe2b, p.pe2c, -c, p.pe2b,
                                       ALU.mult, ALU.add)
            v.scalar_tensor_tensor(p.xq2T[:, fc * 512:(fc + 1) * 512],
                                   p.pe2u, 128.0, p.pe2b,
                                   ALU.subtract, ALU.subtract).then_inc(ve.h, 1)
            ve.inc(1)
        p.V(vq2)
        vq2_sv[fc] = ve.n

    tern2_snap = None
    for ds in range(NDC):
        def ldw2(sy, ds=ds, snap=tern2_snap):
            if snap is None:
                sy.wait_ge(se.h, se.n)
                sy.wait_ge(dst.h, dst.n)
            else:
                sy.wait_ge(se.h, snap)
            sy.dma_start(out=p.w2sl, in_=p.w2[ds * 128:(ds + 1) * 128, :]
                         ).then_inc(dw.h, 16)
            dw.inc(16)
        p.SY(ldw2)

        def sw2(s):
            s.wait_ge(dw.h, dw.n)
        p.S(sw2)
        for sub in range(4):
            _ternarize(p, p.w2sl[:, sub * D:(sub + 1) * D],
                       p.w2tn[:, sub * D:(sub + 1) * D], 1, D)
        tern2_snap = se.n
        pb16 = p.pk[6][:].bitcast(BF16)
        for fq in range(16):
            def ttr(t, fq=fq):
                t.wait_ge(ve.h, ve.n)
                for ff in range(4):
                    fc = fq * 4 + ff
                    t.transpose(pb16[:, ff * 128:(ff + 1) * 128],
                                p.w2tn[:, fc * 128:(fc + 1) * 128],
                                p.identb).then_inc(te.h, 1)
                    te.inc(1)
            p.T(ttr)

            def vev(v, fq=fq):
                v.wait_ge(te.h, te.n)
                v.tensor_copy(p.w2tT[:, fq * 512:(fq + 1) * 512],
                              pb16[:, 0:512]).then_inc(ve.h, 1)
                ve.inc(1)
            p.V(vev)

        bank = p.pk[7] if ds % 2 == 0 else p.pk[5]

        def tm2(t, ds=ds, bank=bank):
            t.wait_ge(ve.h, ve.n)
            if ds >= 2:
                t.wait_ge(se.h, se.n)
            for fc in range(NFC):
                mmi = t.matmul(bank[:, :],
                               p.w2tT[:, fc * 128:(fc + 1) * 128],
                               p.xq2T[:, fc * 512:(fc + 1) * 512],
                               start=(fc == 0), stop=(fc == NFC - 1))
                if fc == NFC - 1:
                    mmi.then_inc(te.h, 1)
            te.inc(1)
        p.T(tm2)

        def vdq2(v, ds=ds, bank=bank):
            v.wait_ge(te.h, te.n)
            v.wait_ge(se.h, se.n)
            v.scalar_tensor_tensor(p.pe2u, bank[:, :], 0.0, p.bc1[:, 0:512],
                                   ALU.add, ALU.mult).then_inc(ve.h, 1)
            ve.inc(1)
        p.V(vdq2)

        def sg2(s, ds=ds):
            s.wait_ge(ve.h, ve.n)
            s.activation(p.gkeep[:, ds * 512:(ds + 1) * 512], p.pe2u, AF.Silu,
                         bias=p.b2s[:, ds:ds + 1]).then_inc(se.h, 1)
            se.inc(1)
        p.S(sg2)

    for ds in range(NDC):
        def ldsT(sc, ds=ds, tt=tt):
            sc.wait_ge(ve.h, ve.n)
            sc.dma_start(out=p.sTr, in_=p.sTsp[ds, :, tt * 512:(tt + 1) * 512]
                         ).then_inc(p.sem["dw2"].h, 16)
            p.sem["dw2"].inc(16)
        p.S(ldsT)

        def vsg(v, ds=ds):
            v.wait_ge(p.sem["dw2"].h, p.sem["dw2"].n)
            v.wait_ge(se.h, se.n)
            v.tensor_tensor(p.xq2T[:, ds * 512:(ds + 1) * 512], p.sTr,
                            p.gkeep[:, ds * 512:(ds + 1) * 512],
                            ALU.mult).then_inc(ve.h, 1)
            ve.inc(1)
        p.V(vsg)

    for j in range(4):
        ot = tt * 4 + j

        def ldx(sy, ot=ot):
            sy.wait_ge(ve.h, ve.n)
            sy.dma_start(out=p.xrel,
                         in_=p.xin[(8 + ot) * 128:(9 + ot) * 128, :]
                         ).then_inc(dx.h, 16)
            dx.inc(16)
        p.SY(ldx)

        pb16g = p.pk[4][:].bitcast(BF16)
        for dq4 in range(4):
            def ttg(t, j=j, dq4=dq4):
                t.wait_ge(ve.h, ve.n)
                for dd in range(4):
                    ds = dq4 * 4 + dd
                    t.transpose(pb16g[:, dd * 128:(dd + 1) * 128],
                                p.xq2T[:, ds * 512 + j * 128:
                                       ds * 512 + (j + 1) * 128],
                                p.identb).then_inc(te.h, 1)
                    te.inc(1)
            p.T(ttg)

            def vout(v, dq4=dq4):
                v.wait_ge(te.h, te.n)
                v.wait_ge(dx.h, dx.n)
                if dq4 == 0:
                    v.wait_ge(dst.h, dst.n)
                for dd in range(4):
                    ds = dq4 * 4 + dd
                    v.tensor_tensor(p.ostg[:, ds * 128:(ds + 1) * 128],
                                    p.xrel[:, ds * 128:(ds + 1) * 128],
                                    pb16g[:, dd * 128:(dd + 1) * 128], ALU.add)
                v.memset(p.st_t0[:, 15:16], 0.0).then_inc(ve.h, 1)
                ve.inc(1)
            p.V(vout)

        def sto(s, ot=ot):
            s.wait_ge(ve.h, ve.n)
            s.dma_start(out=p.out[ot * 128:(ot + 1) * 128, :],
                        in_=p.ostg).then_inc(dst.h, 16)
            dst.inc(16)
        p.S(sto)



def _debug_dump(p):
    if not p.debug:
        return
    ge, ve, te = p.sem["ge"], p.sem["ve"], p.sem["te"]
    d = p.dbg

    def gdump(g):
        g.wait_ge(ve.h, ve.n)
        g.wait_ge(te.h, te.n)
        for s_ in ["dx", "dc", "dw", "dq", "dsp", "se", "dst", "dh2", "dh3",
                   "dw2"]:
            g.wait_ge(p.sem[s_].h, p.sem[s_].n)
        if p.debug == 2:
            for dst_ap, src in [(d["d_kh"], p.kh), (d["d_kl"], p.kl)]:
                g.dma_start(out=dst_ap.bitcast(BF16), in_=src).then_inc(ge.h, 16)
                ge.inc(16)
            g.dma_start(out=d["d_vtm"], in_=p.vtm).then_inc(ge.h, 16)
            ge.inc(16)
            g.dma_start(out=d["d_xq"].bitcast(BF16), in_=p.xqsp).then_inc(ge.h, 16)
            ge.inc(16)
        st_list = [p.st_mu, p.st_rstd, p.st_c1, p.st_dq1, p.st_mx, p.st_t1,
                   p.st_sx, p.st_sxx]
        for k, stt in enumerate(st_list):
            g.dma_start(out=d["d_st"][:, k * NT128:(k + 1) * NT128],
                        in_=stt).then_inc(ge.h, 16)
            ge.inc(16)
    p.G(gdump)


def _finish(p):
    dst, ge = p.sem["dst"], p.sem["ge"]

    def fin(g):
        g.wait_ge(dst.h, dst.n)
        g.wait_ge(ge.h, ge.n)
    p.G(fin)


def _host_consts(core):
    identf = np.eye(128, dtype=np.float32)
    identb = identf.astype(ml_dtypes.bfloat16)
    start = core * TOK
    attl = np.zeros((8, GROUP * NKT * 128), dtype=np.float64)
    attr = np.zeros((8, GROUP * 2 * 512), dtype=np.float64)
    attl[4:8] = 1.0
    for g in range(GROUP):
        s = SLOPE_G[g]
        for kt in range(NKT):
            kl = np.arange(128)
            kg = start - HALO + kt * 128 + kl
            a = s * (kt * 128 + kl - HALO).astype(np.float64)
            a[kg < 0] = NEG
            sp = _split_bf16(a, 4)
            for r in range(4):
                attl[r, (g * NKT + kt) * 128:(g * NKT + kt + 1) * 128] = \
                    sp[r].astype(np.float64)
        for qt in range(2):
            ql = np.arange(512)
            bv = -s * (qt * 512 + ql).astype(np.float64)
            sp = _split_bf16(bv, 4)
            col = (g * 2 + qt) * 512
            attr[0:4, col:col + 512] = 1.0
            for r in range(4):
                attr[4 + r, col:col + 512] = sp[r].astype(np.float64)
    maskb = np.zeros((128, 896), dtype=np.float64)
    for klp in range(128):
        maskb[klp, : klp + 384] = NEG
    return {
        "identf": identf, "identb": identb.view(np.uint16),
        "attl": attl.astype(ml_dtypes.bfloat16).view(np.uint16),
        "attr": attr.astype(ml_dtypes.bfloat16).view(np.uint16),
        "maskb": maskb.astype(ml_dtypes.bfloat16).view(np.uint16),
        "onesc": np.ones((128, 2), dtype=np.float32),
        "onesr": np.ones((1, 128), dtype=np.float32),
    }


def make_inmaps(inputs):
    x = np.ascontiguousarray(np.asarray(inputs["L0_attn_x"], dtype=np.float32)
                             .reshape(T_FULL, D))
    full = {
        "Wq": np.ascontiguousarray(inputs["L0_attn_Wq"], dtype=np.float32),
        "Wk": np.ascontiguousarray(inputs["L0_attn_Wk"], dtype=np.float32),
        "Wv": np.ascontiguousarray(inputs["L0_attn_Wv"], dtype=np.float32),
        "Wo": np.ascontiguousarray(inputs["L0_attn_Wo"], dtype=np.float32),
        "w1": np.ascontiguousarray(inputs["L0_ffn_w1"], dtype=np.float32),
        "b1": np.ascontiguousarray(inputs["L0_ffn_b1"], dtype=np.float32),
        "lng": np.ascontiguousarray(inputs["L0_ffn_ln_g"], dtype=np.float32),
        "lnb": np.ascontiguousarray(inputs["L0_ffn_ln_b"], dtype=np.float32),
        "w2": np.ascontiguousarray(inputs["L0_ffn_w2"], dtype=np.float32),
        "b2": np.ascontiguousarray(inputs["L0_ffn_b2"], dtype=np.float32),
    }
    maps = []
    for c in range(NCORES):
        lo = c * TOK - HALO
        if lo < 0:
            xs = np.concatenate([np.zeros((-lo, D), np.float32),
                                 x[: (c + 1) * TOK]], axis=0)
        else:
            xs = x[lo:(c + 1) * TOK]
        m = dict(full)
        m["xin"] = np.ascontiguousarray(xs)
        m.update(_host_consts(c))
        maps.append(m)
    return maps


def kernel(**inputs):
    nc = build_program(debug=0)
    maps = make_inmaps(inputs)
    res = run_bass_kernel_spmd(nc, maps, core_ids=list(range(NCORES)))
    out = np.concatenate([np.asarray(r["out"]) for r in res.results], axis=0)
    return out.reshape(1, T_FULL, D)


# version 08
# speedup vs baseline: 1.1982x; 1.0129x over previous
import math
import contextlib
import numpy as np
import ml_dtypes

import concourse.bass as bass
import concourse.mybir as mybir
from concourse.bass_utils import run_bass_kernel_spmd

F32 = mybir.dt.float32
F32R = mybir.dt.float32r
BF16 = mybir.dt.bfloat16
AF = mybir.ActivationFunctionType
ALU = mybir.AluOpType
AX = mybir.AxisListType
U16 = mybir.dt.uint16

T_FULL, D, NH, DH, NKV = 8192, 2048, 16, 128, 4
GROUP = NH // NKV
DKV = NKV * DH
F_INNER = 8192
NFC = F_INNER // 128
NDC = D // 128
EPS = 1e-5
NCORES = 8
TOK = T_FULL // NCORES
HALO = 1024
TT = TOK + HALO
NT128 = TT // 128
NOWN = TOK // 128
WINDOWS = [384, 512, 768, 1024]
NKT = TT // 128
NEG = -1.0e9
ISQD = 1.0 / math.sqrt(DH)
FSL = 512
NSL = F_INNER // FSL


def _alibi_slopes(n):
    start = 2 ** (-(2 ** (-(math.log2(n) - 3))))
    return [start * start ** i for i in range(n)]


SLOPE_G = _alibi_slopes(NH)[:GROUP]


def _split_bf16(a, n):
    a = np.asarray(a, dtype=np.float64)
    out, rem = [], a.copy()
    for _ in range(n):
        q = rem.astype(ml_dtypes.bfloat16)
        out.append(q)
        rem = rem - q.astype(np.float64)
    return out


class Sem:
    def __init__(self, h):
        self.h = h
        self.n = 0

    def inc(self, k=1):
        self.n += k
        return self.n


class MemMap:
    BASE = 17408
    LIMIT = 229344

    def __init__(self, nc):
        self.nc = nc
        self.regions = []

    def alloc(self, name, shape, dtype_, off, phases):
        off = off + self.BASE
        size = shape[1] * mybir.dt.size(dtype_)
        end = off + size
        assert end <= self.LIMIT, f"{name} exceeds SBUF: {end}"
        for (n2, o2, e2, ph2) in self.regions:
            if off < e2 and o2 < end and (phases & ph2):
                raise AssertionError(f"overlap {name}/{n2} in phases {phases & ph2}")
        self.regions.append((name, off, end, phases))
        return self.nc.alloc_sbuf_tensor_at(name, shape, dtype_, offset=off).ap()


class P:
    pass


def _ser(v, ve, inst):
    inst.then_inc(ve.h, 1)
    ve.inc(1)
    v.wait_ge(ve.h, ve.n)


PH1, PH2, PH3, PH4A, PH4B, PH4C, PH4D = 1, 2, 4, 8, 16, 32, 64
ALL = 127


def build_program(debug=0):
    nc = bass.Bass("TRN2", target_bir_lowering=False,
                   detect_race_conditions=False)
    p = P()
    p.nc = nc
    p.debug = debug
    dt = nc.dram_tensor

    p.xin = dt("xin", [TT, D], F32, kind="ExternalInput").ap()
    p.Wq = dt("Wq", [D, D], F32, kind="ExternalInput").ap()
    p.Wk = dt("Wk", [D, DKV], F32, kind="ExternalInput").ap()
    p.Wv = dt("Wv", [D, DKV], F32, kind="ExternalInput").ap()
    p.Wo = dt("Wo", [D, D], F32, kind="ExternalInput").ap()
    p.w1 = dt("w1", [F_INNER, D], F32, kind="ExternalInput").ap()
    p.b1 = dt("b1", [F_INNER], F32, kind="ExternalInput").ap()
    p.lng = dt("lng", [F_INNER], F32, kind="ExternalInput").ap()
    p.lnb = dt("lnb", [F_INNER], F32, kind="ExternalInput").ap()
    p.w2 = dt("w2", [D, F_INNER], F32, kind="ExternalInput").ap()
    p.b2 = dt("b2", [D], F32, kind="ExternalInput").ap()
    p.identf_d = dt("identf", [128, 128], F32, kind="ExternalInput").ap()
    p.identb_d = dt("identb", [128, 128], U16, kind="ExternalInput").ap().bitcast(BF16)
    p.attl_d = dt("attl", [8, GROUP * NKT * 128], U16, kind="ExternalInput").ap().bitcast(BF16)
    p.attr_d = dt("attr", [8, GROUP * 2 * 512], U16, kind="ExternalInput").ap().bitcast(BF16)
    p.maskb_d = dt("maskb", [128, 896], U16, kind="ExternalInput").ap().bitcast(BF16)
    p.onesc_d = dt("onesc", [128, 2], F32, kind="ExternalInput").ap()
    p.onesr_d = dt("onesr", [1, 128], F32, kind="ExternalInput").ap()
    p.out = dt("out", [TOK, D], F32, kind="ExternalOutput").ap()
    p.xqsp = dt("xqspill", [NOWN, 128, NDC * 128], BF16).ap()
    p.h2sp = dt("h2spill", [NFC, 128, TOK], F32).ap()
    p.sTsp = dt("sTspill", [NDC, 128, TOK], BF16).ap()

    p.dbg = {}
    if debug:
        for name, shape, dtp in [
            ("d_kh", [128, NKV * TT], U16), ("d_kl", [128, NKV * TT], U16),
            ("d_vtm", [128, NKV * NKT * 128], F32),
            ("d_xq", [NOWN, 128, NDC * 128], U16),
            ("d_st", [128, NT128 * 8], F32),
            ("d_sT", [NDC, 128, TOK], U16),
            ("d_den", [1, 2 * NH * 512], F32),
            ("d_h2", [NFC, 128, TOK], F32),
            ("d_st2", [128, 8 * 8], F32),
            ("d_gT", [128, NDC * TOK], F32),
        ]:
            p.dbg[name] = dt(name, shape, dtp, kind="ExternalOutput").ap()

    es = contextlib.ExitStack()
    mm = MemMap(nc)
    p.mm = mm

    p.kh = mm.alloc("kh", [128, NKV * TT], BF16, 0, PH2 | PH3)
    p.kl = mm.alloc("kl", [128, NKV * TT], BF16, 16384, PH2 | PH3)
    p.vtm = mm.alloc("vtm", [128, NKV * NKT * 128], F32, 32768, PH2 | PH3)
    p.identf = mm.alloc("identf_s", [128, 128], F32, 65536, ALL)
    p.identb = mm.alloc("identb_s", [128, 128], BF16, 66048, ALL)
    p.maskb = mm.alloc("maskb_s", [128, 896], BF16, 66304, ALL)
    p.onesf = mm.alloc("onesf", [128, 2], F32, 68096, ALL)
    p.onesr = mm.alloc("onesr_s", [1, 128], F32, 68128, ALL)
    p.b1s = mm.alloc("b1s", [128, 64], F32, 68640, ALL)
    p.lngs = mm.alloc("lngs", [128, 64], F32, 68896, ALL)
    p.lnbs = mm.alloc("lnbs", [128, 64], F32, 69152, ALL)
    p.b2s = mm.alloc("b2s", [128, 16], F32, 69408, ALL)
    stoff = 69472
    for nm in ["mu", "rstd", "mx", "mn", "sx", "sxx", "c1", "dq1", "t0", "t1"]:
        setattr(p, "st_" + nm, mm.alloc("st_" + nm, [128, NT128], F32, stoff, ALL))
        stoff += 64
    st2off = 70112
    for nm in ["S1", "SG", "SB", "S2", "SG2", "mu2", "rstd2", "r2", "mxh",
               "c2", "dq2", "tmp"]:
        setattr(p, "s2_" + nm, mm.alloc("s2_" + nm, [128, NOWN], F32, st2off, ALL))
        st2off += 32
    p.wstA = mm.alloc("wstA", [128, 192], F32R, 70496, ALL)
    p.wstB = mm.alloc("wstB", [128, 128], F32R, 71264, ALL)
    p.qcols = mm.alloc("qcols", [128, 8], F32, 71776, ALL)
    p.denrow = mm.alloc("denrow", [1, 512], F32, 71808, ALL)
    p.recrow = mm.alloc("recrow", [1, 512], F32, 73856, ALL)
    p.thrc = mm.alloc("thrc", [128, 2], F32, 75904, ALL)
    p.thrn = mm.alloc("thrn", [128, 2], F32, 75936, ALL)
    p.wacc = mm.alloc("wacc", [128, 128], F32, 75968, ALL)
    p.attl = mm.alloc("attl_s", [8, GROUP * NKT * 128], BF16, 76480, PH1 | PH2 | PH3)
    p.attr = mm.alloc("attr_s", [8, GROUP * 2 * 512], BF16, 92864, PH1 | PH2 | PH3)
    PH4ALL = PH4A | PH4B | PH4C | PH4D
    p.rows4 = mm.alloc("rows4", [1, 3968], F32, 76480, PH4ALL)
    p.xnT = mm.alloc("xnT", [128, NDC * 512], F32, 101056, PH2 | PH3)
    p.wbuf = mm.alloc("wbuf", [128, NDC * 512], F32, 133824, PH2)
    p.wq0 = mm.alloc("wq0", [128, D], F32, 133824, PH3)
    p.wq1 = mm.alloc("wq1", [128, D], F32, 142016, PH3)
    p.qhb = mm.alloc("qhb", [128, 512], BF16, 150208, PH3)
    p.qlb = mm.alloc("qlb", [128, 512], BF16, 151232, PH3)
    p.rbc = mm.alloc("rbc", [128, 512], F32, 152256, PH3)
    p.pbuf = mm.alloc("pbuf", [128, 2 * 512], F32, 154304, PH3)
    p.sTst = mm.alloc("sTst", [128, 2 * 512], BF16, 158400, PH3)
    p.wab = mm.alloc("wab", [128, D], F32, 199360, PH3)
    p.xt0 = mm.alloc("xt0", [128, D], F32, 166592, PH1 | PH2)
    p.xnt = mm.alloc("xnt", [128, D], F32, 174784, PH1 | PH2)
    p.scrb = mm.alloc("scrb", [128, D], BF16, 182976, PH1 | PH2)
    p.scrP = mm.alloc("scrP", [128, 1024], F32, 187072, PH1 | PH2)
    p.puP = mm.alloc("puP", [128, 1024], F32, 191168, PH1 | PH2)
    p.pbP = mm.alloc("pbP", [128, 1024], F32, 195264, PH1 | PH2)
    p.xqst = mm.alloc("xqst", [128, D], BF16, 199360, PH1 | PH2)
    p.ysb = mm.alloc("ysb", [128, NH * 512], F32, 166592, PH3)
    p.xqTr = mm.alloc("xqTr", [128, NDC * TOK], BF16, 32768, PH4A)
    p.wtern = mm.alloc("wtern", [128, 4 * D], BF16, 101056, PH4A)
    p.wtern2 = mm.alloc("wtern2", [128, 4 * D], BF16, 0, PH4A)
    p.tmpb2 = mm.alloc("tmpb2", [128, D], BF16, 16384, PH4A)
    p.wqT = mm.alloc("wqT", [128, NDC * FSL], BF16, 117440, PH4A)
    p.wfsl = mm.alloc("wfsl", [128, 4 * D], F32, 133824, PH4A)
    p.hbuf = mm.alloc("hbuf", [128, 8 * 512], F32, 166592, PH4A)
    p.tmpb = mm.alloc("tmpb", [128, D], BF16, 182976, PH4A | PH4C)
    p.bc1 = mm.alloc("bc1", [128, 2 * 512], F32, 191168, PH4A | PH4C)
    p.h2st = mm.alloc("h2st", [128, 1024], F32, 187072, PH4B | PH4C)
    p.pe2u = mm.alloc("pe2u", [128, 512], F32, 195264, PH4B | PH4C)
    p.pe2b = mm.alloc("pe2b", [128, 512], F32, 197312, PH4B | PH4C)
    p.pe2c = mm.alloc("pe2c", [128, 512], F32, 199360,
                      PH4A | PH4B | PH4C)
    p.RAb = mm.alloc("RAb", [128, 512], F32, 201408, PH4B | PH4C)
    p.RCb = mm.alloc("RCb", [128, 512], F32, 203456, PH4B | PH4C)
    p.RBb = mm.alloc("RBb", [128, 512], F32, 205504, PH4B | PH4C)
    p.statu = mm.alloc("statu", [128, 512], F32R, 207552, PH4B)
    p.statq = mm.alloc("statq", [128, 512], F32R, 209600, PH4B)
    p.w2sl = mm.alloc("w2sl", [128, F_INNER], F32, 0, PH4C)
    p.w2tn = mm.alloc("w2tn", [128, F_INNER], BF16, 32768, PH4C)
    p.w2tT = mm.alloc("w2tT", [128, F_INNER], BF16, 49152, PH4C)
    p.xq2T = mm.alloc("xq2T", [128, NFC * 512], BF16, 101056, PH4C)
    p.gkeep = mm.alloc("gkeep", [128, NDC * 512], BF16, 166592, PH4C)
    p.xrel = mm.alloc("xrel", [128, D], F32, 0, PH4D)
    p.ostg = mm.alloc("ostg", [128, D], F32, 8192, PH4D)
    p.sTr = mm.alloc("sTr", [128, 512], BF16, 16384, PH4D)
    p.dq1row = p.rows4[:, 0:1024]
    p.dq2row = p.rows4[:, 1024:2048]
    p.rowa = p.rows4[:, 2048:2560]
    p.rowc = p.rows4[:, 2560:3072]
    p.rowd = p.rows4[:, 3072:3584]
    p.thr1row = p.rows4[:, 3584:3712]
    p.thr2row = p.rows4[:, 3712:3840]

    p.pk = [es.enter_context(nc.psum_tensor(f"pk{i}", [128, 512], F32))
            for i in range(8)]

    p.sem = {}
    for name in ["dx", "dc", "dw", "dq", "dsp", "dst", "te", "ve", "se", "ge",
                 "dh2", "dh3", "dw2", "dwm"]:
        p.sem[name] = Sem(es.enter_context(nc.semaphore(name)))

    p.block = es.enter_context(nc.Block())
    p.T = p.block.tensor
    p.V = p.block.vector
    p.S = p.block.scalar
    p.G = p.block.gpsimd
    p.SY = p.block.sync
    p.wqslot_snap = [None, None]
    p.sspill_snap = {}
    p.wab_guard = None
    p.wmean_snap = None

    _phase0(p)
    _phase1(p)
    if debug == 4:
        _debug_dump(p)
        _finish(p)
        es.close()
        nc._p = p
        return nc
    _phase2(p, groups=(0, 1, 2), peel=True)
    if debug == 5:
        _phase3(p, qt=0)
    elif debug == 6:
        _phase3(p, qt=0)
        p.wab_guard = p.sem["se"].n
        _phase2(p, groups=(3,), peel=True)
    elif debug == 7:
        _phase3(p, qt=0)
        p.wab_guard = p.sem["se"].n
        _phase2(p, groups=(3,), peel=True)
        _phase3(p, qt=1)
    elif debug != 2:
        _phase3(p, qt=0)
        p.wab_guard = p.sem["se"].n
        _phase2(p, groups=(3,), peel=True)
        _phase3(p, qt=1)
        _phase4(p)
    else:
        _phase2(p, groups=(3,), peel=True)
    _debug_dump(p)
    _finish(p)
    es.close()
    nc._p = p
    return nc


def _phase0(p):
    dc = p.sem["dc"]

    def loads(sy):
        for dst_, src in [(p.identf, p.identf_d), (p.identb, p.identb_d),
                          (p.attl, p.attl_d), (p.attr, p.attr_d),
                          (p.maskb, p.maskb_d), (p.onesf, p.onesc_d),
                          (p.onesr, p.onesr_d)]:
            sy.dma_start(out=dst_, in_=src).then_inc(dc.h, 16)
            dc.inc(16)
        for dst_, src in [(p.b1s, p.b1), (p.lngs, p.lng), (p.lnbs, p.lnb)]:
            sy.dma_start(out=dst_, in_=src.rearrange("(c p) -> p c", p=128),
                         allow_slow_non_contiguous=True).then_inc(dc.h, 16)
            dc.inc(16)
        sy.dma_start(out=p.b2s, in_=p.b2.rearrange("(c p) -> p c", p=128),
                     allow_slow_non_contiguous=True).then_inc(dc.h, 16)
        dc.inc(16)
    p.SY(loads)


def _phase1(p):
    dx, ve, se = p.sem["dx"], p.sem["ve"], p.sem["se"]
    for i in range(NT128):
        def load(sy, i=i):
            if i > 0:
                sy.wait_ge(se.h, se.n)
                sy.wait_ge(ve.h, ve.n)
            sy.dma_start(out=p.xt0, in_=p.xin[i * 128:(i + 1) * 128, :]
                         ).then_inc(dx.h, 16)
            dx.inc(16)
        p.SY(load)

        def scal(s, i=i):
            s.wait_ge(dx.h, dx.n)
            s.activation(p.scrb, p.xt0, AF.Square, accum_out=p.st_sxx[:, i:i + 1])
            s.activation(p.xqst, p.xt0, AF.Copy,
                         accum_out=p.st_sx[:, i:i + 1]).then_inc(se.h, 1)
            se.inc(1)
        p.S(scal)

        def vec(v, i=i):
            v.wait_ge(dx.h, dx.n)
            v.reduce_max(p.st_mx[:, i:i + 1], p.xt0, axis=AX.X)
            v.tensor_reduce(p.st_mn[:, i:i + 1], p.xt0, axis=AX.X,
                            op=ALU.min).then_inc(ve.h, 1)
            ve.inc(1)
        p.V(vec)

    def vst(v):
        v.wait_ge(se.h, se.n)
        inv_d = 1.0 / D
        _ser(v, ve, v.tensor_scalar(p.st_mu, p.st_sx, inv_d, None, ALU.mult))
        _ser(v, ve, v.tensor_scalar(p.st_t0, p.st_sxx, inv_d, None, ALU.mult))
        _ser(v, ve, v.tensor_tensor(p.st_t1, p.st_mu, p.st_mu, ALU.mult))
        _ser(v, ve, v.tensor_tensor(p.st_t0, p.st_t0, p.st_t1, ALU.subtract))
        v.tensor_scalar(p.st_t1, p.st_t0, EPS, None, ALU.add).then_inc(ve.h, 1)
        ve.inc(1)
    p.V(vst)

    def sst(s):
        s.wait_ge(ve.h, ve.n)
        s.activation(p.st_rstd, p.st_t1, AF.Sqrt).then_inc(se.h, 1)
        se.inc(1)
    p.S(sst)

    def vst2(v):
        v.wait_ge(se.h, se.n)
        _ser(v, ve, v.reciprocal(p.st_rstd, p.st_rstd))
        _ser(v, ve, v.tensor_tensor(p.st_t1, p.st_rstd, p.st_rstd, ALU.mult))
        _ser(v, ve, v.tensor_tensor(p.st_t1, p.st_t1, p.st_t0, ALU.mult))
        v.tensor_scalar(p.st_t1, p.st_t1, EPS, None, ALU.add).then_inc(ve.h, 1)
        ve.inc(1)
    p.V(vst2)

    def sst2(s):
        s.wait_ge(ve.h, ve.n)
        s.activation(p.st_t1, p.st_t1, AF.Sqrt).then_inc(se.h, 1)
        se.inc(1)
    p.S(sst2)

    def vst3(v):
        v.wait_ge(se.h, se.n)
        _ser(v, ve, v.reciprocal(p.st_t1, p.st_t1))
        _ser(v, ve, v.tensor_tensor(p.st_mx, p.st_mx, p.st_mu, ALU.subtract))
        _ser(v, ve, v.tensor_tensor(p.st_mn, p.st_mu, p.st_mn, ALU.subtract))
        _ser(v, ve, v.tensor_tensor(p.st_mx, p.st_mx, p.st_mn, ALU.max))
        _ser(v, ve, v.tensor_tensor(p.st_mx, p.st_mx, p.st_rstd, ALU.mult))
        _ser(v, ve, v.reciprocal(p.st_c1, p.st_mx))
        _ser(v, ve, v.tensor_scalar(p.st_c1, p.st_c1, 127.0, None, ALU.mult))
        _ser(v, ve, v.tensor_tensor(p.st_dq1, p.st_t1, p.st_mx, ALU.mult))
        v.tensor_scalar(p.st_dq1, p.st_dq1, 1.0 / 127.0, None,
                        ALU.mult).then_inc(ve.h, 1)
        ve.inc(1)
    p.V(vst3)


def _phase2(p, groups, peel):
    dx, dw, dsp = p.sem["dx"], p.sem["dw"], p.sem["dsp"]
    te, ve, se, dc = p.sem["te"], p.sem["ve"], p.sem["se"], p.sem["dc"]

    def wc(t):
        t.wait_ge(dc.h, dc.n)
    p.T(wc)

    for g4 in groups:
        own = g4 >= 2
        for j in range(4):
            i = g4 * 4 + j

            def load(sy, i=i):
                sy.wait_ge(ve.h, ve.n)
                sy.dma_start(out=p.xt0, in_=p.xin[i * 128:(i + 1) * 128, :]
                             ).then_inc(dx.h, 16)
                dx.inc(16)
            p.SY(load)

            def vxn(v, i=i):
                v.wait_ge(dx.h, dx.n)
                v.tensor_scalar(p.xnt, p.xt0, p.st_mu[:, i:i + 1],
                                p.st_rstd[:, i:i + 1], ALU.subtract,
                                ALU.mult).then_inc(ve.h, 1)
                ve.inc(1)
            p.V(vxn)

            for cq in range(4):
                def ttr(t, cq=cq):
                    t.wait_ge(ve.h, ve.n)
                    for cc in range(4):
                        c = cq * 4 + cc
                        t.transpose(p.pk[cq % 2][:, cc * 128:(cc + 1) * 128],
                                    p.xnt[:, c * 128:(c + 1) * 128],
                                    p.identf).then_inc(te.h, 1)
                        te.inc(1)
                p.T(ttr)

                def vev(v, cq=cq, j=j):
                    v.wait_ge(te.h, te.n)
                    for cc in range(4):
                        c = cq * 4 + cc
                        v.tensor_copy(
                            p.xnT[:, c * 512 + j * 128: c * 512 + (j + 1) * 128],
                            p.pk[cq % 2][:, cc * 128:(cc + 1) * 128])
                    v.memset(p.st_t0[:, 15:16], 0.0).then_inc(ve.h, 1)
                    ve.inc(1)
                p.V(vev)

            if own and peel:
                _peel1(p, i)

        def ldwk(sy, g4=g4):
            sy.wait_ge(te.h, te.n)
            sy.wait_ge(p.sem["dh3"].h, p.sem["dh3"].n)
            sy.dma_start(out=p.wbuf.rearrange("p (c n) -> p c n", c=NDC),
                         in_=p.Wk.rearrange("(c p) n -> p c n", p=128)
                         ).then_inc(dw.h, 16)
            dw.inc(16)
        p.SY(ldwk)

        for kv in range(NKV):
            def tk(t, kv=kv):
                t.wait_ge(dw.h, dw.n)
                t.wait_ge(ve.h, ve.n)
                for c in range(NDC):
                    mmi = t.matmul(p.pk[2][:, :],
                                   p.wbuf[:, c * 512 + kv * 128:
                                          c * 512 + (kv + 1) * 128],
                                   p.xnT[:, c * 512:(c + 1) * 512],
                                   start=(c == 0), stop=(c == NDC - 1))
                    if c == NDC - 1:
                        mmi.then_inc(te.h, 1)
                te.inc(1)
            p.T(tk)

            def vk(v, kv=kv, g4=g4):
                v.wait_ge(te.h, te.n)
                dsth = p.kh[:, kv * TT + g4 * 512: kv * TT + (g4 + 1) * 512]
                dstl = p.kl[:, kv * TT + g4 * 512: kv * TT + (g4 + 1) * 512]
                _ser(v, ve, v.tensor_copy(dsth, p.pk[2][:, :]))
                v.tensor_tensor(dstl, p.pk[2][:, :], dsth,
                                ALU.subtract).then_inc(ve.h, 1)
                ve.inc(1)
            p.V(vk)

        def ldwv(sy):
            sy.wait_ge(ve.h, ve.n)
            sy.dma_start(out=p.wbuf.rearrange("p (c n) -> p c n", c=NDC),
                         in_=p.Wv.rearrange("(c p) n -> p c n", p=128)
                         ).then_inc(dw.h, 16)
            dw.inc(16)
        p.SY(ldwv)

        for j in range(4):
            kt = g4 * 4 + j

            def tv(t, j=j):
                t.wait_ge(dw.h, dw.n)
                t.wait_ge(ve.h, ve.n)
                for c in range(NDC):
                    mmi = t.matmul(p.pk[3][:, :],
                                   p.xnT[:, c * 512 + j * 128:
                                         c * 512 + (j + 1) * 128],
                                   p.wbuf[:, c * 512:(c + 1) * 512],
                                   start=(c == 0), stop=(c == NDC - 1))
                    if c == NDC - 1:
                        mmi.then_inc(te.h, 1)
                te.inc(1)
            p.T(tv)

            def vv(v, kt=kt):
                v.wait_ge(te.h, te.n)
                for kvh in range(NKV):
                    v.tensor_copy(
                        p.vtm[:, (kt * NKV + kvh) * 128:(kt * NKV + kvh + 1) * 128],
                        p.pk[3][:, kvh * 128:(kvh + 1) * 128])
                v.memset(p.st_t0[:, 15:16], 0.0).then_inc(ve.h, 1)
                ve.inc(1)
            p.V(vv)
    p.wqslot_snap = [None, None]


def _peel1(p, i):
    ve, te, dsp = p.sem["ve"], p.sem["te"], p.sem["dsp"]
    ot = i - 8

    def vpeel(v, i=i):
        for hh in range(2):
            sl = slice(hh * 1024, (hh + 1) * 1024)
            v.tensor_scalar(p.puP, p.xnt[:, sl], p.st_c1[:, i:i + 1], 128.5,
                            ALU.mult, ALU.add)
            v.tensor_scalar(p.scrP, p.puP, 128.0, -128.0, ALU.is_ge, ALU.mult)
            v.tensor_tensor(p.pbP, p.puP, p.scrP, ALU.add)
            for c in [64.0, 32.0, 16.0, 8.0, 4.0, 2.0, 1.0]:
                v.tensor_scalar(p.scrP, p.pbP, c, -c, ALU.is_ge, ALU.mult)
                v.tensor_tensor(p.pbP, p.pbP, p.scrP, ALU.add)
            v.scalar_tensor_tensor(p.scrb[:, sl], p.puP, 128.0, p.pbP,
                                   ALU.subtract, ALU.subtract)
        v.memset(p.st_t0[:, 15:16], 0.0).then_inc(ve.h, 1)
        ve.inc(1)
    p.V(vpeel)

    pb16 = p.pk[5][:].bitcast(BF16)
    for cq in range(4):
        def ttr(t, cq=cq):
            t.wait_ge(ve.h, ve.n)
            for cc in range(4):
                c = cq * 4 + cc
                t.transpose(pb16[:, cc * 128:(cc + 1) * 128],
                            p.scrb[:, c * 128:(c + 1) * 128],
                            p.identb).then_inc(te.h, 1)
                te.inc(1)
        p.T(ttr)

        def vev(v, cq=cq):
            v.wait_ge(te.h, te.n)
            if cq == 0:
                v.wait_ge(p.sem["dh3"].h, p.sem["dh3"].n)
                if p.wab_guard is not None:
                    v.wait_ge(p.sem["se"].h, p.wab_guard)
            for cc in range(4):
                c = cq * 4 + cc
                v.tensor_copy(p.xqst[:, c * 128:(c + 1) * 128],
                              pb16[:, cc * 128:(cc + 1) * 128])
            v.memset(p.st_t0[:, 15:16], 0.0).then_inc(ve.h, 1)
            ve.inc(1)
        p.V(vev)

    def spill(s, ot=ot):
        s.wait_ge(ve.h, ve.n)
        s.dma_start(out=p.xqsp[ot, :, :], in_=p.xqst
                    ).then_inc(p.sem["dh3"].h, 16)
        p.sem["dh3"].inc(16)
    p.S(spill)


def _wmean_chunks(p, h, qt):
    se, dwm, dh3 = p.sem["se"], p.sem["dwm"], p.sem["dh3"]
    for i in range(4):
        g = qt * 64 + h * 4 + i
        if g < 64:
            src = p.w1.rearrange("(c p) d -> p c d", p=128)[:, g, :]
        else:
            j = g - 64
            src = p.w2.rearrange("(c p) f -> p c f", p=128)[
                :, j // 4, (j % 4) * D:(j % 4 + 1) * D]

        def ldc(sy, src=src, g=g, snap=p.wmean_snap):
            if g == 0 or g == 64:
                sy.wait_ge(dh3.h, dh3.n)
            if snap is not None:
                sy.wait_ge(se.h, snap)
            sy.dma_start(out=p.wab, in_=src).then_inc(dwm.h, 16)
            dwm.inc(16)
        p.SY(ldc)

        def sabs(s, g=g):
            s.wait_ge(dwm.h, dwm.n)
            s.activation(p.wab, p.wab, AF.Abs,
                         accum_out=p.wacc[:, g:g + 1]).then_inc(se.h, 1)
            se.inc(1)
        p.S(sabs)
        p.wmean_snap = se.n


def _phase3(p, qt):
    dw, te, ve, se, dsp = (p.sem["dw"], p.sem["te"], p.sem["ve"], p.sem["se"],
                           p.sem["dsp"])
    dh3 = p.sem["dh3"]
    p.sspill_snap = {}
    Qloc = HALO + qt * 512
    qtile = Qloc // 128

    for h in range(NH):
        g = h // GROUP
        kv = h // GROUP
        W = WINDOWS[g]
        kt0 = (Qloc - W) // 128
        kt1 = Qloc // 128 + 4
        kts = [kt for kt in range(kt0, kt1) if kt >= 0]

        wslot = p.wq0 if h % 2 == 0 else p.wq1

        def ldwq(sy, h=h, wslot=wslot, snap=p.wqslot_snap[h % 2]):
            if snap is None:
                sy.wait_ge(te.h, te.n)
            else:
                sy.wait_ge(te.h, snap)
            sy.dma_start(out=wslot.rearrange("p (c n) -> p c n", c=NDC),
                         in_=p.Wq[:, h * 128:(h + 1) * 128].rearrange(
                             "(c p) n -> p c n", p=128)).then_inc(dw.h, 16)
            dw.inc(16)
        p.SY(ldwq)

        def tq(t, wslot=wslot):
            t.wait_ge(dw.h, dw.n)
            t.wait_ge(ve.h, ve.n)
            for c in range(NDC):
                mmi = t.matmul(p.pk[4][:, :],
                               wslot[:, c * 128:(c + 1) * 128],
                               p.xnT[:, c * 512:(c + 1) * 512],
                               start=(c == 0), stop=(c == NDC - 1))
                if c == NDC - 1:
                    mmi.then_inc(te.h, 1)
            te.inc(1)
        p.T(tq)
        p.wqslot_snap[h % 2] = te.n

        def vq(v):
            v.wait_ge(te.h, te.n)
            _ser(v, ve, v.tensor_scalar(p.pbuf[:, 0:512], p.pk[4][:, :], ISQD,
                                        None, ALU.mult))
            _ser(v, ve, v.tensor_copy(p.qhb, p.pbuf[:, 0:512]))
            v.tensor_tensor(p.qlb, p.pbuf[:, 0:512], p.qhb,
                            ALU.subtract).then_inc(ve.h, 1)
            ve.inc(1)
        p.V(vq)

        nv = len(kts)
        for vi, kt in enumerate(kts):
            sb_ = p.pk[vi % 2]
            diag = kt * 128 >= Qloc

            def tsc(t, kt=kt, vi=vi, sb_=sb_, diag=diag, g=g, kv=kv, h=h,
                    qt=qt, nv=nv):
                if vi == 0:
                    t.wait_ge(ve.h, ve.n)
                if vi >= 2:
                    t.wait_ge(se.h, se.n - 1)
                ksl = slice(kv * TT + kt * 128, kv * TT + (kt + 1) * 128)
                t.matmul(sb_[:, :], p.kh[:, ksl], p.qhb, start=True, stop=False)
                t.matmul(sb_[:, :], p.kh[:, ksl], p.qlb, start=False, stop=False)
                t.matmul(sb_[:, :], p.kl[:, ksl], p.qhb, start=False, stop=False)
                la = p.attl[:, (g * NKT + kt) * 128:(g * NKT + kt + 1) * 128]
                ra = p.attr[:, (g * 2 + qt) * 512:(g * 2 + qt + 1) * 512]
                if diag:
                    t.matmul(sb_[:, :], la, ra, start=False, stop=False)
                    joff = (kt * 128 - Qloc) // 128
                    msl = p.maskb[:, 384 - 128 * joff: 384 - 128 * joff + 512]
                    t.matmul(sb_[:, :], p.identb, msl, start=False,
                             stop=True).then_inc(te.h, 1)
                else:
                    t.matmul(sb_[:, :], la, ra, start=False,
                             stop=True).then_inc(te.h, 1)
                te.inc(1)
            p.T(tsc)

            def sexp(s, vi=vi, sb_=sb_):
                s.wait_ge(te.h, te.n)
                s.activation(p.pbuf[:, (vi % 2) * 512:(vi % 2 + 1) * 512],
                             sb_[:, :], AF.Exp).then_inc(se.h, 1)
                se.inc(1)
            p.S(sexp)

            def tav(t, vi=vi, kt=kt, kv=kv, nv=nv):
                t.wait_ge(se.h, se.n)
                pb = p.pbuf[:, (vi % 2) * 512:(vi % 2 + 1) * 512]
                vsl = p.vtm[:, (kt * NKV + kv) * 128:(kt * NKV + kv + 1) * 128]
                t.matmul(p.pk[2][:, :], vsl, pb,
                         start=(vi == 0), stop=(vi == nv - 1))
                mmi = t.matmul(p.pk[3][0:1, :], p.onesf[:, 0:1], pb,
                               start=(vi == 0), stop=(vi == nv - 1))
                mmi.then_inc(te.h, 1)
                te.inc(1)
            p.T(tav)

        def vden(v):
            v.wait_ge(te.h, te.n)
            _ser(v, ve, v.tensor_copy(p.denrow, p.pk[3][0:1, :]))
            v.reciprocal(p.recrow, p.denrow).then_inc(ve.h, 1)
            ve.inc(1)
        p.V(vden)

        def trb(t):
            t.wait_ge(ve.h, ve.n)
            t.matmul(p.pk[5][:, :], p.onesr, p.recrow, start=True,
                     stop=True).then_inc(te.h, 1)
            te.inc(1)
        p.T(trb)

        def vyev(v, h=h):
            v.wait_ge(te.h, te.n)
            v.tensor_copy(p.rbc, p.pk[5][:, :])
            v.scalar_tensor_tensor(p.ysb[:, h * 512:(h + 1) * 512],
                                   p.pk[2][:, :], 0.0, p.rbc,
                                   ALU.add, ALU.mult).then_inc(ve.h, 1)
            ve.inc(1)
        p.V(vyev)

        _wmean_chunks(p, h, qt)

        if p.debug and qt == 0:
            def dden(g_, h=h):
                g_.wait_ge(ve.h, ve.n)
                g_.dma_start(out=p.dbg["d_den"][:, h * 512:(h + 1) * 512],
                             in_=p.denrow).then_inc(p.sem["ge"].h, 16)
                p.sem["ge"].inc(16)
            p.G(dden)

    for dc_ in range(NDC):
        def ldwo(sy, dc_=dc_, snap=p.wqslot_snap[dc_ % 2]):
            if snap is None:
                sy.wait_ge(te.h, te.n)
            else:
                sy.wait_ge(te.h, snap)
            slot = p.wq0 if dc_ % 2 == 0 else p.wq1
            sy.dma_start(out=slot.rearrange("p (c n) -> p c n", c=NDC),
                         in_=p.Wo[:, dc_ * 128:(dc_ + 1) * 128].rearrange(
                             "(c p) n -> p c n", p=128)).then_inc(dw.h, 16)
            dw.inc(16)
        p.SY(ldwo)

        def two(t, dc_=dc_):
            t.wait_ge(dw.h, dw.n)
            if dc_ == 0:
                t.wait_ge(ve.h, ve.n)
            elif dc_ >= 2:
                t.wait_ge(ve.h, ve.n - 1)
            slot = p.wq0 if dc_ % 2 == 0 else p.wq1
            bank = p.pk[6] if dc_ % 2 == 0 else p.pk[7]
            for c in range(NDC):
                mmi = t.matmul(bank[:, :], slot[:, c * 128:(c + 1) * 128],
                               p.ysb[:, c * 512:(c + 1) * 512],
                               start=(c == 0), stop=(c == NDC - 1))
                if c == NDC - 1:
                    mmi.then_inc(te.h, 1)
            te.inc(1)
        p.T(two)
        p.wqslot_snap[dc_ % 2] = te.n

        def vso(v, dc_=dc_, sv=p.sspill_snap.get(dc_ - 2)):
            v.wait_ge(te.h, te.n)
            if sv is not None:
                v.wait_ge(dh3.h, sv)
            bank = p.pk[6] if dc_ % 2 == 0 else p.pk[7]
            v.tensor_copy(p.sTst[:, (dc_ % 2) * 512:(dc_ % 2 + 1) * 512],
                          bank[:, :]).then_inc(ve.h, 1)
            ve.inc(1)
        p.V(vso)

        def sspill(s_, dc_=dc_, qt=qt):
            s_.wait_ge(ve.h, ve.n)
            s_.dma_start(out=p.sTsp[dc_, :, qt * 512:(qt + 1) * 512],
                         in_=p.sTst[:, (dc_ % 2) * 512:(dc_ % 2 + 1) * 512]
                         ).then_inc(dh3.h, 16)
            dh3.inc(16)
        p.S(sspill)
        p.sspill_snap[dc_] = dh3.n


def _col2row(p, src_ap, ncols, psbank, dst_row):
    te, ve, dsp = p.sem["te"], p.sem["ve"], p.sem["dsp"]

    def tt_(t):
        t.wait_ge(ve.h, ve.n)
        t.transpose(psbank[0:ncols, 0:128], src_ap, p.identf).then_inc(te.h, 1)
        te.inc(1)
    p.T(tt_)

    def vv_(v):
        v.wait_ge(te.h, te.n)
        v.wait_ge(dsp.h, dsp.n)
        v.tensor_copy(p.pe2c[0:ncols, 0:128], psbank[0:ncols, 0:128]
                      ).then_inc(ve.h, 1)
        ve.inc(1)
    p.V(vv_)

    def gg_(g):
        g.wait_ge(ve.h, ve.n)
        g.dma_start(out=dst_row.rearrange("a (c b) -> a c b", b=128),
                    in_=p.pe2c[0:ncols, 0:128],
                    allow_slow_non_contiguous=True).then_inc(dsp.h, 16)
        dsp.inc(16)
    p.G(gg_)


def _ternarize(p, src_f32, dst_bf16, thr_col, width, ve_target=None,
               tmp=None):
    se, ve = p.sem["se"], p.sem["ve"]
    if tmp is None:
        tmp = p.tmpb

    def s_(s, ve_target=ve_target, tmp=tmp):
        s.wait_ge(ve.h, ve.n if ve_target is None else ve_target)
        s.activation(dst_bf16, src_f32, AF.Sign,
                     bias=p.thrn[:, thr_col:thr_col + 1])
        s.activation(tmp[:, 0:width], src_f32, AF.Sign,
                     bias=p.thrc[:, thr_col:thr_col + 1]).then_inc(se.h, 1)
        se.inc(1)
    p.S(s_)

    def v_(v, tmp=tmp):
        v.wait_ge(se.h, se.n)
        v.tensor_tensor(dst_bf16, dst_bf16, tmp[:, 0:width],
                        ALU.add).then_inc(ve.h, 1)
        ve.inc(1)
    p.V(v_)


def _phase4(p):
    dw, dh2, te, ve, se, dsp = (p.sem["dw"], p.sem["dh2"], p.sem["te"],
                                p.sem["ve"], p.sem["se"], p.sem["dsp"])
    dq, dx, dst = p.sem["dq"], p.sem["dx"], p.sem["dst"]


    if p.debug == 11:
        return

    def vtot(v):
        v.wait_ge(se.h, se.n)
        v.reduce_sum(p.st_t0[:, 0:1], p.wacc[:, 0:64], axis=AX.X)
        v.reduce_sum(p.st_t0[:, 1:2], p.wacc[:, 64:128], axis=AX.X
                     ).then_inc(ve.h, 1)
        ve.inc(1)
    p.V(vtot)

    def ttot(t):
        t.wait_ge(ve.h, ve.n)
        t.matmul(p.pk[0][0:1, 0:2], p.onesf[:, 0:1], p.st_t0[:, 0:2],
                 start=True, stop=True).then_inc(te.h, 1)
        te.inc(1)
    p.T(ttot)

    def vthr(v):
        v.wait_ge(te.h, te.n)
        v.tensor_scalar(p.rowd[:, 0:2], p.pk[0][0:1, 0:2],
                        0.5 / (F_INNER * D), None, ALU.mult).then_inc(ve.h, 1)
        ve.inc(1)
    p.V(vthr)

    def tthr(t):
        t.wait_ge(ve.h, ve.n)
        t.matmul(p.pk[1][:, 0:2], p.onesr, p.rowd[:, 0:2],
                 start=True, stop=True)
        t.matmul(p.pk[2][0:1, 0:128], p.rowd[:, 0:1], p.onesr,
                 start=True, stop=True)
        t.matmul(p.pk[3][0:1, 0:128], p.rowd[:, 1:2], p.onesr,
                 start=True, stop=True).then_inc(te.h, 1)
        te.inc(1)
    p.T(tthr)

    def vthr2(v):
        v.wait_ge(te.h, te.n)
        _ser(v, ve, v.tensor_copy(p.thrc[:, 0:2], p.pk[1][:, 0:2]))
        _ser(v, ve, v.tensor_scalar(p.thrn[:, 0:2], p.pk[1][:, 0:2], -1.0,
                                    None, ALU.mult))
        v.tensor_copy(p.thr1row, p.pk[2][0:1, 0:128])
        v.tensor_copy(p.thr2row, p.pk[3][0:1, 0:128]).then_inc(ve.h, 1)
        ve.inc(1)
    p.V(vthr2)

    if p.debug == 12:
        return
    _col2row(p, p.st_dq1[:, 8:16], 8, p.pk[4], p.dq1row)
    if p.debug == 15:
        return
    for tt in range(2):
        def tbc(t, tt=tt):
            t.wait_ge(dsp.h, dsp.n)
            t.wait_ge(ve.h, ve.n)
            t.matmul(p.pk[5][:, :], p.thr1row,
                     p.dq1row[:, tt * 512:(tt + 1) * 512],
                     start=True, stop=True).then_inc(te.h, 1)
            te.inc(1)
        p.T(tbc)

        def vbc(v, tt=tt):
            v.wait_ge(te.h, te.n)
            v.tensor_copy(p.bc1[:, tt * 512:(tt + 1) * 512],
                          p.pk[5][:, :]).then_inc(ve.h, 1)
            ve.inc(1)
        p.V(vbc)

    if p.debug == 13:
        return

    def swst(s):
        s.wait_ge(ve.h, ve.n)
        s.activation(p.wstA.rearrange("p (c k) -> p c k", k=3)[:, :, 1],
                     p.lngs, AF.Square, accum_out=p.st_t1[:, 0:1])
        s.activation(p.wstB.rearrange("p (c k) -> p c k", k=2)[:, :, 1],
                     p.lngs, AF.Square).then_inc(se.h, 1)
        se.inc(1)
    p.S(swst)

    def vwst(v):
        v.wait_ge(se.h, se.n)
        v.memset(p.pe2c[:, 64:128], 1.0)
        v.tensor_copy(p.wstA.rearrange("p (c k) -> p c k", k=3)[:, :, 0],
                      p.pe2c[:, 64:128])
        v.tensor_copy(p.wstB.rearrange("p (c k) -> p c k", k=2)[:, :, 0],
                      p.pe2c[:, 64:128])
        v.tensor_tensor(p.wstA.rearrange("p (c k) -> p c k", k=3)[:, :, 2],
                        p.lngs, p.lnbs, ALU.mult)
        _ser(v, ve, v.tensor_tensor(p.pe2c[:, 0:64], p.lngs, p.lnbs, ALU.mult))
        _ser(v, ve, v.reduce_sum(p.st_t1[:, 1:2], p.pe2c[:, 0:64], axis=AX.X))
        _ser(v, ve, v.tensor_tensor(p.pe2c[:, 0:64], p.lnbs, p.lnbs, ALU.mult))
        v.reduce_sum(p.st_t1[:, 2:3], p.pe2c[:, 0:64], axis=AX.X
                     ).then_inc(ve.h, 1)
        ve.inc(1)
    p.V(vwst)

    def tq3(t):
        t.wait_ge(ve.h, ve.n)
        t.matmul(p.pk[0][0:1, 0:3], p.onesf[:, 0:1], p.st_t1[:, 0:3],
                 start=True, stop=True).then_inc(te.h, 1)
        te.inc(1)
    p.T(tq3)

    def vq4b(v):
        v.wait_ge(te.h, te.n)
        v.tensor_copy(p.rowd[:, 8:11], p.pk[0][0:1, 0:3]).then_inc(ve.h, 1)
        ve.inc(1)
    p.V(vq4b)

    def tq4(t):
        t.wait_ge(ve.h, ve.n)
        t.matmul(p.pk[1][:, 0:3], p.onesr, p.rowd[:, 8:11],
                 start=True, stop=True).then_inc(te.h, 1)
        te.inc(1)
    p.T(tq4)

    def vq5(v):
        v.wait_ge(te.h, te.n)
        v.tensor_copy(p.qcols[:, 0:3], p.pk[1][:, 0:3]).then_inc(ve.h, 1)
        ve.inc(1)
    p.V(vq5)

    if p.debug == 14:
        return
    for ot in range(NOWN):
        def ldxq(g, ot=ot):
            g.wait_ge(te.h, te.n)
            g.dma_start(
                out=p.xqTr.rearrange("p (c n) -> p c n", c=NDC)[
                    :, :, ot * 128:(ot + 1) * 128],
                in_=p.xqsp[ot].rearrange("p (c n) -> p c n", c=NDC)
            ).then_inc(dq.h, 16)
            dq.inc(16)
        p.G(ldxq)

    if p.debug == 8:
        return
    tern1_snap = None
    gsp_snap = None
    evac_snap = {}
    gelu_snap = None
    tst_snap = None
    for sl in range(NSL):
        def ldw(sy, sl=sl, snap=tern1_snap):
            if snap is None:
                sy.wait_ge(se.h, se.n)
                sy.wait_ge(te.h, te.n)
                sy.wait_ge(p.sem["dh3"].h, p.sem["dh3"].n)
            else:
                sy.wait_ge(se.h, snap)
            sy.dma_start(out=p.wfsl.rearrange("p (s d) -> p s d", s=4),
                         in_=p.w1[sl * FSL:(sl + 1) * FSL, :].rearrange(
                             "(s p) d -> p s d", p=128)).then_inc(dw.h, 16)
            dw.inc(16)
        p.SY(ldw)

        def sww(s):
            s.wait_ge(dw.h, dw.n)
        p.S(sww)
        wt = p.wtern if sl % 2 == 0 else p.wtern2
        tmp = p.tmpb if sl % 2 == 0 else p.tmpb2
        vt = evac_snap.get(sl - 2)
        for sub in range(4):
            _ternarize(p, p.wfsl[:, sub * D:(sub + 1) * D],
                       wt[:, sub * D:(sub + 1) * D], 0, D,
                       ve_target=vt if sub == 0 else None, tmp=tmp)
        tern1_snap = se.n
        pb16 = p.pk[0][:].bitcast(BF16)
        for c in range(NDC):
            def ttr(t, c=c, wt=wt, gs=gelu_snap):
                t.wait_ge(ve.h, ve.n)
                if c == 0 and gs is not None:
                    t.wait_ge(se.h, gs)
                for sub in range(4):
                    t.transpose(pb16[:, sub * 128:(sub + 1) * 128],
                                wt[:, sub * D + c * 128:
                                   sub * D + (c + 1) * 128],
                                p.identb).then_inc(te.h, 1)
                    te.inc(1)
            p.T(ttr)

            def vev(v, c=c):
                v.wait_ge(te.h, te.n)
                v.tensor_copy(p.wqT[:, c * 512:(c + 1) * 512],
                              pb16[:, 0:512]).then_inc(ve.h, 1)
                ve.inc(1)
            p.V(vev)
        evac_snap[sl] = ve.n

        if sl == 0:
            def wxq(t):
                t.wait_ge(dq.h, dq.n)
            p.T(wxq)
        for rnd in range(2):
            for kb in range(4):
                k = rnd * 4 + kb

                def tmm(t, k=k, kb=kb, rnd=rnd, gs=gelu_snap):
                    t.wait_ge(ve.h, ve.n)
                    if gs is not None:
                        t.wait_ge(se.h, gs)
                    for c in range(NDC):
                        mmi = t.matmul(
                            p.pk[kb][:, :],
                            p.wqT[:, c * 512 + kb * 128:c * 512 + (kb + 1) * 128],
                            p.xqTr[:, c * TOK + rnd * 512:c * TOK + (rnd + 1) * 512],
                            start=(c == 0), stop=(c == NDC - 1))
                        if c == NDC - 1:
                            mmi.then_inc(te.h, 1)
                    te.inc(1)
                p.T(tmm)

                def vdq(v, k=k, kb=kb, rnd=rnd, snap=gsp_snap):
                    v.wait_ge(te.h, te.n)
                    if k == 0 and snap is not None:
                        v.wait_ge(dh2.h, snap)
                    v.scalar_tensor_tensor(
                        p.hbuf[:, k * 512:(k + 1) * 512], p.pk[kb][:, :], 0.0,
                        p.bc1[:, rnd * 512:(rnd + 1) * 512], ALU.add,
                        ALU.mult).then_inc(ve.h, 1)
                    ve.inc(1)
                p.V(vdq)

            def ssilu(s, sl=sl, rnd=rnd):
                s.wait_ge(ve.h, ve.n)
                for kb in range(4):
                    k = rnd * 4 + kb
                    fcol = sl * 4 + kb
                    s.activation(p.pk[kb][:, :], p.hbuf[:, k * 512:(k + 1) * 512],
                                 AF.Silu, bias=p.b1s[:, fcol:fcol + 1])
                s.memzero(p.st_t1[:, 15:16]).then_inc(se.h, 1)
                se.inc(1)
            p.S(ssilu)

            def sgelu(s, rnd=rnd):
                for kb in range(4):
                    k = rnd * 4 + kb
                    s.activation(p.hbuf[:, k * 512:(k + 1) * 512],
                                 p.pk[kb][:, :], AF.Gelu)
                s.memzero(p.st_t1[:, 15:16]).then_inc(se.h, 1)
                se.inc(1)
            p.S(sgelu)
            gelu_snap = se.n

            for kb in range(4):
                k = rnd * 4 + kb

                def gsp(s, k=k, kb=kb, rnd=rnd, sl=sl):
                    fc = sl * 4 + kb
                    s.dma_start(out=p.h2sp[fc, :, rnd * 512:(rnd + 1) * 512],
                                in_=p.hbuf[:, k * 512:(k + 1) * 512]
                                ).then_inc(dh2.h, 16)
                    p.sem["dh2"].inc(16)
                p.S(gsp)

            for kb in range(4):
                k = rnd * 4 + kb
                fc = sl * 4 + kb

                def vcp(v, k=k, ts=tst_snap, gs=gelu_snap):
                    v.wait_ge(se.h, gs)
                    if ts is not None:
                        v.wait_ge(te.h, ts)
                    v.tensor_copy(p.statu,
                                  p.hbuf[:, k * 512:(k + 1) * 512]
                                  ).then_inc(ve.h, 1)
                    ve.inc(1)
                p.V(vcp)

                def ssq(s, k=k, ts=tst_snap):
                    if ts is not None:
                        s.wait_ge(te.h, ts)
                    s.activation(p.statq, p.hbuf[:, k * 512:(k + 1) * 512],
                                 AF.Square).then_inc(se.h, 1)
                    se.inc(1)
                p.S(ssq)

                def tst(t, fc=fc, rnd=rnd, first=(sl == 0 and kb == 0),
                        last=(sl == NSL - 1 and kb == 3)):
                    t.wait_ge(ve.h, ve.n)
                    t.wait_ge(se.h, se.n)
                    t.matmul(p.pk[4 + 2 * rnd][0:3, :],
                             p.wstA[:, fc * 3:(fc + 1) * 3],
                             p.statu, start=first, stop=last)
                    mmi = t.matmul(p.pk[5 + 2 * rnd][0:2, :],
                                   p.wstB[:, fc * 2:(fc + 1) * 2],
                                   p.statq, start=first, stop=last)
                    mmi.then_inc(te.h, 1)
                    te.inc(1)
                p.T(tst)
                tst_snap = te.n
        gsp_snap = p.sem["dh2"].n

    if p.debug == 9:
        return
    dh = p.sem["dh3"]
    for tt in range(2):
        def vevs(v, tt=tt):
            v.wait_ge(te.h, te.n)
            v.tensor_copy(p.pe2c[0:3, 0:512], p.pk[4 + 2 * tt][0:3, :])
            v.tensor_copy(p.pe2u[0:2, 0:512], p.pk[5 + 2 * tt][0:2, :]
                          ).then_inc(ve.h, 1)
            ve.inc(1)
        p.V(vevs)
        for j in range(4):
            def tts(t, j=j):
                t.wait_ge(ve.h, ve.n)
                t.transpose(p.pk[4][:, 0:3],
                            p.pe2c[0:3, j * 128:(j + 1) * 128],
                            p.identf[0:3, 0:3])
                t.transpose(p.pk[5][:, 0:2],
                            p.pe2u[0:2, j * 128:(j + 1) * 128],
                            p.identf[0:2, 0:2]).then_inc(te.h, 1)
                te.inc(1)
            p.T(tts)

            def vcs(v, tt=tt, j=j):
                v.wait_ge(te.h, te.n)
                col = tt * 4 + j
                v.tensor_copy(p.s2_S1[:, col:col + 1], p.pk[4][:, 0:1])
                v.tensor_copy(p.s2_SG[:, col:col + 1], p.pk[4][:, 1:2])
                v.tensor_copy(p.s2_SB[:, col:col + 1], p.pk[4][:, 2:3])
                v.tensor_copy(p.s2_S2[:, col:col + 1], p.pk[5][:, 0:1])
                v.tensor_copy(p.s2_SG2[:, col:col + 1], p.pk[5][:, 1:2]
                              ).then_inc(ve.h, 1)
                ve.inc(1)
            p.V(vcs)

    IFI = 1.0 / F_INNER

    def vf1(v):
        v.wait_ge(ve.h, ve.n)
        _ser(v, ve, v.tensor_scalar(p.s2_mu2, p.s2_S1, IFI, None, ALU.mult))
        _ser(v, ve, v.tensor_scalar(p.s2_tmp, p.s2_S2, IFI, None, ALU.mult))
        _ser(v, ve, v.tensor_tensor(p.s2_rstd2, p.s2_mu2, p.s2_mu2, ALU.mult))
        _ser(v, ve, v.tensor_tensor(p.s2_tmp, p.s2_tmp, p.s2_rstd2, ALU.subtract))
        v.tensor_scalar(p.s2_tmp, p.s2_tmp, EPS, None, ALU.add).then_inc(ve.h, 1)
        ve.inc(1)
    p.V(vf1)

    def sf1(s):
        s.wait_ge(ve.h, ve.n)
        s.activation(p.s2_rstd2, p.s2_tmp, AF.Sqrt).then_inc(se.h, 1)
        se.inc(1)
    p.S(sf1)

    def vf2(v):
        v.wait_ge(se.h, se.n)
        _ser(v, ve, v.reciprocal(p.s2_rstd2, p.s2_rstd2))
        _ser(v, ve, v.scalar_tensor_tensor(p.s2_tmp, p.s2_mu2, -2.0, p.s2_SG,
                                           ALU.mult, ALU.mult))
        _ser(v, ve, v.tensor_tensor(p.s2_tmp, p.s2_tmp, p.s2_SG2, ALU.add))
        _ser(v, ve, v.tensor_tensor(p.s2_r2, p.s2_mu2, p.s2_mu2, ALU.mult))
        _ser(v, ve, v.scalar_tensor_tensor(p.s2_tmp, p.s2_r2, p.qcols[:, 0:1],
                                           p.s2_tmp, ALU.mult, ALU.add))
        _ser(v, ve, v.tensor_tensor(p.s2_r2, p.s2_rstd2, p.s2_rstd2, ALU.mult))
        _ser(v, ve, v.tensor_tensor(p.s2_tmp, p.s2_tmp, p.s2_r2, ALU.mult))
        _ser(v, ve, v.scalar_tensor_tensor(p.s2_r2, p.s2_mu2, p.qcols[:, 1:2],
                                           p.s2_SB, ALU.mult, ALU.subtract))
        _ser(v, ve, v.tensor_tensor(p.s2_r2, p.s2_r2, p.s2_rstd2, ALU.mult))
        _ser(v, ve, v.scalar_tensor_tensor(p.s2_tmp, p.s2_r2, -2.0, p.s2_tmp,
                                           ALU.mult, ALU.add))
        _ser(v, ve, v.tensor_scalar(p.s2_tmp, p.s2_tmp, p.qcols[:, 2:3], None,
                                    ALU.add))
        v.tensor_scalar(p.s2_tmp, p.s2_tmp, 1.0 / F_INNER, EPS,
                        ALU.mult, ALU.add).then_inc(ve.h, 1)
        ve.inc(1)
    p.V(vf2)

    def sf2(s):
        s.wait_ge(ve.h, ve.n)
        s.activation(p.s2_tmp, p.s2_tmp, AF.Sqrt).then_inc(se.h, 1)
        se.inc(1)
    p.S(sf2)

    def vf3(v):
        v.wait_ge(se.h, se.n)
        v.reciprocal(p.s2_r2, p.s2_tmp).then_inc(ve.h, 1)
        ve.inc(1)
    p.V(vf3)

    for tt in range(2):
        _col2row(p, p.s2_rstd2[:, tt * 4:(tt + 1) * 4], 4, p.pk[6],
                 p.rowa)
        _col2row(p, p.s2_mu2[:, tt * 4:(tt + 1) * 4], 4, p.pk[6], p.rowc)

        def trb(t, tt=tt):
            t.wait_ge(dsp.h, dsp.n)
            t.matmul(p.pk[6][:, :], p.onesr, p.rowa, start=True, stop=True
                     ).then_inc(te.h, 1)
            te.inc(1)
        p.T(trb)

        def vrb(v, tt=tt):
            v.wait_ge(te.h, te.n)
            v.tensor_copy(p.RAb, p.pk[6][:, :])
            _ser(v, ve, v.tensor_tensor(p.rowd, p.rowc, p.rowa, ALU.mult))
            v.tensor_scalar(p.rowd, p.rowd, -1.0, None, ALU.mult
                            ).then_inc(ve.h, 1)
            ve.inc(1)
        p.V(vrb)

        def trc(t, tt=tt):
            t.wait_ge(ve.h, ve.n)
            t.matmul(p.pk[6][:, :], p.onesr, p.rowd, start=True, stop=True
                     ).then_inc(te.h, 1)
            te.inc(1)
        p.T(trc)

        def vrc(v, tt=tt):
            v.wait_ge(te.h, te.n)
            v.tensor_copy(p.RCb, p.pk[6][:, :])
            v.memset(p.s2_mxh[:, tt * 4:(tt + 1) * 4], 0.0).then_inc(ve.h, 1)
            ve.inc(1)
        p.V(vrc)

        vhn_sv = {}
        for fc in range(NFC):
            b = fc % 2
            sv = vhn_sv.get(fc - 2)

            def ldh(sc, fc=fc, tt=tt, b=b, sv=sv):
                if sv is None:
                    sc.wait_ge(ve.h, ve.n)
                else:
                    sc.wait_ge(ve.h, sv)
                sc.dma_start(out=p.h2st[:, b * 512:(b + 1) * 512],
                             in_=p.h2sp[fc, :, tt * 512:(tt + 1) * 512]
                             ).then_inc(dh.h, 16)
                dh.inc(16)
            p.S(ldh)

            def vhn(v, fc=fc, b=b):
                v.wait_ge(dh.h, dh.n)
                gc = p.lngs[:, fc:fc + 1]
                bc = p.lnbs[:, fc:fc + 1]
                v.scalar_tensor_tensor(p.pe2u, p.h2st[:, b * 512:(b + 1) * 512],
                                       gc, p.RAb, ALU.mult, ALU.mult)
                v.scalar_tensor_tensor(p.pe2u, p.RCb, gc, p.pe2u,
                                       ALU.mult, ALU.add)
                v.tensor_scalar(p.pe2u, p.pe2u, bc, None, ALU.add
                                ).then_inc(ve.h, 1)
                ve.inc(1)
            p.V(vhn)
            vhn_sv[fc] = ve.n

            for j in range(4):
                def tth(t, j=j):
                    if j == 0:
                        t.wait_ge(ve.h, ve.n)
                    t.transpose(p.pk[7][:, j * 128:(j + 1) * 128],
                                p.pe2u[:, j * 128:(j + 1) * 128],
                                p.identf).then_inc(te.h, 1)
                    te.inc(1)
                p.T(tth)

            def vmx(v, tt=tt, fc=fc):
                v.wait_ge(te.h, te.n)
                for j in range(3):
                    v.reduce_max(p.s2_tmp[:, j:j + 1],
                                 p.pk[7][:, j * 128:(j + 1) * 128], axis=AX.X,
                                 apply_absolute_value=True)
                _ser(v, ve, v.reduce_max(p.s2_tmp[:, 3:4],
                                         p.pk[7][:, 384:512], axis=AX.X,
                                         apply_absolute_value=True))
                for j in range(4):
                    col = tt * 4 + j
                    inst = v.tensor_tensor(p.s2_mxh[:, col:col + 1],
                                           p.s2_mxh[:, col:col + 1],
                                           p.s2_tmp[:, j:j + 1], ALU.max)
                inst.then_inc(ve.h, 1)
                ve.inc(1)
            p.V(vmx)

    if p.debug == 3:
        return
    def vf4(v):
        v.wait_ge(ve.h, ve.n)
        _ser(v, ve, v.reciprocal(p.s2_c2, p.s2_mxh))
        _ser(v, ve, v.tensor_scalar(p.s2_c2, p.s2_c2, 127.0, None, ALU.mult))
        _ser(v, ve, v.tensor_tensor(p.s2_dq2, p.s2_r2, p.s2_mxh, ALU.mult))
        v.tensor_scalar(p.s2_dq2, p.s2_dq2, 1.0 / 127.0, None, ALU.mult
                        ).then_inc(ve.h, 1)
        ve.inc(1)
    p.V(vf4)
    _col2row(p, p.s2_dq2, 8, p.pk[6], p.dq2row)

    for tt in range(2):
        _phase4c_tt(p, tt)
        if p.debug == 10 and tt == 0:
            return


def _phase4c_tt(p, tt):
    dw, dh, te, ve, se, dsp = (p.sem["dw"], p.sem["dh3"], p.sem["te"],
                               p.sem["ve"], p.sem["se"], p.sem["dsp"])
    dx, dst = p.sem["dx"], p.sem["dst"]

    def vr0(v):
        v.wait_ge(ve.h, ve.n)
        v.tensor_tensor(p.s2_tmp, p.s2_rstd2, p.s2_c2, ALU.mult
                        ).then_inc(ve.h, 1)
        ve.inc(1)
        v.wait_ge(ve.h, ve.n)
    p.V(vr0)
    _col2row(p, p.s2_tmp[:, tt * 4:(tt + 1) * 4], 4, p.pk[6], p.rowa)
    _col2row(p, p.s2_c2[:, tt * 4:(tt + 1) * 4], 4, p.pk[6], p.rowc)

    def vr1(v):
        v.wait_ge(dsp.h, dsp.n)
        _ser(v, ve, v.tensor_tensor(p.s2_tmp, p.s2_tmp, p.s2_mu2, ALU.mult))
        v.tensor_scalar(p.s2_tmp, p.s2_tmp, -1.0, None, ALU.mult
                        ).then_inc(ve.h, 1)
        ve.inc(1)
    p.V(vr1)
    _col2row(p, p.s2_tmp[:, tt * 4:(tt + 1) * 4], 4, p.pk[6], p.rowd)

    for nm, row, dstb in [("ra", "rowa", "RAb"), ("rb", "rowc", "RBb"),
                          ("rc", "rowd", "RCb")]:
        def tbb(t, row=row):
            t.wait_ge(dsp.h, dsp.n)
            t.wait_ge(ve.h, ve.n)
            t.matmul(p.pk[6][:, :], p.onesr, getattr(p, row), start=True,
                     stop=True).then_inc(te.h, 1)
            te.inc(1)
        p.T(tbb)

        def vbb(v, dstb=dstb):
            v.wait_ge(te.h, te.n)
            v.tensor_copy(getattr(p, dstb), p.pk[6][:, :]).then_inc(ve.h, 1)
            ve.inc(1)
        p.V(vbb)

    def tb2(t, tt=tt):
        t.wait_ge(ve.h, ve.n)
        t.matmul(p.pk[6][:, :], p.thr2row, p.dq2row[:, tt * 512:(tt + 1) * 512],
                 start=True, stop=True).then_inc(te.h, 1)
        te.inc(1)
    p.T(tb2)

    def vb2(v):
        v.wait_ge(te.h, te.n)
        v.tensor_copy(p.bc1[:, 0:512], p.pk[6][:, :]).then_inc(ve.h, 1)
        ve.inc(1)
    p.V(vb2)

    vq2_sv = {}
    for fc in range(NFC):
        b = fc % 2
        sv = vq2_sv.get(fc - 2)

        def ldh(sc, fc=fc, tt=tt, b=b, sv=sv):
            if sv is None:
                sc.wait_ge(ve.h, ve.n)
            else:
                sc.wait_ge(ve.h, sv)
            sc.dma_start(out=p.h2st[:, b * 512:(b + 1) * 512],
                         in_=p.h2sp[fc, :, tt * 512:(tt + 1) * 512]
                         ).then_inc(dh.h, 16)
            dh.inc(16)
        p.S(ldh)

        def vq2(v, fc=fc, b=b):
            v.wait_ge(dh.h, dh.n)
            gc = p.lngs[:, fc:fc + 1]
            bc = p.lnbs[:, fc:fc + 1]
            v.scalar_tensor_tensor(p.pe2u, p.h2st[:, b * 512:(b + 1) * 512],
                                   gc, p.RAb, ALU.mult, ALU.mult)
            v.scalar_tensor_tensor(p.pe2u, p.RCb, gc, p.pe2u, ALU.mult, ALU.add)
            v.scalar_tensor_tensor(p.pe2u, p.RBb, bc, p.pe2u, ALU.mult, ALU.add)
            v.tensor_scalar(p.pe2b, p.pe2u, 128.5, None, ALU.add)
            v.tensor_scalar(p.pe2c, p.pe2b, 128.0, -128.0, ALU.is_ge, ALU.mult)
            v.tensor_tensor(p.pe2u, p.pe2b, p.pe2c, ALU.add)
            for c in [64.0, 32.0, 16.0, 8.0, 4.0, 2.0, 1.0]:
                v.tensor_scalar(p.pe2c, p.pe2u, c, -c, ALU.is_ge, ALU.mult)
                v.tensor_tensor(p.pe2u, p.pe2u, p.pe2c, ALU.add)
            v.scalar_tensor_tensor(p.xq2T[:, fc * 512:(fc + 1) * 512],
                                   p.pe2b, 128.0, p.pe2u,
                                   ALU.subtract, ALU.subtract).then_inc(ve.h, 1)
            ve.inc(1)
        p.V(vq2)
        vq2_sv[fc] = ve.n

    tern2_snap = None
    for ds in range(NDC):
        def ldw2(sy, ds=ds, snap=tern2_snap):
            if snap is None:
                sy.wait_ge(se.h, se.n)
                sy.wait_ge(dst.h, dst.n)
            else:
                sy.wait_ge(se.h, snap)
            sy.dma_start(out=p.w2sl, in_=p.w2[ds * 128:(ds + 1) * 128, :]
                         ).then_inc(dw.h, 16)
            dw.inc(16)
        p.SY(ldw2)

        def sw2(s):
            s.wait_ge(dw.h, dw.n)
        p.S(sw2)
        for sub in range(4):
            _ternarize(p, p.w2sl[:, sub * D:(sub + 1) * D],
                       p.w2tn[:, sub * D:(sub + 1) * D], 1, D)
        tern2_snap = se.n
        pb16 = p.pk[6][:].bitcast(BF16)
        for fq in range(16):
            def ttr(t, fq=fq):
                t.wait_ge(ve.h, ve.n)
                for ff in range(4):
                    fc = fq * 4 + ff
                    t.transpose(pb16[:, ff * 128:(ff + 1) * 128],
                                p.w2tn[:, fc * 128:(fc + 1) * 128],
                                p.identb).then_inc(te.h, 1)
                    te.inc(1)
            p.T(ttr)

            def vev(v, fq=fq):
                v.wait_ge(te.h, te.n)
                v.tensor_copy(p.w2tT[:, fq * 512:(fq + 1) * 512],
                              pb16[:, 0:512]).then_inc(ve.h, 1)
                ve.inc(1)
            p.V(vev)

        bank = p.pk[7] if ds % 2 == 0 else p.pk[5]

        def tm2(t, ds=ds, bank=bank):
            t.wait_ge(ve.h, ve.n)
            if ds >= 2:
                t.wait_ge(se.h, se.n)
            for fc in range(NFC):
                mmi = t.matmul(bank[:, :],
                               p.w2tT[:, fc * 128:(fc + 1) * 128],
                               p.xq2T[:, fc * 512:(fc + 1) * 512],
                               start=(fc == 0), stop=(fc == NFC - 1))
                if fc == NFC - 1:
                    mmi.then_inc(te.h, 1)
            te.inc(1)
        p.T(tm2)

        def vdq2(v, ds=ds, bank=bank):
            v.wait_ge(te.h, te.n)
            v.wait_ge(se.h, se.n)
            v.scalar_tensor_tensor(p.pe2u, bank[:, :], 0.0, p.bc1[:, 0:512],
                                   ALU.add, ALU.mult).then_inc(ve.h, 1)
            ve.inc(1)
        p.V(vdq2)

        def sg2(s, ds=ds):
            s.wait_ge(ve.h, ve.n)
            s.activation(p.gkeep[:, ds * 512:(ds + 1) * 512], p.pe2u, AF.Silu,
                         bias=p.b2s[:, ds:ds + 1]).then_inc(se.h, 1)
            se.inc(1)
        p.S(sg2)

    for ds in range(NDC):
        def ldsT(sc, ds=ds, tt=tt):
            sc.wait_ge(ve.h, ve.n)
            sc.dma_start(out=p.sTr, in_=p.sTsp[ds, :, tt * 512:(tt + 1) * 512]
                         ).then_inc(p.sem["dw2"].h, 16)
            p.sem["dw2"].inc(16)
        p.S(ldsT)

        def vsg(v, ds=ds):
            v.wait_ge(p.sem["dw2"].h, p.sem["dw2"].n)
            v.wait_ge(se.h, se.n)
            v.tensor_tensor(p.xq2T[:, ds * 512:(ds + 1) * 512], p.sTr,
                            p.gkeep[:, ds * 512:(ds + 1) * 512],
                            ALU.mult).then_inc(ve.h, 1)
            ve.inc(1)
        p.V(vsg)

    for j in range(4):
        ot = tt * 4 + j

        def ldx(sy, ot=ot):
            sy.wait_ge(ve.h, ve.n)
            sy.dma_start(out=p.xrel,
                         in_=p.xin[(8 + ot) * 128:(9 + ot) * 128, :]
                         ).then_inc(dx.h, 16)
            dx.inc(16)
        p.SY(ldx)

        pb16g = p.pk[4][:].bitcast(BF16)
        for dq4 in range(4):
            def ttg(t, j=j, dq4=dq4):
                t.wait_ge(ve.h, ve.n)
                for dd in range(4):
                    ds = dq4 * 4 + dd
                    t.transpose(pb16g[:, dd * 128:(dd + 1) * 128],
                                p.xq2T[:, ds * 512 + j * 128:
                                       ds * 512 + (j + 1) * 128],
                                p.identb).then_inc(te.h, 1)
                    te.inc(1)
            p.T(ttg)

            def vout(v, dq4=dq4):
                v.wait_ge(te.h, te.n)
                v.wait_ge(dx.h, dx.n)
                if dq4 == 0:
                    v.wait_ge(dst.h, dst.n)
                for dd in range(4):
                    ds = dq4 * 4 + dd
                    v.tensor_tensor(p.ostg[:, ds * 128:(ds + 1) * 128],
                                    p.xrel[:, ds * 128:(ds + 1) * 128],
                                    pb16g[:, dd * 128:(dd + 1) * 128], ALU.add)
                v.memset(p.st_t0[:, 15:16], 0.0).then_inc(ve.h, 1)
                ve.inc(1)
            p.V(vout)

        def sto(s, ot=ot):
            s.wait_ge(ve.h, ve.n)
            s.dma_start(out=p.out[ot * 128:(ot + 1) * 128, :],
                        in_=p.ostg).then_inc(dst.h, 16)
            dst.inc(16)
        p.S(sto)



def _debug_dump(p):
    if not p.debug:
        return
    ge, ve, te = p.sem["ge"], p.sem["ve"], p.sem["te"]
    d = p.dbg

    def gdump(g):
        g.wait_ge(ve.h, ve.n)
        g.wait_ge(te.h, te.n)
        for s_ in ["dx", "dc", "dw", "dq", "dsp", "se", "dst", "dh2", "dh3",
                   "dw2"]:
            g.wait_ge(p.sem[s_].h, p.sem[s_].n)
        if p.debug == 2:
            for dst_ap, src in [(d["d_kh"], p.kh), (d["d_kl"], p.kl)]:
                g.dma_start(out=dst_ap.bitcast(BF16), in_=src).then_inc(ge.h, 16)
                ge.inc(16)
            g.dma_start(out=d["d_vtm"], in_=p.vtm).then_inc(ge.h, 16)
            ge.inc(16)
            g.dma_start(out=d["d_xq"].bitcast(BF16), in_=p.xqsp).then_inc(ge.h, 16)
            ge.inc(16)
        st_list = [p.st_mu, p.st_rstd, p.st_c1, p.st_dq1, p.st_mx, p.st_t1,
                   p.st_sx, p.st_sxx]
        for k, stt in enumerate(st_list):
            g.dma_start(out=d["d_st"][:, k * NT128:(k + 1) * NT128],
                        in_=stt).then_inc(ge.h, 16)
            ge.inc(16)
    p.G(gdump)


def _finish(p):
    dst, ge = p.sem["dst"], p.sem["ge"]

    def fin(g):
        g.wait_ge(dst.h, dst.n)
        g.wait_ge(ge.h, ge.n)
    p.G(fin)


def _host_consts(core):
    identf = np.eye(128, dtype=np.float32)
    identb = identf.astype(ml_dtypes.bfloat16)
    start = core * TOK
    attl = np.zeros((8, GROUP * NKT * 128), dtype=np.float64)
    attr = np.zeros((8, GROUP * 2 * 512), dtype=np.float64)
    attl[4:8] = 1.0
    for g in range(GROUP):
        s = SLOPE_G[g]
        for kt in range(NKT):
            kl = np.arange(128)
            kg = start - HALO + kt * 128 + kl
            a = s * (kt * 128 + kl - HALO).astype(np.float64)
            a[kg < 0] = NEG
            sp = _split_bf16(a, 4)
            for r in range(4):
                attl[r, (g * NKT + kt) * 128:(g * NKT + kt + 1) * 128] = \
                    sp[r].astype(np.float64)
        for qt in range(2):
            ql = np.arange(512)
            bv = -s * (qt * 512 + ql).astype(np.float64)
            sp = _split_bf16(bv, 4)
            col = (g * 2 + qt) * 512
            attr[0:4, col:col + 512] = 1.0
            for r in range(4):
                attr[4 + r, col:col + 512] = sp[r].astype(np.float64)
    maskb = np.zeros((128, 896), dtype=np.float64)
    for klp in range(128):
        maskb[klp, : klp + 384] = NEG
    return {
        "identf": identf, "identb": identb.view(np.uint16),
        "attl": attl.astype(ml_dtypes.bfloat16).view(np.uint16),
        "attr": attr.astype(ml_dtypes.bfloat16).view(np.uint16),
        "maskb": maskb.astype(ml_dtypes.bfloat16).view(np.uint16),
        "onesc": np.ones((128, 2), dtype=np.float32),
        "onesr": np.ones((1, 128), dtype=np.float32),
    }


def make_inmaps(inputs):
    x = np.ascontiguousarray(np.asarray(inputs["L0_attn_x"], dtype=np.float32)
                             .reshape(T_FULL, D))
    full = {
        "Wq": np.ascontiguousarray(inputs["L0_attn_Wq"], dtype=np.float32),
        "Wk": np.ascontiguousarray(inputs["L0_attn_Wk"], dtype=np.float32),
        "Wv": np.ascontiguousarray(inputs["L0_attn_Wv"], dtype=np.float32),
        "Wo": np.ascontiguousarray(inputs["L0_attn_Wo"], dtype=np.float32),
        "w1": np.ascontiguousarray(inputs["L0_ffn_w1"], dtype=np.float32),
        "b1": np.ascontiguousarray(inputs["L0_ffn_b1"], dtype=np.float32),
        "lng": np.ascontiguousarray(inputs["L0_ffn_ln_g"], dtype=np.float32),
        "lnb": np.ascontiguousarray(inputs["L0_ffn_ln_b"], dtype=np.float32),
        "w2": np.ascontiguousarray(inputs["L0_ffn_w2"], dtype=np.float32),
        "b2": np.ascontiguousarray(inputs["L0_ffn_b2"], dtype=np.float32),
    }
    maps = []
    for c in range(NCORES):
        lo = c * TOK - HALO
        if lo < 0:
            xs = np.concatenate([np.zeros((-lo, D), np.float32),
                                 x[: (c + 1) * TOK]], axis=0)
        else:
            xs = x[lo:(c + 1) * TOK]
        m = dict(full)
        m["xin"] = np.ascontiguousarray(xs)
        m.update(_host_consts(c))
        maps.append(m)
    return maps


def kernel(**inputs):
    nc = build_program(debug=0)
    maps = make_inmaps(inputs)
    res = run_bass_kernel_spmd(nc, maps, core_ids=list(range(NCORES)))
    out = np.concatenate([np.asarray(r["out"]) for r in res.results], axis=0)
    return out.reshape(1, T_FULL, D)


# version 09
# speedup vs baseline: 1.2125x; 1.0119x over previous
import math
import contextlib
import numpy as np
import ml_dtypes

import concourse.bass as bass
import concourse.mybir as mybir
from concourse.bass_utils import run_bass_kernel_spmd

F32 = mybir.dt.float32
F32R = mybir.dt.float32r
BF16 = mybir.dt.bfloat16
AF = mybir.ActivationFunctionType
ALU = mybir.AluOpType
AX = mybir.AxisListType
U16 = mybir.dt.uint16

T_FULL, D, NH, DH, NKV = 8192, 2048, 16, 128, 4
GROUP = NH // NKV
DKV = NKV * DH
F_INNER = 8192
NFC = F_INNER // 128
NDC = D // 128
EPS = 1e-5
NCORES = 8
TOK = T_FULL // NCORES
HALO = 1024
TT = TOK + HALO
NT128 = TT // 128
NOWN = TOK // 128
WINDOWS = [384, 512, 768, 1024]
NKT = TT // 128
NEG = -1.0e9
ISQD = 1.0 / math.sqrt(DH)
FSL = 512
NSL = F_INNER // FSL


def _alibi_slopes(n):
    start = 2 ** (-(2 ** (-(math.log2(n) - 3))))
    return [start * start ** i for i in range(n)]


SLOPE_G = _alibi_slopes(NH)[:GROUP]


def _split_bf16(a, n):
    a = np.asarray(a, dtype=np.float64)
    out, rem = [], a.copy()
    for _ in range(n):
        q = rem.astype(ml_dtypes.bfloat16)
        out.append(q)
        rem = rem - q.astype(np.float64)
    return out


class Sem:
    def __init__(self, h):
        self.h = h
        self.n = 0

    def inc(self, k=1):
        self.n += k
        return self.n


class MemMap:
    BASE = 17408
    LIMIT = 229344

    def __init__(self, nc):
        self.nc = nc
        self.regions = []

    def alloc(self, name, shape, dtype_, off, phases):
        off = off + self.BASE
        size = shape[1] * mybir.dt.size(dtype_)
        end = off + size
        assert end <= self.LIMIT, f"{name} exceeds SBUF: {end}"
        for (n2, o2, e2, ph2) in self.regions:
            if off < e2 and o2 < end and (phases & ph2):
                raise AssertionError(f"overlap {name}/{n2} in phases {phases & ph2}")
        self.regions.append((name, off, end, phases))
        return self.nc.alloc_sbuf_tensor_at(name, shape, dtype_, offset=off).ap()


class P:
    pass


def _ser(v, ve, inst):
    inst.then_inc(ve.h, 1)
    ve.inc(1)
    v.wait_ge(ve.h, ve.n)


PH1, PH2, PH3, PH4A, PH4B, PH4C, PH4D = 1, 2, 4, 8, 16, 32, 64
ALL = 127


def build_program(debug=0):
    nc = bass.Bass("TRN2", target_bir_lowering=False,
                   detect_race_conditions=False)
    p = P()
    p.nc = nc
    p.debug = debug
    dt = nc.dram_tensor

    p.xin = dt("xin", [TT, D], F32, kind="ExternalInput").ap()
    p.Wq = dt("Wq", [D, D], F32, kind="ExternalInput").ap()
    p.Wk = dt("Wk", [D, DKV], F32, kind="ExternalInput").ap()
    p.Wv = dt("Wv", [D, DKV], F32, kind="ExternalInput").ap()
    p.Wo = dt("Wo", [D, D], F32, kind="ExternalInput").ap()
    p.w1 = dt("w1", [F_INNER, D], F32, kind="ExternalInput").ap()
    p.b1 = dt("b1", [F_INNER], F32, kind="ExternalInput").ap()
    p.lng = dt("lng", [F_INNER], F32, kind="ExternalInput").ap()
    p.lnb = dt("lnb", [F_INNER], F32, kind="ExternalInput").ap()
    p.w2 = dt("w2", [D, F_INNER], F32, kind="ExternalInput").ap()
    p.b2 = dt("b2", [D], F32, kind="ExternalInput").ap()
    p.identf_d = dt("identf", [128, 128], F32, kind="ExternalInput").ap()
    p.identb_d = dt("identb", [128, 128], U16, kind="ExternalInput").ap().bitcast(BF16)
    p.attl_d = dt("attl", [8, GROUP * NKT * 128], U16, kind="ExternalInput").ap().bitcast(BF16)
    p.attr_d = dt("attr", [8, GROUP * 2 * 512], U16, kind="ExternalInput").ap().bitcast(BF16)
    p.maskb_d = dt("maskb", [128, 896], U16, kind="ExternalInput").ap().bitcast(BF16)
    p.onesc_d = dt("onesc", [128, 2], F32, kind="ExternalInput").ap()
    p.onesr_d = dt("onesr", [1, 128], F32, kind="ExternalInput").ap()
    p.out = dt("out", [TOK, D], F32, kind="ExternalOutput").ap()
    p.xqsp = dt("xqspill", [NOWN, 128, NDC * 128], BF16).ap()
    p.h2sp = dt("h2spill", [NFC, 128, TOK], F32).ap()
    p.sTsp = dt("sTspill", [NDC, 128, TOK], BF16).ap()

    p.dbg = {}
    if debug:
        for name, shape, dtp in [
            ("d_kh", [128, NKV * TT], U16), ("d_kl", [128, NKV * TT], U16),
            ("d_vtm", [128, NKV * NKT * 128], F32),
            ("d_xq", [NOWN, 128, NDC * 128], U16),
            ("d_st", [128, NT128 * 8], F32),
            ("d_sT", [NDC, 128, TOK], U16),
            ("d_den", [1, 2 * NH * 512], F32),
            ("d_h2", [NFC, 128, TOK], F32),
            ("d_st2", [128, 8 * 8], F32),
            ("d_gT", [128, NDC * TOK], F32),
        ]:
            p.dbg[name] = dt(name, shape, dtp, kind="ExternalOutput").ap()

    es = contextlib.ExitStack()
    mm = MemMap(nc)
    p.mm = mm

    p.kh = mm.alloc("kh", [128, NKV * TT], BF16, 0, PH2 | PH3)
    p.kl = mm.alloc("kl", [128, NKV * TT], BF16, 16384, PH2 | PH3)
    p.vtm = mm.alloc("vtm", [128, NKV * NKT * 128], F32, 32768, PH2 | PH3)
    p.identf = mm.alloc("identf_s", [128, 128], F32, 65536, ALL)
    p.identb = mm.alloc("identb_s", [128, 128], BF16, 66048, ALL)
    p.maskb = mm.alloc("maskb_s", [128, 896], BF16, 66304, ALL)
    p.onesf = mm.alloc("onesf", [128, 2], F32, 68096, ALL)
    p.onesr = mm.alloc("onesr_s", [1, 128], F32, 68128, ALL)
    p.b1s = mm.alloc("b1s", [128, 64], F32, 68640, ALL)
    p.lngs = mm.alloc("lngs", [128, 64], F32, 68896, ALL)
    p.lnbs = mm.alloc("lnbs", [128, 64], F32, 69152, ALL)
    p.b2s = mm.alloc("b2s", [128, 16], F32, 69408, ALL)
    stoff = 69472
    for nm in ["mu", "rstd", "mx", "mn", "sx", "sxx", "c1", "dq1", "t0", "t1"]:
        setattr(p, "st_" + nm, mm.alloc("st_" + nm, [128, NT128], F32, stoff, ALL))
        stoff += 64
    st2off = 70112
    for nm in ["S1", "SG", "SB", "S2", "SG2", "mu2", "rstd2", "r2", "mxh",
               "c2", "dq2", "tmp"]:
        setattr(p, "s2_" + nm, mm.alloc("s2_" + nm, [128, NOWN], F32, st2off, ALL))
        st2off += 32
    p.wstA = mm.alloc("wstA", [128, 192], F32R, 70496, ALL)
    p.wstB = mm.alloc("wstB", [128, 128], F32R, 71264, ALL)
    p.qcols = mm.alloc("qcols", [128, 8], F32, 71776, ALL)
    p.denrow = mm.alloc("denrow", [1, 512], F32, 71808, ALL)
    p.recrow = mm.alloc("recrow", [1, 512], F32, 73856, ALL)
    p.thrc = mm.alloc("thrc", [128, 2], F32, 75904, ALL)
    p.thrn = mm.alloc("thrn", [128, 2], F32, 75936, ALL)
    p.wacc = mm.alloc("wacc", [128, 128], F32, 75968, ALL)
    p.attl = mm.alloc("attl_s", [8, GROUP * NKT * 128], BF16, 76480, PH1 | PH2 | PH3)
    p.attr = mm.alloc("attr_s", [8, GROUP * 2 * 512], BF16, 92864, PH1 | PH2 | PH3)
    PH4ALL = PH4A | PH4B | PH4C | PH4D
    p.rows4 = mm.alloc("rows4", [1, 3968], F32, 76480, PH4ALL)
    p.xnT = mm.alloc("xnT", [128, NDC * 512], F32, 101056, PH2 | PH3)
    p.wbuf = mm.alloc("wbuf", [128, NDC * 512], F32, 133824, PH2)
    p.wq0 = mm.alloc("wq0", [128, D], F32, 133824, PH3)
    p.wq1 = mm.alloc("wq1", [128, D], F32, 142016, PH3)
    p.qhb = mm.alloc("qhb", [128, 512], BF16, 150208, PH3)
    p.qlb = mm.alloc("qlb", [128, 512], BF16, 151232, PH3)
    p.rbc = mm.alloc("rbc", [128, 512], F32, 152256, PH3)
    p.pbuf = mm.alloc("pbuf", [128, 2 * 512], F32, 154304, PH3)
    p.sTst = mm.alloc("sTst", [128, 2 * 512], BF16, 158400, PH3)
    p.wab = mm.alloc("wab", [128, D], F32, 199360, PH3)
    p.xt0 = mm.alloc("xt0", [128, D], F32, 166592, PH1 | PH2)
    p.xnt = mm.alloc("xnt", [128, D], F32, 174784, PH1 | PH2)
    p.scrb = mm.alloc("scrb", [128, D], BF16, 182976, PH1 | PH2)
    p.scrP = mm.alloc("scrP", [128, 1024], F32, 187072, PH1 | PH2)
    p.puP = mm.alloc("puP", [128, 1024], F32, 191168, PH1 | PH2)
    p.pbP = mm.alloc("pbP", [128, 1024], F32, 195264, PH1 | PH2)
    p.xqst = mm.alloc("xqst", [128, D], BF16, 199360, PH1 | PH2)
    p.ysb = mm.alloc("ysb", [128, NH * 512], F32, 166592, PH3)
    p.xqTr = mm.alloc("xqTr", [128, NDC * TOK], BF16, 32768, PH4A)
    p.wtern = mm.alloc("wtern", [128, 4 * D], BF16, 101056, PH4A)
    p.wtern2 = mm.alloc("wtern2", [128, 4 * D], BF16, 0, PH4A)
    p.tmpb2 = mm.alloc("tmpb2", [128, D], BF16, 16384, PH4A)
    p.wqT = mm.alloc("wqT", [128, NDC * FSL], BF16, 117440, PH4A)
    p.wfsl = mm.alloc("wfsl", [128, 4 * D], F32, 133824, PH4A)
    p.hbuf = mm.alloc("hbuf", [128, 8 * 512], F32, 166592, PH4A)
    p.tmpb = mm.alloc("tmpb", [128, D], BF16, 182976, PH4A | PH4C)
    p.bc1 = mm.alloc("bc1", [128, 2 * 512], F32, 191168, PH4A | PH4C)
    p.h2st = mm.alloc("h2st", [128, 1024], F32, 187072, PH4B | PH4C)
    p.pe2u = mm.alloc("pe2u", [128, 512], F32, 195264, PH4B | PH4C)
    p.pe2b = mm.alloc("pe2b", [128, 512], F32, 197312, PH4B | PH4C)
    p.pe2c = mm.alloc("pe2c", [128, 512], F32, 199360,
                      PH4A | PH4B | PH4C)
    p.RAb = mm.alloc("RAb", [128, 512], F32, 201408, PH4B | PH4C)
    p.RCb = mm.alloc("RCb", [128, 512], F32, 203456, PH4B | PH4C)
    p.RBb = mm.alloc("RBb", [128, 512], F32, 205504, PH4B | PH4C)
    p.statu = mm.alloc("statu", [128, 512], F32R, 207552, PH4B)
    p.statq = mm.alloc("statq", [128, 512], F32R, 209600, PH4B)
    p.w2sl = mm.alloc("w2sl", [128, F_INNER], F32, 0, PH4C)
    p.w2tn = mm.alloc("w2tn", [128, F_INNER], BF16, 32768, PH4C)
    p.w2tT = mm.alloc("w2tT", [128, F_INNER], BF16, 49152, PH4C)
    p.xq2T = mm.alloc("xq2T", [128, NFC * 512], BF16, 101056, PH4C)
    p.gkeep = mm.alloc("gkeep", [128, NDC * 512], BF16, 166592, PH4C)
    p.ostg = mm.alloc("ostg", [128, D], F32, 8192, PH4D)
    p.xrelA = mm.alloc("xrelA", [128, D], F32, 201408, 0)
    p.xrelB = mm.alloc("xrelB", [128, D], F32, 187072, 0)
    p.sTrA = mm.alloc("sTrA", [128, 512], BF16, 16384, PH4D)
    p.sTrB = mm.alloc("sTrB", [128, 512], BF16, 195264, 0)
    p.dq1row = p.rows4[:, 0:1024]
    p.dq2row = p.rows4[:, 1024:2048]
    p.rowa = p.rows4[:, 2048:2560]
    p.rowc = p.rows4[:, 2560:3072]
    p.rowd = p.rows4[:, 3072:3584]
    p.thr1row = p.rows4[:, 3584:3712]
    p.thr2row = p.rows4[:, 3712:3840]

    p.pk = [es.enter_context(nc.psum_tensor(f"pk{i}", [128, 512], F32))
            for i in range(8)]

    p.sem = {}
    for name in ["dx", "dc", "dw", "dq", "dsp", "dst", "te", "ve", "se", "ge",
                 "dh2", "dh3", "dw2", "dwm"]:
        p.sem[name] = Sem(es.enter_context(nc.semaphore(name)))

    p.block = es.enter_context(nc.Block())
    p.T = p.block.tensor
    p.V = p.block.vector
    p.S = p.block.scalar
    p.G = p.block.gpsimd
    p.SY = p.block.sync
    p.wqslot_snap = [None, None]
    p.sspill_snap = {}
    p.wab_guard = None
    p.wmean_snap = None

    _phase0(p)
    _phase1(p)
    if debug == 4:
        _debug_dump(p)
        _finish(p)
        es.close()
        nc._p = p
        return nc
    _phase2(p, groups=(0, 1, 2), peel=True)
    if debug == 5:
        _phase3(p, qt=0)
    elif debug == 6:
        _phase3(p, qt=0)
        p.wab_guard = p.sem["se"].n
        _phase2(p, groups=(3,), peel=True)
    elif debug == 7:
        _phase3(p, qt=0)
        p.wab_guard = p.sem["se"].n
        _phase2(p, groups=(3,), peel=True)
        _phase3(p, qt=1)
    elif debug != 2:
        _phase3(p, qt=0)
        p.wab_guard = p.sem["se"].n
        _phase2(p, groups=(3,), peel=True)
        _phase3(p, qt=1)
        _phase4(p)
    else:
        _phase2(p, groups=(3,), peel=True)
    _debug_dump(p)
    _finish(p)
    es.close()
    nc._p = p
    return nc


def _phase0(p):
    dc = p.sem["dc"]

    def loads(sy):
        for dst_, src in [(p.identf, p.identf_d), (p.identb, p.identb_d),
                          (p.attl, p.attl_d), (p.attr, p.attr_d),
                          (p.maskb, p.maskb_d), (p.onesf, p.onesc_d),
                          (p.onesr, p.onesr_d)]:
            sy.dma_start(out=dst_, in_=src).then_inc(dc.h, 16)
            dc.inc(16)
        for dst_, src in [(p.b1s, p.b1), (p.lngs, p.lng), (p.lnbs, p.lnb)]:
            sy.dma_start(out=dst_, in_=src.rearrange("(c p) -> p c", p=128),
                         allow_slow_non_contiguous=True).then_inc(dc.h, 16)
            dc.inc(16)
        sy.dma_start(out=p.b2s, in_=p.b2.rearrange("(c p) -> p c", p=128),
                     allow_slow_non_contiguous=True).then_inc(dc.h, 16)
        dc.inc(16)
    p.SY(loads)


def _phase1(p):
    dx, ve, se = p.sem["dx"], p.sem["ve"], p.sem["se"]
    for i in range(NT128):
        def load(sy, i=i):
            if i > 0:
                sy.wait_ge(se.h, se.n)
                sy.wait_ge(ve.h, ve.n)
            sy.dma_start(out=p.xt0, in_=p.xin[i * 128:(i + 1) * 128, :]
                         ).then_inc(dx.h, 16)
            dx.inc(16)
        p.SY(load)

        def scal(s, i=i):
            s.wait_ge(dx.h, dx.n)
            s.activation(p.scrb, p.xt0, AF.Square, accum_out=p.st_sxx[:, i:i + 1])
            s.activation(p.xqst, p.xt0, AF.Copy,
                         accum_out=p.st_sx[:, i:i + 1]).then_inc(se.h, 1)
            se.inc(1)
        p.S(scal)

        def vec(v, i=i):
            v.wait_ge(dx.h, dx.n)
            v.reduce_max(p.st_mx[:, i:i + 1], p.xt0, axis=AX.X)
            v.tensor_reduce(p.st_mn[:, i:i + 1], p.xt0, axis=AX.X,
                            op=ALU.min).then_inc(ve.h, 1)
            ve.inc(1)
        p.V(vec)

    def vst(v):
        v.wait_ge(se.h, se.n)
        inv_d = 1.0 / D
        _ser(v, ve, v.tensor_scalar(p.st_mu, p.st_sx, inv_d, None, ALU.mult))
        _ser(v, ve, v.tensor_scalar(p.st_t0, p.st_sxx, inv_d, None, ALU.mult))
        _ser(v, ve, v.tensor_tensor(p.st_t1, p.st_mu, p.st_mu, ALU.mult))
        _ser(v, ve, v.tensor_tensor(p.st_t0, p.st_t0, p.st_t1, ALU.subtract))
        v.tensor_scalar(p.st_t1, p.st_t0, EPS, None, ALU.add).then_inc(ve.h, 1)
        ve.inc(1)
    p.V(vst)

    def sst(s):
        s.wait_ge(ve.h, ve.n)
        s.activation(p.st_rstd, p.st_t1, AF.Sqrt).then_inc(se.h, 1)
        se.inc(1)
    p.S(sst)

    def vst2(v):
        v.wait_ge(se.h, se.n)
        _ser(v, ve, v.reciprocal(p.st_rstd, p.st_rstd))
        _ser(v, ve, v.tensor_tensor(p.st_t1, p.st_rstd, p.st_rstd, ALU.mult))
        _ser(v, ve, v.tensor_tensor(p.st_t1, p.st_t1, p.st_t0, ALU.mult))
        v.tensor_scalar(p.st_t1, p.st_t1, EPS, None, ALU.add).then_inc(ve.h, 1)
        ve.inc(1)
    p.V(vst2)

    def sst2(s):
        s.wait_ge(ve.h, ve.n)
        s.activation(p.st_t1, p.st_t1, AF.Sqrt).then_inc(se.h, 1)
        se.inc(1)
    p.S(sst2)

    def vst3(v):
        v.wait_ge(se.h, se.n)
        _ser(v, ve, v.reciprocal(p.st_t1, p.st_t1))
        _ser(v, ve, v.tensor_tensor(p.st_mx, p.st_mx, p.st_mu, ALU.subtract))
        _ser(v, ve, v.tensor_tensor(p.st_mn, p.st_mu, p.st_mn, ALU.subtract))
        _ser(v, ve, v.tensor_tensor(p.st_mx, p.st_mx, p.st_mn, ALU.max))
        _ser(v, ve, v.tensor_tensor(p.st_mx, p.st_mx, p.st_rstd, ALU.mult))
        _ser(v, ve, v.reciprocal(p.st_c1, p.st_mx))
        _ser(v, ve, v.tensor_scalar(p.st_c1, p.st_c1, 127.0, None, ALU.mult))
        _ser(v, ve, v.tensor_tensor(p.st_dq1, p.st_t1, p.st_mx, ALU.mult))
        v.tensor_scalar(p.st_dq1, p.st_dq1, 1.0 / 127.0, None,
                        ALU.mult).then_inc(ve.h, 1)
        ve.inc(1)
    p.V(vst3)


def _phase2(p, groups, peel):
    dx, dw, dsp = p.sem["dx"], p.sem["dw"], p.sem["dsp"]
    te, ve, se, dc = p.sem["te"], p.sem["ve"], p.sem["se"], p.sem["dc"]

    def wc(t):
        t.wait_ge(dc.h, dc.n)
    p.T(wc)

    for g4 in groups:
        own = g4 >= 2
        for j in range(4):
            i = g4 * 4 + j

            def load(sy, i=i):
                sy.wait_ge(ve.h, ve.n)
                sy.dma_start(out=p.xt0, in_=p.xin[i * 128:(i + 1) * 128, :]
                             ).then_inc(dx.h, 16)
                dx.inc(16)
            p.SY(load)

            def vxn(v, i=i):
                v.wait_ge(dx.h, dx.n)
                v.tensor_scalar(p.xnt, p.xt0, p.st_mu[:, i:i + 1],
                                p.st_rstd[:, i:i + 1], ALU.subtract,
                                ALU.mult).then_inc(ve.h, 1)
                ve.inc(1)
            p.V(vxn)

            for cq in range(4):
                def ttr(t, cq=cq):
                    t.wait_ge(ve.h, ve.n)
                    for cc in range(4):
                        c = cq * 4 + cc
                        t.transpose(p.pk[cq % 2][:, cc * 128:(cc + 1) * 128],
                                    p.xnt[:, c * 128:(c + 1) * 128],
                                    p.identf).then_inc(te.h, 1)
                        te.inc(1)
                p.T(ttr)

                def vev(v, cq=cq, j=j):
                    v.wait_ge(te.h, te.n)
                    for cc in range(4):
                        c = cq * 4 + cc
                        v.tensor_copy(
                            p.xnT[:, c * 512 + j * 128: c * 512 + (j + 1) * 128],
                            p.pk[cq % 2][:, cc * 128:(cc + 1) * 128])
                    v.memset(p.st_t0[:, 15:16], 0.0).then_inc(ve.h, 1)
                    ve.inc(1)
                p.V(vev)

            if own and peel:
                _peel1(p, i)

        def ldwk(sy, g4=g4):
            sy.wait_ge(te.h, te.n)
            sy.wait_ge(p.sem["dh3"].h, p.sem["dh3"].n)
            sy.dma_start(out=p.wbuf.rearrange("p (c n) -> p c n", c=NDC),
                         in_=p.Wk.rearrange("(c p) n -> p c n", p=128)
                         ).then_inc(dw.h, 16)
            dw.inc(16)
        p.SY(ldwk)

        for kv in range(NKV):
            def tk(t, kv=kv):
                t.wait_ge(dw.h, dw.n)
                t.wait_ge(ve.h, ve.n)
                for c in range(NDC):
                    mmi = t.matmul(p.pk[2][:, :],
                                   p.wbuf[:, c * 512 + kv * 128:
                                          c * 512 + (kv + 1) * 128],
                                   p.xnT[:, c * 512:(c + 1) * 512],
                                   start=(c == 0), stop=(c == NDC - 1))
                    if c == NDC - 1:
                        mmi.then_inc(te.h, 1)
                te.inc(1)
            p.T(tk)

            def vk(v, kv=kv, g4=g4):
                v.wait_ge(te.h, te.n)
                dsth = p.kh[:, kv * TT + g4 * 512: kv * TT + (g4 + 1) * 512]
                dstl = p.kl[:, kv * TT + g4 * 512: kv * TT + (g4 + 1) * 512]
                _ser(v, ve, v.tensor_copy(dsth, p.pk[2][:, :]))
                v.tensor_tensor(dstl, p.pk[2][:, :], dsth,
                                ALU.subtract).then_inc(ve.h, 1)
                ve.inc(1)
            p.V(vk)

        def ldwv(sy):
            sy.wait_ge(ve.h, ve.n)
            sy.dma_start(out=p.wbuf.rearrange("p (c n) -> p c n", c=NDC),
                         in_=p.Wv.rearrange("(c p) n -> p c n", p=128)
                         ).then_inc(dw.h, 16)
            dw.inc(16)
        p.SY(ldwv)

        for j in range(4):
            kt = g4 * 4 + j

            def tv(t, j=j):
                t.wait_ge(dw.h, dw.n)
                t.wait_ge(ve.h, ve.n)
                for c in range(NDC):
                    mmi = t.matmul(p.pk[3][:, :],
                                   p.xnT[:, c * 512 + j * 128:
                                         c * 512 + (j + 1) * 128],
                                   p.wbuf[:, c * 512:(c + 1) * 512],
                                   start=(c == 0), stop=(c == NDC - 1))
                    if c == NDC - 1:
                        mmi.then_inc(te.h, 1)
                te.inc(1)
            p.T(tv)

            def vv(v, kt=kt):
                v.wait_ge(te.h, te.n)
                for kvh in range(NKV):
                    v.tensor_copy(
                        p.vtm[:, (kt * NKV + kvh) * 128:(kt * NKV + kvh + 1) * 128],
                        p.pk[3][:, kvh * 128:(kvh + 1) * 128])
                v.memset(p.st_t0[:, 15:16], 0.0).then_inc(ve.h, 1)
                ve.inc(1)
            p.V(vv)
    p.wqslot_snap = [None, None]


def _peel1(p, i):
    ve, te, dsp = p.sem["ve"], p.sem["te"], p.sem["dsp"]
    ot = i - 8

    def vpeel(v, i=i):
        for hh in range(2):
            sl = slice(hh * 1024, (hh + 1) * 1024)
            v.tensor_scalar(p.puP, p.xnt[:, sl], p.st_c1[:, i:i + 1], 128.5,
                            ALU.mult, ALU.add)
            v.tensor_scalar(p.scrP, p.puP, 128.0, -128.0, ALU.is_ge, ALU.mult)
            v.tensor_tensor(p.pbP, p.puP, p.scrP, ALU.add)
            for c in [64.0, 32.0, 16.0, 8.0, 4.0, 2.0, 1.0]:
                v.tensor_scalar(p.scrP, p.pbP, c, -c, ALU.is_ge, ALU.mult)
                v.tensor_tensor(p.pbP, p.pbP, p.scrP, ALU.add)
            v.scalar_tensor_tensor(p.scrb[:, sl], p.puP, 128.0, p.pbP,
                                   ALU.subtract, ALU.subtract)
        v.memset(p.st_t0[:, 15:16], 0.0).then_inc(ve.h, 1)
        ve.inc(1)
    p.V(vpeel)

    pb16 = p.pk[5][:].bitcast(BF16)
    for cq in range(4):
        def ttr(t, cq=cq):
            t.wait_ge(ve.h, ve.n)
            for cc in range(4):
                c = cq * 4 + cc
                t.transpose(pb16[:, cc * 128:(cc + 1) * 128],
                            p.scrb[:, c * 128:(c + 1) * 128],
                            p.identb).then_inc(te.h, 1)
                te.inc(1)
        p.T(ttr)

        def vev(v, cq=cq):
            v.wait_ge(te.h, te.n)
            if cq == 0:
                v.wait_ge(p.sem["dh3"].h, p.sem["dh3"].n)
                if p.wab_guard is not None:
                    v.wait_ge(p.sem["se"].h, p.wab_guard)
            for cc in range(4):
                c = cq * 4 + cc
                v.tensor_copy(p.xqst[:, c * 128:(c + 1) * 128],
                              pb16[:, cc * 128:(cc + 1) * 128])
            v.memset(p.st_t0[:, 15:16], 0.0).then_inc(ve.h, 1)
            ve.inc(1)
        p.V(vev)

    def spill(s, ot=ot):
        s.wait_ge(ve.h, ve.n)
        s.dma_start(out=p.xqsp[ot, :, :], in_=p.xqst
                    ).then_inc(p.sem["dh3"].h, 16)
        p.sem["dh3"].inc(16)
    p.S(spill)


def _wmean_chunks(p, h, qt):
    se, dwm, dh3 = p.sem["se"], p.sem["dwm"], p.sem["dh3"]
    for i in range(4):
        g = qt * 64 + h * 4 + i
        if g < 64:
            src = p.w1.rearrange("(c p) d -> p c d", p=128)[:, g, :]
        else:
            j = g - 64
            src = p.w2.rearrange("(c p) f -> p c f", p=128)[
                :, j // 4, (j % 4) * D:(j % 4 + 1) * D]

        def ldc(sy, src=src, g=g, snap=p.wmean_snap):
            if g == 0 or g == 64:
                sy.wait_ge(dh3.h, dh3.n)
            if snap is not None:
                sy.wait_ge(se.h, snap)
            sy.dma_start(out=p.wab, in_=src).then_inc(dwm.h, 16)
            dwm.inc(16)
        p.SY(ldc)

        def sabs(s, g=g):
            s.wait_ge(dwm.h, dwm.n)
            s.activation(p.wab, p.wab, AF.Abs,
                         accum_out=p.wacc[:, g:g + 1]).then_inc(se.h, 1)
            se.inc(1)
        p.S(sabs)
        p.wmean_snap = se.n


def _phase3(p, qt):
    dw, te, ve, se, dsp = (p.sem["dw"], p.sem["te"], p.sem["ve"], p.sem["se"],
                           p.sem["dsp"])
    dh3 = p.sem["dh3"]
    p.sspill_snap = {}
    Qloc = HALO + qt * 512
    qtile = Qloc // 128

    for h in range(NH):
        g = h // GROUP
        kv = h // GROUP
        W = WINDOWS[g]
        kt0 = (Qloc - W) // 128
        kt1 = Qloc // 128 + 4
        kts = [kt for kt in range(kt0, kt1) if kt >= 0]

        wslot = p.wq0 if h % 2 == 0 else p.wq1

        def ldwq(sy, h=h, wslot=wslot, snap=p.wqslot_snap[h % 2]):
            if snap is None:
                sy.wait_ge(te.h, te.n)
            else:
                sy.wait_ge(te.h, snap)
            sy.dma_start(out=wslot.rearrange("p (c n) -> p c n", c=NDC),
                         in_=p.Wq[:, h * 128:(h + 1) * 128].rearrange(
                             "(c p) n -> p c n", p=128)).then_inc(dw.h, 16)
            dw.inc(16)
        p.SY(ldwq)

        def tq(t, wslot=wslot):
            t.wait_ge(dw.h, dw.n)
            t.wait_ge(ve.h, ve.n)
            for c in range(NDC):
                mmi = t.matmul(p.pk[4][:, :],
                               wslot[:, c * 128:(c + 1) * 128],
                               p.xnT[:, c * 512:(c + 1) * 512],
                               start=(c == 0), stop=(c == NDC - 1))
                if c == NDC - 1:
                    mmi.then_inc(te.h, 1)
            te.inc(1)
        p.T(tq)
        p.wqslot_snap[h % 2] = te.n

        def vq(v):
            v.wait_ge(te.h, te.n)
            _ser(v, ve, v.tensor_scalar(p.pbuf[:, 0:512], p.pk[4][:, :], ISQD,
                                        None, ALU.mult))
            _ser(v, ve, v.tensor_copy(p.qhb, p.pbuf[:, 0:512]))
            v.tensor_tensor(p.qlb, p.pbuf[:, 0:512], p.qhb,
                            ALU.subtract).then_inc(ve.h, 1)
            ve.inc(1)
        p.V(vq)

        nv = len(kts)
        for vi, kt in enumerate(kts):
            sb_ = p.pk[vi % 2]
            diag = kt * 128 >= Qloc

            def tsc(t, kt=kt, vi=vi, sb_=sb_, diag=diag, g=g, kv=kv, h=h,
                    qt=qt, nv=nv):
                if vi == 0:
                    t.wait_ge(ve.h, ve.n)
                if vi >= 2:
                    t.wait_ge(se.h, se.n - 1)
                ksl = slice(kv * TT + kt * 128, kv * TT + (kt + 1) * 128)
                t.matmul(sb_[:, :], p.kh[:, ksl], p.qhb, start=True, stop=False)
                t.matmul(sb_[:, :], p.kh[:, ksl], p.qlb, start=False, stop=False)
                t.matmul(sb_[:, :], p.kl[:, ksl], p.qhb, start=False, stop=False)
                la = p.attl[:, (g * NKT + kt) * 128:(g * NKT + kt + 1) * 128]
                ra = p.attr[:, (g * 2 + qt) * 512:(g * 2 + qt + 1) * 512]
                if diag:
                    t.matmul(sb_[:, :], la, ra, start=False, stop=False)
                    joff = (kt * 128 - Qloc) // 128
                    msl = p.maskb[:, 384 - 128 * joff: 384 - 128 * joff + 512]
                    t.matmul(sb_[:, :], p.identb, msl, start=False,
                             stop=True).then_inc(te.h, 1)
                else:
                    t.matmul(sb_[:, :], la, ra, start=False,
                             stop=True).then_inc(te.h, 1)
                te.inc(1)
            p.T(tsc)

            def sexp(s, vi=vi, sb_=sb_):
                s.wait_ge(te.h, te.n)
                s.activation(p.pbuf[:, (vi % 2) * 512:(vi % 2 + 1) * 512],
                             sb_[:, :], AF.Exp).then_inc(se.h, 1)
                se.inc(1)
            p.S(sexp)

            def tav(t, vi=vi, kt=kt, kv=kv, nv=nv):
                t.wait_ge(se.h, se.n)
                pb = p.pbuf[:, (vi % 2) * 512:(vi % 2 + 1) * 512]
                vsl = p.vtm[:, (kt * NKV + kv) * 128:(kt * NKV + kv + 1) * 128]
                t.matmul(p.pk[2][:, :], vsl, pb,
                         start=(vi == 0), stop=(vi == nv - 1))
                mmi = t.matmul(p.pk[3][0:1, :], p.onesf[:, 0:1], pb,
                               start=(vi == 0), stop=(vi == nv - 1))
                mmi.then_inc(te.h, 1)
                te.inc(1)
            p.T(tav)

        def vden(v):
            v.wait_ge(te.h, te.n)
            _ser(v, ve, v.tensor_copy(p.denrow, p.pk[3][0:1, :]))
            v.reciprocal(p.recrow, p.denrow).then_inc(ve.h, 1)
            ve.inc(1)
        p.V(vden)

        def trb(t):
            t.wait_ge(ve.h, ve.n)
            t.matmul(p.pk[5][:, :], p.onesr, p.recrow, start=True,
                     stop=True).then_inc(te.h, 1)
            te.inc(1)
        p.T(trb)

        def vyev(v, h=h):
            v.wait_ge(te.h, te.n)
            v.tensor_copy(p.rbc, p.pk[5][:, :])
            v.scalar_tensor_tensor(p.ysb[:, h * 512:(h + 1) * 512],
                                   p.pk[2][:, :], 0.0, p.rbc,
                                   ALU.add, ALU.mult).then_inc(ve.h, 1)
            ve.inc(1)
        p.V(vyev)

        _wmean_chunks(p, h, qt)

        if p.debug and qt == 0:
            def dden(g_, h=h):
                g_.wait_ge(ve.h, ve.n)
                g_.dma_start(out=p.dbg["d_den"][:, h * 512:(h + 1) * 512],
                             in_=p.denrow).then_inc(p.sem["ge"].h, 16)
                p.sem["ge"].inc(16)
            p.G(dden)

    for dc_ in range(NDC):
        def ldwo(sy, dc_=dc_, snap=p.wqslot_snap[dc_ % 2]):
            if snap is None:
                sy.wait_ge(te.h, te.n)
            else:
                sy.wait_ge(te.h, snap)
            slot = p.wq0 if dc_ % 2 == 0 else p.wq1
            sy.dma_start(out=slot.rearrange("p (c n) -> p c n", c=NDC),
                         in_=p.Wo[:, dc_ * 128:(dc_ + 1) * 128].rearrange(
                             "(c p) n -> p c n", p=128)).then_inc(dw.h, 16)
            dw.inc(16)
        p.SY(ldwo)

        def two(t, dc_=dc_):
            t.wait_ge(dw.h, dw.n)
            if dc_ == 0:
                t.wait_ge(ve.h, ve.n)
            elif dc_ >= 2:
                t.wait_ge(ve.h, ve.n - 1)
            slot = p.wq0 if dc_ % 2 == 0 else p.wq1
            bank = p.pk[6] if dc_ % 2 == 0 else p.pk[7]
            for c in range(NDC):
                mmi = t.matmul(bank[:, :], slot[:, c * 128:(c + 1) * 128],
                               p.ysb[:, c * 512:(c + 1) * 512],
                               start=(c == 0), stop=(c == NDC - 1))
                if c == NDC - 1:
                    mmi.then_inc(te.h, 1)
            te.inc(1)
        p.T(two)
        p.wqslot_snap[dc_ % 2] = te.n

        def vso(v, dc_=dc_, sv=p.sspill_snap.get(dc_ - 2)):
            v.wait_ge(te.h, te.n)
            if sv is not None:
                v.wait_ge(dh3.h, sv)
            bank = p.pk[6] if dc_ % 2 == 0 else p.pk[7]
            v.tensor_copy(p.sTst[:, (dc_ % 2) * 512:(dc_ % 2 + 1) * 512],
                          bank[:, :]).then_inc(ve.h, 1)
            ve.inc(1)
        p.V(vso)

        def sspill(s_, dc_=dc_, qt=qt):
            s_.wait_ge(ve.h, ve.n)
            s_.dma_start(out=p.sTsp[dc_, :, qt * 512:(qt + 1) * 512],
                         in_=p.sTst[:, (dc_ % 2) * 512:(dc_ % 2 + 1) * 512]
                         ).then_inc(dh3.h, 16)
            dh3.inc(16)
        p.S(sspill)
        p.sspill_snap[dc_] = dh3.n


def _col2row(p, src_ap, ncols, psbank, dst_row):
    te, ve, dsp = p.sem["te"], p.sem["ve"], p.sem["dsp"]

    def tt_(t):
        t.wait_ge(ve.h, ve.n)
        t.transpose(psbank[0:ncols, 0:128], src_ap, p.identf).then_inc(te.h, 1)
        te.inc(1)
    p.T(tt_)

    def vv_(v):
        v.wait_ge(te.h, te.n)
        v.wait_ge(dsp.h, dsp.n)
        v.tensor_copy(p.pe2c[0:ncols, 0:128], psbank[0:ncols, 0:128]
                      ).then_inc(ve.h, 1)
        ve.inc(1)
    p.V(vv_)

    def gg_(g):
        g.wait_ge(ve.h, ve.n)
        g.dma_start(out=dst_row.rearrange("a (c b) -> a c b", b=128),
                    in_=p.pe2c[0:ncols, 0:128],
                    allow_slow_non_contiguous=True).then_inc(dsp.h, 16)
        dsp.inc(16)
    p.G(gg_)


def _ternarize(p, src_f32, dst_bf16, thr_col, width, ve_target=None,
               tmp=None):
    se, ve = p.sem["se"], p.sem["ve"]
    if tmp is None:
        tmp = p.tmpb

    def s_(s, ve_target=ve_target, tmp=tmp):
        s.wait_ge(ve.h, ve.n if ve_target is None else ve_target)
        s.activation(dst_bf16, src_f32, AF.Sign,
                     bias=p.thrn[:, thr_col:thr_col + 1])
        s.activation(tmp[:, 0:width], src_f32, AF.Sign,
                     bias=p.thrc[:, thr_col:thr_col + 1]).then_inc(se.h, 1)
        se.inc(1)
    p.S(s_)

    def v_(v, tmp=tmp):
        v.wait_ge(se.h, se.n)
        v.tensor_tensor(dst_bf16, dst_bf16, tmp[:, 0:width],
                        ALU.add).then_inc(ve.h, 1)
        ve.inc(1)
    p.V(v_)


def _phase4(p):
    dw, dh2, te, ve, se, dsp = (p.sem["dw"], p.sem["dh2"], p.sem["te"],
                                p.sem["ve"], p.sem["se"], p.sem["dsp"])
    dq, dx, dst = p.sem["dq"], p.sem["dx"], p.sem["dst"]


    if p.debug == 11:
        return

    def vtot(v):
        v.wait_ge(se.h, se.n)
        v.reduce_sum(p.st_t0[:, 0:1], p.wacc[:, 0:64], axis=AX.X)
        v.reduce_sum(p.st_t0[:, 1:2], p.wacc[:, 64:128], axis=AX.X
                     ).then_inc(ve.h, 1)
        ve.inc(1)
    p.V(vtot)

    def ttot(t):
        t.wait_ge(ve.h, ve.n)
        t.matmul(p.pk[0][0:1, 0:2], p.onesf[:, 0:1], p.st_t0[:, 0:2],
                 start=True, stop=True).then_inc(te.h, 1)
        te.inc(1)
    p.T(ttot)

    def vthr(v):
        v.wait_ge(te.h, te.n)
        v.tensor_scalar(p.rowd[:, 0:2], p.pk[0][0:1, 0:2],
                        0.5 / (F_INNER * D), None, ALU.mult).then_inc(ve.h, 1)
        ve.inc(1)
    p.V(vthr)

    def tthr(t):
        t.wait_ge(ve.h, ve.n)
        t.matmul(p.pk[1][:, 0:2], p.onesr, p.rowd[:, 0:2],
                 start=True, stop=True)
        t.matmul(p.pk[2][0:1, 0:128], p.rowd[:, 0:1], p.onesr,
                 start=True, stop=True)
        t.matmul(p.pk[3][0:1, 0:128], p.rowd[:, 1:2], p.onesr,
                 start=True, stop=True).then_inc(te.h, 1)
        te.inc(1)
    p.T(tthr)

    def vthr2(v):
        v.wait_ge(te.h, te.n)
        _ser(v, ve, v.tensor_copy(p.thrc[:, 0:2], p.pk[1][:, 0:2]))
        _ser(v, ve, v.tensor_scalar(p.thrn[:, 0:2], p.pk[1][:, 0:2], -1.0,
                                    None, ALU.mult))
        v.tensor_copy(p.thr1row, p.pk[2][0:1, 0:128])
        v.tensor_copy(p.thr2row, p.pk[3][0:1, 0:128]).then_inc(ve.h, 1)
        ve.inc(1)
    p.V(vthr2)

    if p.debug == 12:
        return
    _col2row(p, p.st_dq1[:, 8:16], 8, p.pk[4], p.dq1row)
    if p.debug == 15:
        return
    for tt in range(2):
        def tbc(t, tt=tt):
            t.wait_ge(dsp.h, dsp.n)
            t.wait_ge(ve.h, ve.n)
            t.matmul(p.pk[5][:, :], p.thr1row,
                     p.dq1row[:, tt * 512:(tt + 1) * 512],
                     start=True, stop=True).then_inc(te.h, 1)
            te.inc(1)
        p.T(tbc)

        def vbc(v, tt=tt):
            v.wait_ge(te.h, te.n)
            v.tensor_copy(p.bc1[:, tt * 512:(tt + 1) * 512],
                          p.pk[5][:, :]).then_inc(ve.h, 1)
            ve.inc(1)
        p.V(vbc)

    if p.debug == 13:
        return

    def swst(s):
        s.wait_ge(ve.h, ve.n)
        s.activation(p.wstA.rearrange("p (c k) -> p c k", k=3)[:, :, 1],
                     p.lngs, AF.Square, accum_out=p.st_t1[:, 0:1])
        s.activation(p.wstB.rearrange("p (c k) -> p c k", k=2)[:, :, 1],
                     p.lngs, AF.Square).then_inc(se.h, 1)
        se.inc(1)
    p.S(swst)

    def vwst(v):
        v.wait_ge(se.h, se.n)
        v.memset(p.pe2c[:, 64:128], 1.0)
        v.tensor_copy(p.wstA.rearrange("p (c k) -> p c k", k=3)[:, :, 0],
                      p.pe2c[:, 64:128])
        v.tensor_copy(p.wstB.rearrange("p (c k) -> p c k", k=2)[:, :, 0],
                      p.pe2c[:, 64:128])
        v.tensor_tensor(p.wstA.rearrange("p (c k) -> p c k", k=3)[:, :, 2],
                        p.lngs, p.lnbs, ALU.mult)
        _ser(v, ve, v.tensor_tensor(p.pe2c[:, 0:64], p.lngs, p.lnbs, ALU.mult))
        _ser(v, ve, v.reduce_sum(p.st_t1[:, 1:2], p.pe2c[:, 0:64], axis=AX.X))
        _ser(v, ve, v.tensor_tensor(p.pe2c[:, 0:64], p.lnbs, p.lnbs, ALU.mult))
        v.reduce_sum(p.st_t1[:, 2:3], p.pe2c[:, 0:64], axis=AX.X
                     ).then_inc(ve.h, 1)
        ve.inc(1)
    p.V(vwst)

    def tq3(t):
        t.wait_ge(ve.h, ve.n)
        t.matmul(p.pk[0][0:1, 0:3], p.onesf[:, 0:1], p.st_t1[:, 0:3],
                 start=True, stop=True).then_inc(te.h, 1)
        te.inc(1)
    p.T(tq3)

    def vq4b(v):
        v.wait_ge(te.h, te.n)
        v.tensor_copy(p.rowd[:, 8:11], p.pk[0][0:1, 0:3]).then_inc(ve.h, 1)
        ve.inc(1)
    p.V(vq4b)

    def tq4(t):
        t.wait_ge(ve.h, ve.n)
        t.matmul(p.pk[1][:, 0:3], p.onesr, p.rowd[:, 8:11],
                 start=True, stop=True).then_inc(te.h, 1)
        te.inc(1)
    p.T(tq4)

    def vq5(v):
        v.wait_ge(te.h, te.n)
        v.tensor_copy(p.qcols[:, 0:3], p.pk[1][:, 0:3]).then_inc(ve.h, 1)
        ve.inc(1)
    p.V(vq5)

    if p.debug == 14:
        return
    for ot in range(NOWN):
        def ldxq(g, ot=ot):
            g.wait_ge(te.h, te.n)
            g.dma_start(
                out=p.xqTr.rearrange("p (c n) -> p c n", c=NDC)[
                    :, :, ot * 128:(ot + 1) * 128],
                in_=p.xqsp[ot].rearrange("p (c n) -> p c n", c=NDC)
            ).then_inc(dq.h, 16)
            dq.inc(16)
        p.G(ldxq)

    if p.debug == 8:
        return
    tern1_snap = None
    gsp_snap = None
    evac_snap = {}
    gelu_snap = None
    tst_snap = None
    for sl in range(NSL):
        def ldw(sy, sl=sl, snap=tern1_snap):
            if snap is None:
                sy.wait_ge(se.h, se.n)
                sy.wait_ge(te.h, te.n)
                sy.wait_ge(p.sem["dh3"].h, p.sem["dh3"].n)
            else:
                sy.wait_ge(se.h, snap)
            sy.dma_start(out=p.wfsl.rearrange("p (s d) -> p s d", s=4),
                         in_=p.w1[sl * FSL:(sl + 1) * FSL, :].rearrange(
                             "(s p) d -> p s d", p=128)).then_inc(dw.h, 16)
            dw.inc(16)
        p.SY(ldw)

        def sww(s):
            s.wait_ge(dw.h, dw.n)
        p.S(sww)
        wt = p.wtern if sl % 2 == 0 else p.wtern2
        tmp = p.tmpb if sl % 2 == 0 else p.tmpb2
        vt = evac_snap.get(sl - 2)
        for sub in range(4):
            _ternarize(p, p.wfsl[:, sub * D:(sub + 1) * D],
                       wt[:, sub * D:(sub + 1) * D], 0, D,
                       ve_target=vt if sub == 0 else None, tmp=tmp)
        tern1_snap = se.n
        pb16 = p.pk[0][:].bitcast(BF16)
        for c in range(NDC):
            def ttr(t, c=c, wt=wt, gs=gelu_snap):
                t.wait_ge(ve.h, ve.n)
                if c == 0 and gs is not None:
                    t.wait_ge(se.h, gs)
                for sub in range(4):
                    t.transpose(pb16[:, sub * 128:(sub + 1) * 128],
                                wt[:, sub * D + c * 128:
                                   sub * D + (c + 1) * 128],
                                p.identb).then_inc(te.h, 1)
                    te.inc(1)
            p.T(ttr)

            def vev(v, c=c):
                v.wait_ge(te.h, te.n)
                v.tensor_copy(p.wqT[:, c * 512:(c + 1) * 512],
                              pb16[:, 0:512]).then_inc(ve.h, 1)
                ve.inc(1)
            p.V(vev)
        evac_snap[sl] = ve.n

        if sl == 0:
            def wxq(t):
                t.wait_ge(dq.h, dq.n)
            p.T(wxq)
        for rnd in range(2):
            for kb in range(4):
                k = rnd * 4 + kb

                def tmm(t, k=k, kb=kb, rnd=rnd, gs=gelu_snap):
                    t.wait_ge(ve.h, ve.n)
                    if gs is not None:
                        t.wait_ge(se.h, gs)
                    for c in range(NDC):
                        mmi = t.matmul(
                            p.pk[kb][:, :],
                            p.wqT[:, c * 512 + kb * 128:c * 512 + (kb + 1) * 128],
                            p.xqTr[:, c * TOK + rnd * 512:c * TOK + (rnd + 1) * 512],
                            start=(c == 0), stop=(c == NDC - 1))
                        if c == NDC - 1:
                            mmi.then_inc(te.h, 1)
                    te.inc(1)
                p.T(tmm)

                def vdq(v, k=k, kb=kb, rnd=rnd, snap=gsp_snap):
                    v.wait_ge(te.h, te.n)
                    if k == 0 and snap is not None:
                        v.wait_ge(dh2.h, snap)
                    v.scalar_tensor_tensor(
                        p.hbuf[:, k * 512:(k + 1) * 512], p.pk[kb][:, :], 0.0,
                        p.bc1[:, rnd * 512:(rnd + 1) * 512], ALU.add,
                        ALU.mult).then_inc(ve.h, 1)
                    ve.inc(1)
                p.V(vdq)

            def ssilu(s, sl=sl, rnd=rnd):
                s.wait_ge(ve.h, ve.n)
                for kb in range(4):
                    k = rnd * 4 + kb
                    fcol = sl * 4 + kb
                    s.activation(p.pk[kb][:, :], p.hbuf[:, k * 512:(k + 1) * 512],
                                 AF.Silu, bias=p.b1s[:, fcol:fcol + 1])
                s.memzero(p.st_t1[:, 15:16]).then_inc(se.h, 1)
                se.inc(1)
            p.S(ssilu)

            def sgelu(s, rnd=rnd):
                for kb in range(4):
                    k = rnd * 4 + kb
                    s.activation(p.hbuf[:, k * 512:(k + 1) * 512],
                                 p.pk[kb][:, :], AF.Gelu)
                s.memzero(p.st_t1[:, 15:16]).then_inc(se.h, 1)
                se.inc(1)
            p.S(sgelu)
            gelu_snap = se.n

            for kb in range(4):
                k = rnd * 4 + kb

                def gsp(s, k=k, kb=kb, rnd=rnd, sl=sl):
                    fc = sl * 4 + kb
                    s.dma_start(out=p.h2sp[fc, :, rnd * 512:(rnd + 1) * 512],
                                in_=p.hbuf[:, k * 512:(k + 1) * 512]
                                ).then_inc(dh2.h, 16)
                    p.sem["dh2"].inc(16)
                p.S(gsp)

            for kb in range(4):
                k = rnd * 4 + kb
                fc = sl * 4 + kb

                def vcp(v, k=k, ts=tst_snap, gs=gelu_snap):
                    v.wait_ge(se.h, gs)
                    if ts is not None:
                        v.wait_ge(te.h, ts)
                    v.tensor_copy(p.statu,
                                  p.hbuf[:, k * 512:(k + 1) * 512]
                                  ).then_inc(ve.h, 1)
                    ve.inc(1)
                p.V(vcp)

                def ssq(s, k=k, ts=tst_snap):
                    if ts is not None:
                        s.wait_ge(te.h, ts)
                    s.activation(p.statq, p.hbuf[:, k * 512:(k + 1) * 512],
                                 AF.Square).then_inc(se.h, 1)
                    se.inc(1)
                p.S(ssq)

                def tst(t, fc=fc, rnd=rnd, first=(sl == 0 and kb == 0),
                        last=(sl == NSL - 1 and kb == 3)):
                    t.wait_ge(ve.h, ve.n)
                    t.wait_ge(se.h, se.n)
                    t.matmul(p.pk[4 + 2 * rnd][0:3, :],
                             p.wstA[:, fc * 3:(fc + 1) * 3],
                             p.statu, start=first, stop=last)
                    mmi = t.matmul(p.pk[5 + 2 * rnd][0:2, :],
                                   p.wstB[:, fc * 2:(fc + 1) * 2],
                                   p.statq, start=first, stop=last)
                    mmi.then_inc(te.h, 1)
                    te.inc(1)
                p.T(tst)
                tst_snap = te.n
        gsp_snap = p.sem["dh2"].n

    if p.debug == 9:
        return
    dh = p.sem["dh3"]
    for tt in range(2):
        def vevs(v, tt=tt):
            v.wait_ge(te.h, te.n)
            v.tensor_copy(p.pe2c[0:3, 0:512], p.pk[4 + 2 * tt][0:3, :])
            v.tensor_copy(p.pe2u[0:2, 0:512], p.pk[5 + 2 * tt][0:2, :]
                          ).then_inc(ve.h, 1)
            ve.inc(1)
        p.V(vevs)
        for j in range(4):
            def tts(t, j=j):
                t.wait_ge(ve.h, ve.n)
                t.transpose(p.pk[4][:, 0:3],
                            p.pe2c[0:3, j * 128:(j + 1) * 128],
                            p.identf[0:3, 0:3])
                t.transpose(p.pk[5][:, 0:2],
                            p.pe2u[0:2, j * 128:(j + 1) * 128],
                            p.identf[0:2, 0:2]).then_inc(te.h, 1)
                te.inc(1)
            p.T(tts)

            def vcs(v, tt=tt, j=j):
                v.wait_ge(te.h, te.n)
                col = tt * 4 + j
                v.tensor_copy(p.s2_S1[:, col:col + 1], p.pk[4][:, 0:1])
                v.tensor_copy(p.s2_SG[:, col:col + 1], p.pk[4][:, 1:2])
                v.tensor_copy(p.s2_SB[:, col:col + 1], p.pk[4][:, 2:3])
                v.tensor_copy(p.s2_S2[:, col:col + 1], p.pk[5][:, 0:1])
                v.tensor_copy(p.s2_SG2[:, col:col + 1], p.pk[5][:, 1:2]
                              ).then_inc(ve.h, 1)
                ve.inc(1)
            p.V(vcs)

    IFI = 1.0 / F_INNER

    def vf1(v):
        v.wait_ge(ve.h, ve.n)
        _ser(v, ve, v.tensor_scalar(p.s2_mu2, p.s2_S1, IFI, None, ALU.mult))
        _ser(v, ve, v.tensor_scalar(p.s2_tmp, p.s2_S2, IFI, None, ALU.mult))
        _ser(v, ve, v.tensor_tensor(p.s2_rstd2, p.s2_mu2, p.s2_mu2, ALU.mult))
        _ser(v, ve, v.tensor_tensor(p.s2_tmp, p.s2_tmp, p.s2_rstd2, ALU.subtract))
        v.tensor_scalar(p.s2_tmp, p.s2_tmp, EPS, None, ALU.add).then_inc(ve.h, 1)
        ve.inc(1)
    p.V(vf1)

    def sf1(s):
        s.wait_ge(ve.h, ve.n)
        s.activation(p.s2_rstd2, p.s2_tmp, AF.Sqrt).then_inc(se.h, 1)
        se.inc(1)
    p.S(sf1)

    def vf2(v):
        v.wait_ge(se.h, se.n)
        _ser(v, ve, v.reciprocal(p.s2_rstd2, p.s2_rstd2))
        _ser(v, ve, v.scalar_tensor_tensor(p.s2_tmp, p.s2_mu2, -2.0, p.s2_SG,
                                           ALU.mult, ALU.mult))
        _ser(v, ve, v.tensor_tensor(p.s2_tmp, p.s2_tmp, p.s2_SG2, ALU.add))
        _ser(v, ve, v.tensor_tensor(p.s2_r2, p.s2_mu2, p.s2_mu2, ALU.mult))
        _ser(v, ve, v.scalar_tensor_tensor(p.s2_tmp, p.s2_r2, p.qcols[:, 0:1],
                                           p.s2_tmp, ALU.mult, ALU.add))
        _ser(v, ve, v.tensor_tensor(p.s2_r2, p.s2_rstd2, p.s2_rstd2, ALU.mult))
        _ser(v, ve, v.tensor_tensor(p.s2_tmp, p.s2_tmp, p.s2_r2, ALU.mult))
        _ser(v, ve, v.scalar_tensor_tensor(p.s2_r2, p.s2_mu2, p.qcols[:, 1:2],
                                           p.s2_SB, ALU.mult, ALU.subtract))
        _ser(v, ve, v.tensor_tensor(p.s2_r2, p.s2_r2, p.s2_rstd2, ALU.mult))
        _ser(v, ve, v.scalar_tensor_tensor(p.s2_tmp, p.s2_r2, -2.0, p.s2_tmp,
                                           ALU.mult, ALU.add))
        _ser(v, ve, v.tensor_scalar(p.s2_tmp, p.s2_tmp, p.qcols[:, 2:3], None,
                                    ALU.add))
        v.tensor_scalar(p.s2_tmp, p.s2_tmp, 1.0 / F_INNER, EPS,
                        ALU.mult, ALU.add).then_inc(ve.h, 1)
        ve.inc(1)
    p.V(vf2)

    def sf2(s):
        s.wait_ge(ve.h, ve.n)
        s.activation(p.s2_tmp, p.s2_tmp, AF.Sqrt).then_inc(se.h, 1)
        se.inc(1)
    p.S(sf2)

    def vf3(v):
        v.wait_ge(se.h, se.n)
        v.reciprocal(p.s2_r2, p.s2_tmp).then_inc(ve.h, 1)
        ve.inc(1)
    p.V(vf3)

    for tt in range(2):
        _col2row(p, p.s2_rstd2[:, tt * 4:(tt + 1) * 4], 4, p.pk[6],
                 p.rowa)
        _col2row(p, p.s2_mu2[:, tt * 4:(tt + 1) * 4], 4, p.pk[6], p.rowc)

        def trb(t, tt=tt):
            t.wait_ge(dsp.h, dsp.n)
            t.matmul(p.pk[6][:, :], p.onesr, p.rowa, start=True, stop=True
                     ).then_inc(te.h, 1)
            te.inc(1)
        p.T(trb)

        def vrb(v, tt=tt):
            v.wait_ge(te.h, te.n)
            v.tensor_copy(p.RAb, p.pk[6][:, :])
            _ser(v, ve, v.tensor_tensor(p.rowd, p.rowc, p.rowa, ALU.mult))
            v.tensor_scalar(p.rowd, p.rowd, -1.0, None, ALU.mult
                            ).then_inc(ve.h, 1)
            ve.inc(1)
        p.V(vrb)

        def trc(t, tt=tt):
            t.wait_ge(ve.h, ve.n)
            t.matmul(p.pk[6][:, :], p.onesr, p.rowd, start=True, stop=True
                     ).then_inc(te.h, 1)
            te.inc(1)
        p.T(trc)

        def vrc(v, tt=tt):
            v.wait_ge(te.h, te.n)
            v.tensor_copy(p.RCb, p.pk[6][:, :])
            v.memset(p.s2_mxh[:, tt * 4:(tt + 1) * 4], 0.0).then_inc(ve.h, 1)
            ve.inc(1)
        p.V(vrc)

        vhn_sv = {}
        for fc in range(NFC):
            b = fc % 2
            sv = vhn_sv.get(fc - 2)

            def ldh(sc, fc=fc, tt=tt, b=b, sv=sv):
                if sv is None:
                    sc.wait_ge(ve.h, ve.n)
                else:
                    sc.wait_ge(ve.h, sv)
                sc.dma_start(out=p.h2st[:, b * 512:(b + 1) * 512],
                             in_=p.h2sp[fc, :, tt * 512:(tt + 1) * 512]
                             ).then_inc(dh.h, 16)
                dh.inc(16)
            p.S(ldh)

            def vhn(v, fc=fc, b=b):
                v.wait_ge(dh.h, dh.n)
                gc = p.lngs[:, fc:fc + 1]
                bc = p.lnbs[:, fc:fc + 1]
                v.scalar_tensor_tensor(p.pe2u, p.h2st[:, b * 512:(b + 1) * 512],
                                       gc, p.RAb, ALU.mult, ALU.mult)
                v.scalar_tensor_tensor(p.pe2u, p.RCb, gc, p.pe2u,
                                       ALU.mult, ALU.add)
                v.tensor_scalar(p.pe2u, p.pe2u, bc, None, ALU.add
                                ).then_inc(ve.h, 1)
                ve.inc(1)
            p.V(vhn)
            vhn_sv[fc] = ve.n

            for j in range(4):
                def tth(t, j=j):
                    if j == 0:
                        t.wait_ge(ve.h, ve.n)
                    t.transpose(p.pk[7][:, j * 128:(j + 1) * 128],
                                p.pe2u[:, j * 128:(j + 1) * 128],
                                p.identf).then_inc(te.h, 1)
                    te.inc(1)
                p.T(tth)

            def vmx(v, tt=tt, fc=fc):
                v.wait_ge(te.h, te.n)
                for j in range(3):
                    v.reduce_max(p.s2_tmp[:, j:j + 1],
                                 p.pk[7][:, j * 128:(j + 1) * 128], axis=AX.X,
                                 apply_absolute_value=True)
                _ser(v, ve, v.reduce_max(p.s2_tmp[:, 3:4],
                                         p.pk[7][:, 384:512], axis=AX.X,
                                         apply_absolute_value=True))
                for j in range(4):
                    col = tt * 4 + j
                    inst = v.tensor_tensor(p.s2_mxh[:, col:col + 1],
                                           p.s2_mxh[:, col:col + 1],
                                           p.s2_tmp[:, j:j + 1], ALU.max)
                inst.then_inc(ve.h, 1)
                ve.inc(1)
            p.V(vmx)

    if p.debug == 3:
        return
    def vf4(v):
        v.wait_ge(ve.h, ve.n)
        _ser(v, ve, v.reciprocal(p.s2_c2, p.s2_mxh))
        _ser(v, ve, v.tensor_scalar(p.s2_c2, p.s2_c2, 127.0, None, ALU.mult))
        _ser(v, ve, v.tensor_tensor(p.s2_dq2, p.s2_r2, p.s2_mxh, ALU.mult))
        v.tensor_scalar(p.s2_dq2, p.s2_dq2, 1.0 / 127.0, None, ALU.mult
                        ).then_inc(ve.h, 1)
        ve.inc(1)
    p.V(vf4)
    _col2row(p, p.s2_dq2, 8, p.pk[6], p.dq2row)

    for tt in range(2):
        _phase4c_tt(p, tt)
        if p.debug == 10 and tt == 0:
            return


def _phase4c_tt(p, tt):
    dw, dh, te, ve, se, dsp = (p.sem["dw"], p.sem["dh3"], p.sem["te"],
                               p.sem["ve"], p.sem["se"], p.sem["dsp"])
    dx, dst = p.sem["dx"], p.sem["dst"]

    def vr0(v):
        v.wait_ge(ve.h, ve.n)
        v.tensor_tensor(p.s2_tmp, p.s2_rstd2, p.s2_c2, ALU.mult
                        ).then_inc(ve.h, 1)
        ve.inc(1)
        v.wait_ge(ve.h, ve.n)
    p.V(vr0)
    _col2row(p, p.s2_tmp[:, tt * 4:(tt + 1) * 4], 4, p.pk[6], p.rowa)
    _col2row(p, p.s2_c2[:, tt * 4:(tt + 1) * 4], 4, p.pk[6], p.rowc)

    def vr1(v):
        v.wait_ge(dsp.h, dsp.n)
        _ser(v, ve, v.tensor_tensor(p.s2_tmp, p.s2_tmp, p.s2_mu2, ALU.mult))
        v.tensor_scalar(p.s2_tmp, p.s2_tmp, -1.0, None, ALU.mult
                        ).then_inc(ve.h, 1)
        ve.inc(1)
    p.V(vr1)
    _col2row(p, p.s2_tmp[:, tt * 4:(tt + 1) * 4], 4, p.pk[6], p.rowd)

    for nm, row, dstb in [("ra", "rowa", "RAb"), ("rb", "rowc", "RBb"),
                          ("rc", "rowd", "RCb")]:
        def tbb(t, row=row):
            t.wait_ge(dsp.h, dsp.n)
            t.wait_ge(ve.h, ve.n)
            t.matmul(p.pk[6][:, :], p.onesr, getattr(p, row), start=True,
                     stop=True).then_inc(te.h, 1)
            te.inc(1)
        p.T(tbb)

        def vbb(v, dstb=dstb):
            v.wait_ge(te.h, te.n)
            v.tensor_copy(getattr(p, dstb), p.pk[6][:, :]).then_inc(ve.h, 1)
            ve.inc(1)
        p.V(vbb)

    def tb2(t, tt=tt):
        t.wait_ge(ve.h, ve.n)
        t.matmul(p.pk[6][:, :], p.thr2row, p.dq2row[:, tt * 512:(tt + 1) * 512],
                 start=True, stop=True).then_inc(te.h, 1)
        te.inc(1)
    p.T(tb2)

    def vb2(v):
        v.wait_ge(te.h, te.n)
        v.tensor_copy(p.bc1[:, 0:512], p.pk[6][:, :]).then_inc(ve.h, 1)
        ve.inc(1)
    p.V(vb2)

    vq2_sv = {}
    for fc in range(NFC):
        b = fc % 2
        sv = vq2_sv.get(fc - 2)

        def ldh(sc, fc=fc, tt=tt, b=b, sv=sv):
            if sv is None:
                sc.wait_ge(ve.h, ve.n)
            else:
                sc.wait_ge(ve.h, sv)
            sc.dma_start(out=p.h2st[:, b * 512:(b + 1) * 512],
                         in_=p.h2sp[fc, :, tt * 512:(tt + 1) * 512]
                         ).then_inc(dh.h, 16)
            dh.inc(16)
        p.S(ldh)

        def vq2(v, fc=fc, b=b):
            v.wait_ge(dh.h, dh.n)
            gc = p.lngs[:, fc:fc + 1]
            bc = p.lnbs[:, fc:fc + 1]
            v.scalar_tensor_tensor(p.pe2u, p.h2st[:, b * 512:(b + 1) * 512],
                                   gc, p.RAb, ALU.mult, ALU.mult)
            v.scalar_tensor_tensor(p.pe2u, p.RCb, gc, p.pe2u, ALU.mult, ALU.add)
            v.scalar_tensor_tensor(p.pe2u, p.RBb, bc, p.pe2u, ALU.mult, ALU.add)
            v.tensor_scalar(p.pe2b, p.pe2u, 128.5, None, ALU.add)
            v.tensor_scalar(p.pe2c, p.pe2b, 128.0, -128.0, ALU.is_ge, ALU.mult)
            v.tensor_tensor(p.pe2u, p.pe2b, p.pe2c, ALU.add)
            for c in [64.0, 32.0, 16.0, 8.0, 4.0, 2.0, 1.0]:
                v.tensor_scalar(p.pe2c, p.pe2u, c, -c, ALU.is_ge, ALU.mult)
                v.tensor_tensor(p.pe2u, p.pe2u, p.pe2c, ALU.add)
            v.scalar_tensor_tensor(p.xq2T[:, fc * 512:(fc + 1) * 512],
                                   p.pe2b, 128.0, p.pe2u,
                                   ALU.subtract, ALU.subtract).then_inc(ve.h, 1)
            ve.inc(1)
        p.V(vq2)
        vq2_sv[fc] = ve.n

    tern2_snap = None
    for ds in range(NDC):
        def ldw2(sy, ds=ds, snap=tern2_snap):
            if snap is None:
                sy.wait_ge(se.h, se.n)
                sy.wait_ge(dst.h, dst.n)
            else:
                sy.wait_ge(se.h, snap)
            sy.dma_start(out=p.w2sl, in_=p.w2[ds * 128:(ds + 1) * 128, :]
                         ).then_inc(dw.h, 16)
            dw.inc(16)
        p.SY(ldw2)

        def sw2(s):
            s.wait_ge(dw.h, dw.n)
        p.S(sw2)
        for sub in range(4):
            _ternarize(p, p.w2sl[:, sub * D:(sub + 1) * D],
                       p.w2tn[:, sub * D:(sub + 1) * D], 1, D)
        tern2_snap = se.n
        pb16 = p.pk[6][:].bitcast(BF16)
        for fq in range(16):
            def ttr(t, fq=fq):
                t.wait_ge(ve.h, ve.n)
                for ff in range(4):
                    fc = fq * 4 + ff
                    t.transpose(pb16[:, ff * 128:(ff + 1) * 128],
                                p.w2tn[:, fc * 128:(fc + 1) * 128],
                                p.identb).then_inc(te.h, 1)
                    te.inc(1)
            p.T(ttr)

            def vev(v, fq=fq):
                v.wait_ge(te.h, te.n)
                v.tensor_copy(p.w2tT[:, fq * 512:(fq + 1) * 512],
                              pb16[:, 0:512]).then_inc(ve.h, 1)
                ve.inc(1)
            p.V(vev)

        bank = p.pk[7] if ds % 2 == 0 else p.pk[5]

        def tm2(t, ds=ds, bank=bank):
            t.wait_ge(ve.h, ve.n)
            if ds >= 2:
                t.wait_ge(se.h, se.n)
            for fc in range(NFC):
                mmi = t.matmul(bank[:, :],
                               p.w2tT[:, fc * 128:(fc + 1) * 128],
                               p.xq2T[:, fc * 512:(fc + 1) * 512],
                               start=(fc == 0), stop=(fc == NFC - 1))
                if fc == NFC - 1:
                    mmi.then_inc(te.h, 1)
            te.inc(1)
        p.T(tm2)

        def vdq2(v, ds=ds, bank=bank):
            v.wait_ge(te.h, te.n)
            v.wait_ge(se.h, se.n)
            v.scalar_tensor_tensor(p.pe2u, bank[:, :], 0.0, p.bc1[:, 0:512],
                                   ALU.add, ALU.mult).then_inc(ve.h, 1)
            ve.inc(1)
        p.V(vdq2)

        def sg2(s, ds=ds):
            s.wait_ge(ve.h, ve.n)
            s.activation(p.gkeep[:, ds * 512:(ds + 1) * 512], p.pe2u, AF.Silu,
                         bias=p.b2s[:, ds:ds + 1]).then_inc(se.h, 1)
            se.inc(1)
        p.S(sg2)

    vsg_snap = {}
    for ds in range(NDC):
        sTr = p.sTrA if ds % 2 == 0 else p.sTrB
        sv = vsg_snap.get(ds - 2)

        def ldsT(sc, ds=ds, tt=tt, sTr=sTr, sv=sv):
            if sv is None:
                sc.wait_ge(ve.h, ve.n)
            else:
                sc.wait_ge(ve.h, sv)
            sc.dma_start(out=sTr, in_=p.sTsp[ds, :, tt * 512:(tt + 1) * 512]
                         ).then_inc(p.sem["dw2"].h, 16)
            p.sem["dw2"].inc(16)
        p.S(ldsT)

        def vsg(v, ds=ds, sTr=sTr):
            v.wait_ge(p.sem["dw2"].h, p.sem["dw2"].n)
            v.wait_ge(se.h, se.n)
            v.tensor_tensor(p.xq2T[:, ds * 512:(ds + 1) * 512], sTr,
                            p.gkeep[:, ds * 512:(ds + 1) * 512],
                            ALU.mult).then_inc(ve.h, 1)
            ve.inc(1)
        p.V(vsg)
        vsg_snap[ds] = ve.n

    vout_snap = {}
    for j in range(4):
        ot = tt * 4 + j
        xrel = p.xrelA if j % 2 == 0 else p.xrelB
        xv = vout_snap.get(j - 2)

        def ldx(sy, ot=ot, xv=xv):
            if xv is None:
                sy.wait_ge(ve.h, ve.n)
            else:
                sy.wait_ge(ve.h, xv)
            sy.dma_start(out=p.xrelA if ot % 2 == 0 else p.xrelB,
                         in_=p.xin[(8 + ot) * 128:(9 + ot) * 128, :]
                         ).then_inc(dx.h, 16)
            dx.inc(16)
        p.SY(ldx)
        ldx_cnt = dx.n

        pb16g = p.pk[4][:].bitcast(BF16)
        for dq4 in range(4):
            def ttg(t, j=j, dq4=dq4):
                t.wait_ge(ve.h, ve.n)
                for dd in range(4):
                    ds = dq4 * 4 + dd
                    t.transpose(pb16g[:, dd * 128:(dd + 1) * 128],
                                p.xq2T[:, ds * 512 + j * 128:
                                       ds * 512 + (j + 1) * 128],
                                p.identb).then_inc(te.h, 1)
                    te.inc(1)
            p.T(ttg)

            def vout(v, dq4=dq4, xrel=xrel, xc=ldx_cnt):
                v.wait_ge(te.h, te.n)
                v.wait_ge(dx.h, xc)
                if dq4 == 0:
                    v.wait_ge(dst.h, dst.n)
                for dd in range(4):
                    ds = dq4 * 4 + dd
                    v.tensor_tensor(p.ostg[:, ds * 128:(ds + 1) * 128],
                                    xrel[:, ds * 128:(ds + 1) * 128],
                                    pb16g[:, dd * 128:(dd + 1) * 128], ALU.add)
                v.memset(p.st_t0[:, 15:16], 0.0).then_inc(ve.h, 1)
                ve.inc(1)
            p.V(vout)
        vout_snap[j] = ve.n

        def sto(s, ot=ot):
            s.wait_ge(ve.h, ve.n)
            s.dma_start(out=p.out[ot * 128:(ot + 1) * 128, :],
                        in_=p.ostg).then_inc(dst.h, 16)
            dst.inc(16)
        p.S(sto)



def _debug_dump(p):
    if not p.debug:
        return
    ge, ve, te = p.sem["ge"], p.sem["ve"], p.sem["te"]
    d = p.dbg

    def gdump(g):
        g.wait_ge(ve.h, ve.n)
        g.wait_ge(te.h, te.n)
        for s_ in ["dx", "dc", "dw", "dq", "dsp", "se", "dst", "dh2", "dh3",
                   "dw2"]:
            g.wait_ge(p.sem[s_].h, p.sem[s_].n)
        if p.debug == 2:
            for dst_ap, src in [(d["d_kh"], p.kh), (d["d_kl"], p.kl)]:
                g.dma_start(out=dst_ap.bitcast(BF16), in_=src).then_inc(ge.h, 16)
                ge.inc(16)
            g.dma_start(out=d["d_vtm"], in_=p.vtm).then_inc(ge.h, 16)
            ge.inc(16)
            g.dma_start(out=d["d_xq"].bitcast(BF16), in_=p.xqsp).then_inc(ge.h, 16)
            ge.inc(16)
        st_list = [p.st_mu, p.st_rstd, p.st_c1, p.st_dq1, p.st_mx, p.st_t1,
                   p.st_sx, p.st_sxx]
        for k, stt in enumerate(st_list):
            g.dma_start(out=d["d_st"][:, k * NT128:(k + 1) * NT128],
                        in_=stt).then_inc(ge.h, 16)
            ge.inc(16)
    p.G(gdump)


def _finish(p):
    dst, ge = p.sem["dst"], p.sem["ge"]

    def fin(g):
        g.wait_ge(dst.h, dst.n)
        g.wait_ge(ge.h, ge.n)
    p.G(fin)


def _host_consts(core):
    identf = np.eye(128, dtype=np.float32)
    identb = identf.astype(ml_dtypes.bfloat16)
    start = core * TOK
    attl = np.zeros((8, GROUP * NKT * 128), dtype=np.float64)
    attr = np.zeros((8, GROUP * 2 * 512), dtype=np.float64)
    attl[4:8] = 1.0
    for g in range(GROUP):
        s = SLOPE_G[g]
        for kt in range(NKT):
            kl = np.arange(128)
            kg = start - HALO + kt * 128 + kl
            a = s * (kt * 128 + kl - HALO).astype(np.float64)
            a[kg < 0] = NEG
            sp = _split_bf16(a, 4)
            for r in range(4):
                attl[r, (g * NKT + kt) * 128:(g * NKT + kt + 1) * 128] = \
                    sp[r].astype(np.float64)
        for qt in range(2):
            ql = np.arange(512)
            bv = -s * (qt * 512 + ql).astype(np.float64)
            sp = _split_bf16(bv, 4)
            col = (g * 2 + qt) * 512
            attr[0:4, col:col + 512] = 1.0
            for r in range(4):
                attr[4 + r, col:col + 512] = sp[r].astype(np.float64)
    maskb = np.zeros((128, 896), dtype=np.float64)
    for klp in range(128):
        maskb[klp, : klp + 384] = NEG
    return {
        "identf": identf, "identb": identb.view(np.uint16),
        "attl": attl.astype(ml_dtypes.bfloat16).view(np.uint16),
        "attr": attr.astype(ml_dtypes.bfloat16).view(np.uint16),
        "maskb": maskb.astype(ml_dtypes.bfloat16).view(np.uint16),
        "onesc": np.ones((128, 2), dtype=np.float32),
        "onesr": np.ones((1, 128), dtype=np.float32),
    }


def make_inmaps(inputs):
    x = np.ascontiguousarray(np.asarray(inputs["L0_attn_x"], dtype=np.float32)
                             .reshape(T_FULL, D))
    full = {
        "Wq": np.ascontiguousarray(inputs["L0_attn_Wq"], dtype=np.float32),
        "Wk": np.ascontiguousarray(inputs["L0_attn_Wk"], dtype=np.float32),
        "Wv": np.ascontiguousarray(inputs["L0_attn_Wv"], dtype=np.float32),
        "Wo": np.ascontiguousarray(inputs["L0_attn_Wo"], dtype=np.float32),
        "w1": np.ascontiguousarray(inputs["L0_ffn_w1"], dtype=np.float32),
        "b1": np.ascontiguousarray(inputs["L0_ffn_b1"], dtype=np.float32),
        "lng": np.ascontiguousarray(inputs["L0_ffn_ln_g"], dtype=np.float32),
        "lnb": np.ascontiguousarray(inputs["L0_ffn_ln_b"], dtype=np.float32),
        "w2": np.ascontiguousarray(inputs["L0_ffn_w2"], dtype=np.float32),
        "b2": np.ascontiguousarray(inputs["L0_ffn_b2"], dtype=np.float32),
    }
    maps = []
    for c in range(NCORES):
        lo = c * TOK - HALO
        if lo < 0:
            xs = np.concatenate([np.zeros((-lo, D), np.float32),
                                 x[: (c + 1) * TOK]], axis=0)
        else:
            xs = x[lo:(c + 1) * TOK]
        m = dict(full)
        m["xin"] = np.ascontiguousarray(xs)
        m.update(_host_consts(c))
        maps.append(m)
    return maps


def kernel(**inputs):
    nc = build_program(debug=0)
    maps = make_inmaps(inputs)
    res = run_bass_kernel_spmd(nc, maps, core_ids=list(range(NCORES)))
    out = np.concatenate([np.asarray(r["out"]) for r in res.results], axis=0)
    return out.reshape(1, T_FULL, D)
